# version 09
# baseline (speedup 1.0000x reference)
import numpy as np
import ml_dtypes
import concourse.bass as bass
import concourse.mybir as mybir
from concourse.bass_utils import run_bass_kernel_spmd
from contextlib import ExitStack

F32 = mybir.dt.float32
BF16 = mybir.dt.bfloat16
AF = mybir.ActivationFunctionType
ALU = mybir.AluOpType
AX = mybir.AxisListType

P = 128
D = 1024
E3 = 3072
HID = 4096
KD = D // P
KH = HID // P
NB = 512
NH = 16
DK = 64


class Res:
    __slots__ = ("name", "writers", "readers")

    def __init__(self, name):
        self.name = name
        self.writers = []
        self.readers = []


class Op:
    __slots__ = ("idx", "eng", "fn", "deps", "dma", "need_inc",
                 "sem_key", "sem_val", "waits", "tag", "eng_pos")

    def __init__(self, idx, eng, fn, dma, tag):
        self.idx = idx
        self.eng_pos = -1
        self.eng = eng
        self.fn = fn
        self.dma = dma
        self.deps = []
        self.need_inc = False
        self.sem_key = None
        self.sem_val = None
        self.waits = []
        self.tag = tag


class Plan:
    def __init__(self):
        self.ops = []
        self._eng_n = {}

    def add(self, eng, fn, reads=(), writes=(), dma=False, tag=""):
        op = Op(len(self.ops), eng, fn, dma, tag)
        op.eng_pos = self._eng_n.get(eng, 0)
        self._eng_n[eng] = op.eng_pos + 1
        deps = []
        for r in reads:
            deps.extend(r.writers)
        for w in writes:
            deps.extend(w.writers)
            deps.extend(w.readers)
        best = {}
        for d in deps:
            if d is op:
                continue
            key = ("Q" + d.eng) if d.dma else d.eng
            if key not in best or best[key].idx < d.idx:
                best[key] = d
        op.deps = list(best.values())
        for r in reads:
            r.readers.append(op)
        for w in writes:
            w.writers = [op]
            w.readers = []
        self.ops.append(op)
        return op

    def alias_barrier(self, dst_res, src_res):
        ops = []
        for r in src_res:
            ops.extend(r.writers)
            ops.extend(r.readers)
        for d in dst_res:
            d.readers.extend(ops)

    def finalize(self):
        for op in self.ops:
            resolved = []
            for d in op.deps:
                if d.dma:
                    resolved.append(d)
                elif d.eng != op.eng:
                    d.need_inc = True
                    resolved.append(d)
                elif (op.eng in ("V", "S") and not op.dma
                      and op.eng_pos - d.eng_pos == 1):
                    d.need_inc = True
                    resolved.append(d)
            op.deps = resolved
        ccnt, qcnt = {}, {}
        for op in self.ops:
            if op.dma:
                q = "Q" + op.eng
                qcnt[q] = qcnt.get(q, 0) + 16
                op.sem_key, op.sem_val = q, qcnt[q]
            elif op.need_inc:
                ccnt[op.eng] = ccnt.get(op.eng, 0) + 1
                op.sem_key, op.sem_val = op.eng, ccnt[op.eng]
        last = {}
        for op in self.ops:
            ws = {}
            for d in op.deps:
                k, v = d.sem_key, d.sem_val
                assert k is not None, (op.tag, d.tag)
                if last.get((op.eng, k), 0) < v and ws.get(k, 0) < v:
                    ws[k] = v
            for k, v in ws.items():
                last[(op.eng, k)] = v
            op.waits = sorted(ws.items())
        self.final_q = dict(qcnt)
        self.final_c = dict(ccnt)

    def emit(self, block, sems):
        per_eng = {}
        for op in self.ops:
            per_eng.setdefault(op.eng, []).append(op)

        def runner(ops):
            def go(eng):
                for op in ops:
                    for k, v in op.waits:
                        eng.wait_ge(sems[k], v)
                    inst = op.fn(eng)
                    if op.dma:
                        inst.then_inc(sems[op.sem_key], 16)
                    elif op.need_inc:
                        inst.then_inc(sems[op.sem_key], 1)
            return go

        for ename, attr in (("T", "tensor"), ("V", "vector"), ("S", "scalar"),
                            ("Y", "sync"), ("G", "gpsimd")):
            if ename in per_eng:
                getattr(block, attr)(runner(per_eng[ename]))


def build_core(T_CORE=2048, debug=False):
    C = T_CORE // NB
    TT = NB // P

    nc = bass.Bass("TRN2", target_bir_lowering=False)

    dx = nc.dram_tensor("x", [T_CORE, D], F32, kind="ExternalInput").ap()
    d_wqkv = nc.dram_tensor("W_qkv", [D, E3], F32, kind="ExternalInput").ap()
    d_bqkv = nc.dram_tensor("b_qkv", [E3], F32, kind="ExternalInput").ap()
    d_wout = nc.dram_tensor("W_out", [D, D], F32, kind="ExternalInput").ap()
    d_bout = nc.dram_tensor("b_out", [D], F32, kind="ExternalInput").ap()
    d_w1 = nc.dram_tensor("w1_w", [HID, D], F32, kind="ExternalInput").ap()
    d_w1s = nc.dram_tensor("w1_s", [HID], F32, kind="ExternalInput").ap()
    d_w2 = nc.dram_tensor("w2_w", [HID, D], F32, kind="ExternalInput").ap()
    d_w2s = nc.dram_tensor("w2_s", [HID], F32, kind="ExternalInput").ap()
    d_h1 = nc.dram_tensor("h1_w", [HID, HID], F32, kind="ExternalInput").ap()
    d_h1s = nc.dram_tensor("h1_s", [HID], F32, kind="ExternalInput").ap()
    d_h2 = nc.dram_tensor("h2_w", [HID, HID], F32, kind="ExternalInput").ap()
    d_h2s = nc.dram_tensor("h2_s", [HID], F32, kind="ExternalInput").ap()
    d_w3 = nc.dram_tensor("w3_w", [D, HID], F32, kind="ExternalInput").ap()
    d_w3s = nc.dram_tensor("w3_s", [D], F32, kind="ExternalInput").ap()
    d_ident = nc.dram_tensor("ident", [P, P], BF16, kind="ExternalInput").ap()
    dy = nc.dram_tensor("y", [T_CORE, D], F32, kind="ExternalOutput").ap()

    s_qkv_hi = nc.dram_tensor("s_qkv_hi", [D, E3], BF16, kind="Internal").ap()
    s_qkv_lo = nc.dram_tensor("s_qkv_lo", [D, E3], BF16, kind="Internal").ap()
    s_out_hi = nc.dram_tensor("s_out_hi", [D, D], BF16, kind="Internal").ap()
    s_out_lo = nc.dram_tensor("s_out_lo", [D, D], BF16, kind="Internal").ap()
    s_nat = {
        "w1": nc.dram_tensor("s_nat_w1", [HID, D], BF16, kind="Internal").ap(),
        "w2": nc.dram_tensor("s_nat_w2", [HID, D], BF16, kind="Internal").ap(),
        "h1": nc.dram_tensor("s_nat_h1", [HID, HID], BF16, kind="Internal").ap(),
        "h2": nc.dram_tensor("s_nat_h2", [HID, HID], BF16, kind="Internal").ap(),
        "w3": nc.dram_tensor("s_nat_w3", [D, HID], BF16, kind="Internal").ap(),
    }
    s_t = {
        "w1": nc.dram_tensor("s_t_w1", [D, HID], BF16, kind="Internal").ap(),
        "w2": nc.dram_tensor("s_t_w2", [D, HID], BF16, kind="Internal").ap(),
        "h1": nc.dram_tensor("s_t_h1", [HID, HID], BF16, kind="Internal").ap(),
        "h2": nc.dram_tensor("s_t_h2", [HID, HID], BF16, kind="Internal").ap(),
        "w3": nc.dram_tensor("s_t_w3", [HID, D], BF16, kind="Internal").ap(),
    }
    NROW = E3 + 2 * D
    s_bias = nc.dram_tensor("s_bias", [2, NROW], BF16, kind="Internal").ap()
    s_x2 = nc.dram_tensor("s_x2", [T_CORE, D], F32, kind="Internal").ap()

    dbg = {}
    if debug:
        for nm, shp, dt_ in (("dbg_qkv", [NB, E3], F32),
                             ("dbg_probs", [NB, NH * NH], F32),
                             ("dbg_ctx", [NB, D], F32),
                             ("dbg_x2", [NB, D], F32),
                             ("dbg_h0", [HID, NB], BF16),
                             ("dbg_h1p", [HID, NB], BF16),
                             ("dbg_xb", [P, 4 * D], F32),
                             ("dbg_stats", [P, 32], F32),
                             ("dbg_xnT", [P, KD * NB], BF16)):
            dbg[nm] = nc.dram_tensor(nm, shp, dt_, kind="ExternalOutput").ap()

    ctx_mgr = ExitStack()
    E = ctx_mgr.enter_context

    def addr_of(handle):
        name = handle.name
        for a in nc.m.functions[0].allocations:
            if isinstance(a, mybir.MemoryLocationSet):
                for ml in a.memorylocations:
                    if ml.name == name:
                        return ml.addr
        raise KeyError(name)

    ident = E(nc.sbuf_tensor("t_ident", [P, P], BF16))
    ones2 = E(nc.sbuf_tensor("t_ones2", [2, P], BF16))
    b2_qkv = E(nc.sbuf_tensor("t_b2qkv", [2, E3], BF16))
    b2_out = E(nc.sbuf_tensor("t_b2out", [2, D], BF16))
    s3pair = E(nc.sbuf_tensor("t_s3pair", [2, D], BF16))
    s3_rep = E(nc.sbuf_tensor("t_s3rep", [P, D], F32))
    s1c = E(nc.sbuf_tensor("t_s1c", [P, KH], F32))
    s2c = E(nc.sbuf_tensor("t_s2c", [P, KH], F32))
    h1sc = E(nc.sbuf_tensor("t_h1sc", [P, KH], F32))
    h2sc = E(nc.sbuf_tensor("t_h2sc", [P, KH], F32))
    stq = [E(nc.sbuf_tensor(f"t_st{i}", [P, TT], F32)) for i in range(8)]
    nm16 = E(nc.sbuf_tensor("t_nm16", [P, NH], F32))
    den16 = E(nc.sbuf_tensor("t_den16", [P, NH], F32))
    rec16 = E(nc.sbuf_tensor("t_rec16", [P, NH], F32))
    sqt = E(nc.sbuf_tensor("t_sq", [P, D], F32))
    xnf = E(nc.sbuf_tensor("t_xnf", [P, D], F32))
    tf512 = E(nc.sbuf_tensor("t_tf512", [P, NB], F32))
    tf512b = E(nc.sbuf_tensor("t_tf512b", [P, NB], F32))
    h0f = E(nc.sbuf_tensor("t_h0f", [P, NB], F32))

    xb = E(nc.sbuf_tensor("t_xb", [P, TT * D], F32))
    x2b = E(nc.sbuf_tensor("t_x2b", [P, TT * D], F32))
    tokh = E(nc.sbuf_tensor("t_tokh", [P, TT * D], BF16))
    tokl = E(nc.sbuf_tensor("t_tokl", [P, TT * D], BF16))
    xnT_hi = E(nc.sbuf_tensor("t_xnT_hi", [P, KD * NB], BF16))
    xnT_lo = E(nc.sbuf_tensor("t_xnT_lo", [P, KD * NB], BF16))
    wslab = E(nc.sbuf_tensor("t_wslab", [P, 2 * 4096], BF16))
    arena = E(nc.sbuf_tensor("t_arena", [P, 40960], BF16))
    abase = addr_of(arena)

    def at(name, shape, dtype, off):
        return nc.alloc_sbuf_tensor_at(name, shape, dtype, offset=abase + off)

    qkvb = at("v_qkv", [P, TT * E3], F32, 0)
    sc_tmp = at("v_sctmp", [P, NH * DK], F32, 49152)
    scoreb = at("v_score", [P, NH * NH], F32, 53248)
    probb = at("v_prob", [P, NH * NH], F32, 54272)
    acc_a = at("v_acca", [P, D], F32, 55296)
    acc_b = at("v_accb", [P, D], F32, 59392)
    ctxT_hi = at("v_ctxThi", [P, KD * NB], BF16, 63488)
    ctxT_lo = at("v_ctxTlo", [P, KD * NB], BF16, 71680)
    h0_hi = at("v_h0hi", [P, KH * NB], BF16, 0)
    h0_lo = at("v_h0lo", [P, KH * NB], BF16, 32768)
    w3slab = at("v_w3slab", [P, 8 * D], BF16, 65536)
    prepst_t = E(nc.sbuf_tensor("t_prepst", [P, 2 * HID], BF16))
    prepst = [prepst_t.ap()[:, 0:HID], prepst_t.ap()[:, HID:2 * HID]]

    def wslab_slot(slot):
        if slot < 2:
            return wslab.ap()[:, slot * 4096:(slot + 1) * 4096]
        return prepst_t.ap()[:, (slot - 2) * 4096:(slot - 1) * 4096]
    prepw = at("v_prepw", [P, E3], F32, 0)
    prep_hi = at("v_prephi", [P, E3], BF16, 12288)
    prep_lo = at("v_preplo", [P, E3], BF16, 18432)
    prow = at("v_prow", [1, NROW], F32, 24576)
    prow_b = at("v_prowb", [1, 2 * NROW], BF16, 45056)

    h1p_hi_parts = [
        nc.alloc_sbuf_tensor_at("v_h1phiA", [P, 8192], BF16, offset=addr_of(xb)),
        nc.alloc_sbuf_tensor_at("v_h1phiB", [P, 8192], BF16, offset=addr_of(x2b)),
    ]
    h1p_lo_parts = [
        nc.alloc_sbuf_tensor_at("v_h1ploA", [P, 4096], BF16, offset=addr_of(tokh)),
        nc.alloc_sbuf_tensor_at("v_h1ploB", [P, 4096], BF16, offset=addr_of(tokl)),
        nc.alloc_sbuf_tensor_at("v_h1ploC", [P, 4096], BF16, offset=addr_of(xnT_hi)),
        nc.alloc_sbuf_tensor_at("v_h1ploD", [P, 4096], BF16, offset=addr_of(xnT_lo)),
    ]

    def h1p_slice(hi, k):
        if hi:
            t = h1p_hi_parts[k // 16]
            k = k % 16
        else:
            t = h1p_lo_parts[k // 8]
            k = k % 8
        return t.ap()[:, k * NB:(k + 1) * NB]

    pb = [E(nc.psum_tensor(f"t_pb{i}", [P, NB], F32)) for i in range(8)]

    sems = {}
    for k in ("T", "V", "S", "G", "QY", "QG", "QS"):
        sems[k] = E(nc.semaphore("sem_" + k))
    done_sem = E(nc.semaphore("sem_done"))

    pl = Plan()
    A = pl.add
    r_pb = [Res(f"pb{i}") for i in range(8)]
    r_scr = {k: Res("scr_" + k) for k in
             ("qkv_hi", "qkv_lo", "out_hi", "out_lo",
              "nat_w1", "nat_w2", "nat_h1", "nat_h2", "nat_w3",
              "t_w1", "t_w2", "t_h1", "t_h2", "t_w3", "bias", "x2sp")}
    r_ident = Res("ident")
    r_ones2 = Res("ones2")
    r_b2q = Res("b2q")
    r_b2o = Res("b2o")
    r_s3p = Res("s3p")
    r_s3rep = Res("s3rep")
    r_scl = {k: Res("scl_" + k) for k in ("s1", "s2", "h1", "h2")}
    r_prepst = [Res("prepstA"), Res("prepstB")]
    r_prepw = Res("prepw")
    r_prow = Res("prow")
    r_xb = [Res(f"xb{t}") for t in range(4)]
    r_x2 = [Res(f"x2_{t}") for t in range(4)]
    r_tok = [Res(f"tok{t}") for t in range(4)]
    r_xnT = Res("xnT")
    r_qkv = [Res(f"qkv{t}") for t in range(4)]
    r_att = Res("att")
    r_scores = Res("scores")
    r_probs = Res("probs")
    r_smx = Res("smx")
    r_acc = Res("acc")
    r_ctxT = [Res(f"ctxT{t}") for t in range(4)]
    r_wslab = [Res("wslabA"), Res("wslabB"), Res("wslabC"), Res("wslabD")]
    r_w3slab = [Res("w3slabA"), Res("w3slabB")]
    r_h0 = Res("h0")
    r_h1p = Res("h1p")
    r_stats = Res("stats")
    r_tmpA = Res("tmpA")
    r_tmpB = Res("tmpB")
    r_tf = Res("tf")
    r_tfb = Res("tfb")
    r_h0f = Res("h0f")
    r_ydram = Res("ydram")
    r_dbg = Res("dbg")

    ATTN_ARENA = [r_qkv[0], r_qkv[1], r_qkv[2], r_qkv[3], r_att,
                  r_scores, r_probs, r_smx, r_acc,
                  r_ctxT[0], r_ctxT[1], r_ctxT[2], r_ctxT[3]]
    FFN_ARENA = [r_h0, r_w3slab[0], r_w3slab[1]]
    TOKSIDE = [r_xb[0], r_xb[1], r_xb[2], r_xb[3],
               r_x2[0], r_x2[1], r_x2[2], r_x2[3],
               r_tok[0], r_tok[1], r_tok[2], r_tok[3], r_xnT]

    A("Y", lambda e: e.dma_start(out=ident.ap()[:], in_=d_ident[:, :]),
      writes=[r_ident], dma=True, tag="ld_ident")
    A("V", lambda e: e.memset(ones2.ap()[:], 1.0), writes=[r_ones2], tag="ones2")
    for dst, src, rr in ((s1c, d_w1s, r_scl["s1"]), (s2c, d_w2s, r_scl["s2"]),
                         (h1sc, d_h1s, r_scl["h1"]), (h2sc, d_h2s, r_scl["h2"])):
        A("Y", (lambda dst, src: lambda e: e.dma_start(
            out=dst.ap()[:], in_=src.rearrange("(o p) -> p o", p=P),
            allow_slow_non_contiguous=True))(dst, src),
          writes=[rr], dma=True, tag="ld_scale")

    A("Y", lambda e: e.dma_start(out=prow.ap()[0:1, 0:E3], in_=d_bqkv.unsqueeze(0)),
      writes=[r_prow], dma=True, tag="ld_bqkv")
    A("Y", lambda e: e.dma_start(out=prow.ap()[0:1, E3:E3 + D], in_=d_bout.unsqueeze(0)),
      writes=[r_prow], dma=True, tag="ld_bout")
    A("Y", lambda e: e.dma_start(out=prow.ap()[0:1, E3 + D:NROW], in_=d_w3s.unsqueeze(0)),
      writes=[r_prow], dma=True, tag="ld_w3s")
    A("V", lambda e: e.tensor_copy(prow_b.ap()[0:1, 0:NROW], prow.ap()[0:1, 0:NROW]),
      reads=[r_prow], writes=[r_prow], tag="bias_hi")
    A("V", lambda e: e.tensor_sub(prow_b.ap()[0:1, NROW:2 * NROW],
                                  prow.ap()[0:1, 0:NROW], prow_b.ap()[0:1, 0:NROW]),
      reads=[r_prow], writes=[r_prow], tag="bias_lo")
    A("S", lambda e: e.dma_start(out=s_bias.rearrange("a b -> (a b)").unsqueeze(0),
                                 in_=prow_b.ap()[0:1, 0:2 * NROW]),
      reads=[r_prow], writes=[r_scr["bias"]], dma=True, tag="st_bias")
    A("Y", lambda e: e.dma_start(out=b2_qkv.ap()[:], in_=s_bias[:, 0:E3]),
      reads=[r_scr["bias"]], writes=[r_b2q], dma=True, tag="ld_b2qkv")
    A("Y", lambda e: e.dma_start(out=b2_out.ap()[:], in_=s_bias[:, E3:E3 + D]),
      reads=[r_scr["bias"]], writes=[r_b2o], dma=True, tag="ld_b2out")
    A("Y", lambda e: e.dma_start(out=s3pair.ap()[:], in_=s_bias[:, E3 + D:NROW]),
      reads=[r_scr["bias"]], writes=[r_s3p], dma=True, tag="ld_s3pair")
    for db in range(2):
        A("T", (lambda db: lambda e: e.matmul(
            pb[6 + db].ap()[:], ones2.ap()[:],
            s3pair.ap()[:, db * NB:(db + 1) * NB], start=True, stop=True))(db),
          reads=[r_ones2, r_s3p], writes=[r_pb[6 + db]], tag="s3rep_mm")
        A("S", (lambda db: lambda e: e.copy(
            s3_rep.ap()[:, db * NB:(db + 1) * NB], pb[6 + db].ap()[:]))(db),
          reads=[r_pb[6 + db]], writes=[r_s3rep], tag="s3rep_ev")

    for dsrc, dhi, dlo, width, rhi, rlo, tagn in (
            (d_wqkv, s_qkv_hi, s_qkv_lo, E3, r_scr["qkv_hi"], r_scr["qkv_lo"], "qkv"),
            (d_wout, s_out_hi, s_out_lo, D, r_scr["out_hi"], r_scr["out_lo"], "out")):
        for kt in range(KD):
            sl = slice(kt * P, (kt + 1) * P)
            A("G", (lambda dsrc, sl, width: lambda e: e.dma_start(
                out=prepw.ap()[:, 0:width], in_=dsrc[sl, :]))(dsrc, sl, width),
              writes=[r_prepw], dma=True, tag=f"p_{tagn}ld{kt}")
            A("V", (lambda width: lambda e: e.tensor_copy(
                prep_hi.ap()[:, 0:width], prepw.ap()[:, 0:width]))(width),
              reads=[r_prepw], writes=[r_prepw], tag=f"p_{tagn}hi{kt}")
            A("V", (lambda width: lambda e: e.tensor_sub(
                prep_lo.ap()[:, 0:width], prepw.ap()[:, 0:width],
                prep_hi.ap()[:, 0:width]))(width),
              reads=[r_prepw], writes=[r_prepw], tag=f"p_{tagn}lo{kt}")
            A("G", (lambda dhi, sl, width: lambda e: e.dma_start(
                out=dhi[sl, :], in_=prep_hi.ap()[:, 0:width]))(dhi, sl, width),
              reads=[r_prepw], writes=[rhi], dma=True, tag=f"p_{tagn}sh{kt}")
            A("G", (lambda dlo, sl, width: lambda e: e.dma_start(
                out=dlo[sl, :], in_=prep_lo.ap()[:, 0:width]))(dlo, sl, width),
              reads=[r_prepw], writes=[rlo], dma=True, tag=f"p_{tagn}sl{kt}")

    PREP_DIMS = {"w1": (HID, D), "w2": (HID, D), "h1": (HID, HID),
                 "h2": (HID, HID), "w3": (D, HID)}
    PREP_SRC = {"w1": d_w1, "w2": d_w2, "h1": d_h1, "h2": d_h2, "w3": d_w3}
    for wname in ("w1", "w2", "h1", "h2", "w3"):
        A("G", (lambda wname, dsrc: lambda e: e.dma_start(
            out=s_nat[wname][:, :], in_=dsrc[:, :]))(wname, PREP_SRC[wname]),
          writes=[r_scr["nat_" + wname]], dma=True, tag=f"p_cast_{wname}")

    def plan_prep_tr(wname):
        O, I = PREP_DIMS[wname]
        for it in range(I // P):
            sl = it % 2
            A("Y", (lambda wname, it, O, sl: lambda e: e.dma_start(
                out=prepst[sl][:, 0:O],
                in_=s_nat[wname][0:O, it * P:(it + 1) * P],
                transpose=True))(wname, it, O, sl),
              reads=[r_scr["nat_" + wname]], writes=[r_prepst[sl]], dma=True,
              tag=f"p_tld_{wname}_{it}")
            A("Y", (lambda wname, it, O, sl: lambda e: e.dma_start(
                out=s_t[wname][it * P:(it + 1) * P, 0:O],
                in_=prepst[sl][:, 0:O]))(wname, it, O, sl),
              reads=[r_prepst[sl]], writes=[r_scr["t_" + wname]], dma=True,
              tag=f"p_tst_{wname}_{it}")

    for _w in ("w1", "w2", "h1", "h2", "w3"):
        plan_prep_tr(_w)

    def mm_group(bank, pairs, extra_reads, tag):
        N = len(pairs)
        for i, (st, mv) in enumerate(pairs):
            A("T", (lambda st, mv, s0, s1: lambda e: e.matmul(
                pb[bank].ap()[:], st, mv, start=s0, stop=s1))(
                    st, mv, i == 0, i == N - 1),
              reads=list(extra_reads), writes=[r_pb[bank]], tag=f"{tag}_mm{i}")

    def mm_group_n(bank, width, pairs, extra_reads, tag):
        N = len(pairs)
        for i, (st, mv) in enumerate(pairs):
            A("T", (lambda st, mv, s0, s1: lambda e: e.matmul(
                pb[bank].ap()[:, 0:width], st, mv, start=s0, stop=s1))(
                    st, mv, i == 0, i == N - 1),
              reads=list(extra_reads), writes=[r_pb[bank]], tag=f"{tag}_mm{i}")

    def ln_stage(src_tile, src_res, tag):
        s_sx, s_sq, s_m, s_r = stq[0], stq[1], stq[2], stq[3]
        s_t1, s_t2, s_t3, s_t4 = stq[4], stq[5], stq[6], stq[7]
        for i in range(TT):
            A("V", (lambda i: lambda e: e.tensor_mul(
                sqt.ap()[:], src_tile(i), src_tile(i)))(i),
              reads=[src_res[i]], writes=[r_tmpA], tag=f"{tag}_sq{i}")
            A("V", (lambda i: lambda e: e.tensor_reduce(
                s_sx.ap()[:, i:i + 1], src_tile(i), AX.X, ALU.add))(i),
              reads=[src_res[i]], writes=[r_stats], tag=f"{tag}_sx{i}")
            A("V", (lambda i: lambda e: e.tensor_reduce(
                s_sq.ap()[:, i:i + 1], sqt.ap()[:], AX.X, ALU.add))(i),
              reads=[r_tmpA], writes=[r_stats], tag=f"{tag}_sqr{i}")
        A("V", lambda e: e.tensor_scalar_mul(s_m.ap()[:], s_sx.ap()[:], 1.0 / D),
          reads=[r_stats], writes=[r_stats], tag=f"{tag}_m")
        A("V", lambda e: e.tensor_scalar_mul(s_t1.ap()[:], s_sq.ap()[:], 1.0 / D),
          reads=[r_stats], writes=[r_stats], tag=f"{tag}_q")
        A("V", lambda e: e.tensor_mul(s_t2.ap()[:], s_m.ap()[:], s_m.ap()[:]),
          reads=[r_stats], writes=[r_stats], tag=f"{tag}_m2")
        A("V", lambda e: e.tensor_sub(s_t3.ap()[:], s_t1.ap()[:], s_t2.ap()[:]),
          reads=[r_stats], writes=[r_stats], tag=f"{tag}_v")
        A("V", lambda e: e.tensor_scalar_add(s_t4.ap()[:], s_t3.ap()[:], 1e-5),
          reads=[r_stats], writes=[r_stats], tag=f"{tag}_a")
        A("S", lambda e: e.activation(s_t1.ap()[:], s_t4.ap()[:], AF.Ln),
          reads=[r_stats], writes=[r_stats], tag=f"{tag}_ln")
        A("S", lambda e: e.activation(s_t2.ap()[:], s_t1.ap()[:], AF.Exp,
                                      scale=-0.5),
          reads=[r_stats], writes=[r_stats], tag=f"{tag}_y0")
        A("V", lambda e: e.tensor_mul(s_t1.ap()[:], s_t2.ap()[:], s_t2.ap()[:]),
          reads=[r_stats], writes=[r_stats], tag=f"{tag}_n1")
        A("V", lambda e: e.tensor_mul(s_t3.ap()[:], s_t1.ap()[:], s_t4.ap()[:]),
          reads=[r_stats], writes=[r_stats], tag=f"{tag}_n2")
        A("V", lambda e: e.tensor_scalar(s_t1.ap()[:], s_t3.ap()[:], -0.5, 1.5,
                                         ALU.mult, ALU.add),
          reads=[r_stats], writes=[r_stats], tag=f"{tag}_n3")
        A("V", lambda e: e.tensor_mul(s_r.ap()[:], s_t2.ap()[:], s_t1.ap()[:]),
          reads=[r_stats], writes=[r_stats], tag=f"{tag}_r")
        for i in range(TT):
            A("V", (lambda i: lambda e: e.tensor_scalar(
                xnf.ap()[:], src_tile(i), s_m.ap()[:, i:i + 1],
                s_r.ap()[:, i:i + 1], ALU.subtract, ALU.mult))(i),
              reads=[src_res[i], r_stats], writes=[r_tmpB], tag=f"{tag}_nm{i}")
            A("V", (lambda i: lambda e: e.tensor_copy(
                tokh.ap()[:, i * D:(i + 1) * D], xnf.ap()[:]))(i),
              reads=[r_tmpB], writes=[r_tok[i]], tag=f"{tag}_hi{i}")
            A("V", (lambda i: lambda e: e.tensor_sub(
                tokl.ap()[:, i * D:(i + 1) * D], xnf.ap()[:],
                tokh.ap()[:, i * D:(i + 1) * D]))(i),
              reads=[r_tmpB], writes=[r_tok[i]], tag=f"{tag}_lo{i}")
        for hi in (True, False):
            tsrc = tokh if hi else tokl
            tdst = xnT_hi if hi else xnT_lo
            for j in range(KD):
                bank = 2 + (j % 2)
                pbf = pb[bank].ap().bitcast(BF16)
                for i in range(TT):
                    A("T", (lambda i, j, pbf, tsrc: lambda e: e.transpose(
                        pbf[:, i * P:(i + 1) * P],
                        tsrc.ap()[:, i * D + j * P: i * D + (j + 1) * P],
                        ident.ap()[:]))(i, j, pbf, tsrc),
                      reads=[r_tok[i], r_ident], writes=[r_pb[bank]],
                      tag=f"{tag}_tr{j}_{i}")
                A("S", (lambda j, pbf, tdst: lambda e: e.copy(
                    tdst.ap()[:, j * NB:(j + 1) * NB], pbf[:, 0:NB]))(j, pbf, tdst),
                  reads=[r_pb[bank]], writes=[r_xnT], tag=f"{tag}_te{j}")

    xb_tile = lambda i: xb.ap()[:, i * D:(i + 1) * D]
    x2_tile = lambda i: x2b.ap()[:, i * D:(i + 1) * D]

    for c in range(C):
        tok0 = c * NB
        NSLOT = 2 if c == 0 else 4
        pl.alias_barrier(TOKSIDE, [r_h1p])
        pl.alias_barrier(ATTN_ARENA, FFN_ARENA + [r_prepw, r_prow])
        if c == 1:
            pl.alias_barrier([r_wslab[2], r_wslab[3]],
                             [r_prepst[0], r_prepst[1]])

        A("S" if c == 0 else "Y", (lambda tok0: lambda e: e.dma_start(
            out=xb.ap().rearrange("p (i d) -> p i d", d=D),
            in_=dx[tok0:tok0 + NB, :].rearrange("(i p) d -> p i d", p=P)))(tok0),
          writes=[r_xb[0], r_xb[1], r_xb[2], r_xb[3]], dma=True, tag=f"c{c}_x")

        ln_stage(xb_tile, r_xb, f"c{c}L1")

        if debug and c == 0:
            A("S", lambda e: e.dma_start(out=dbg["dbg_xb"][:, :], in_=xb.ap()[:]),
              reads=[r_xb[0], r_xb[1], r_xb[2], r_xb[3]], writes=[r_dbg], dma=True, tag="dxb")
            A("S", (lambda i: lambda e: e.dma_start(out=dbg["dbg_stats"][:, 0:4], in_=stq[0].ap()[:]))(0),
              reads=[r_stats], writes=[r_dbg], dma=True, tag="dst0")
            A("S", (lambda i: lambda e: e.dma_start(out=dbg["dbg_stats"][:, 4:8], in_=stq[1].ap()[:]))(1),
              reads=[r_stats], writes=[r_dbg], dma=True, tag="dst1")
            A("S", (lambda i: lambda e: e.dma_start(out=dbg["dbg_stats"][:, 8:12], in_=stq[2].ap()[:]))(2),
              reads=[r_stats], writes=[r_dbg], dma=True, tag="dst2")
            A("S", (lambda i: lambda e: e.dma_start(out=dbg["dbg_stats"][:, 12:16], in_=stq[3].ap()[:]))(3),
              reads=[r_stats], writes=[r_dbg], dma=True, tag="dst3")
            A("S", (lambda i: lambda e: e.dma_start(out=dbg["dbg_stats"][:, 16:20], in_=stq[4].ap()[:]))(4),
              reads=[r_stats], writes=[r_dbg], dma=True, tag="dst4")
            A("S", (lambda i: lambda e: e.dma_start(out=dbg["dbg_stats"][:, 20:24], in_=stq[5].ap()[:]))(5),
              reads=[r_stats], writes=[r_dbg], dma=True, tag="dst5")
            A("S", (lambda i: lambda e: e.dma_start(out=dbg["dbg_stats"][:, 24:28], in_=stq[6].ap()[:]))(6),
              reads=[r_stats], writes=[r_dbg], dma=True, tag="dst6")
            A("S", (lambda i: lambda e: e.dma_start(out=dbg["dbg_stats"][:, 28:32], in_=stq[7].ap()[:]))(7),
              reads=[r_stats], writes=[r_dbg], dma=True, tag="dst7")
            A("S", lambda e: e.dma_start(out=dbg["dbg_xnT"][:, :], in_=xnT_hi.ap()[:]),
              reads=[r_xnT], writes=[r_dbg], dma=True, tag="dxnT")

        EBW = 256
        ldq = "S" if c == 0 else "Y"
        gi = 0
        for t in range(TT):
            for eb in range(E3 // EBW):
                slot = gi % NSLOT
                wv = wslab_slot(slot)
                A(ldq, (lambda eb, wv: lambda e: e.dma_start(
                    out=wv[:, 0:KD * EBW].rearrange("p (k n) -> p k n", n=EBW),
                    in_=s_qkv_hi[:, eb * EBW:(eb + 1) * EBW]
                    .rearrange("(k p) n -> p k n", p=P)))(eb, wv),
                  reads=[r_scr["qkv_hi"]], writes=[r_wslab[slot]], dma=True,
                  tag=f"c{c}_qwh{t}_{eb}")
                A(ldq, (lambda eb, wv: lambda e: e.dma_start(
                    out=wv[:, KD * EBW:2 * KD * EBW].rearrange("p (k n) -> p k n", n=EBW),
                    in_=s_qkv_lo[:, eb * EBW:(eb + 1) * EBW]
                    .rearrange("(k p) n -> p k n", p=P)))(eb, wv),
                  reads=[r_scr["qkv_lo"]], writes=[r_wslab[slot]], dma=True,
                  tag=f"c{c}_qwl{t}_{eb}")
                bank = gi % 2
                pairs = [(ones2.ap()[:], b2_qkv.ap()[:, eb * EBW:(eb + 1) * EBW])]
                for k in range(KD):
                    xh_sl = xnT_hi.ap()[:, k * NB + t * P: k * NB + (t + 1) * P]
                    xl_sl = xnT_lo.ap()[:, k * NB + t * P: k * NB + (t + 1) * P]
                    whi = wv[:, k * EBW:(k + 1) * EBW]
                    wlo = wv[:, KD * EBW + k * EBW: KD * EBW + (k + 1) * EBW]
                    pairs += [(xh_sl, whi), (xl_sl, whi), (xh_sl, wlo)]
                mm_group_n(bank, EBW, pairs,
                           [r_xnT, r_wslab[slot], r_ones2, r_b2q], f"c{c}_qk{t}_{eb}")
                A("S", (lambda t, eb, bank: lambda e: e.copy(
                    qkvb.ap()[:, t * E3 + eb * EBW: t * E3 + (eb + 1) * EBW],
                    pb[bank].ap()[:, 0:EBW]))(t, eb, bank),
                  reads=[r_pb[bank]], writes=[r_qkv[t]], tag=f"c{c}_qe{t}_{eb}")
                gi += 1

        if debug and c == 0:
            for t in range(TT):
                A("S", (lambda t: lambda e: e.dma_start(
                    out=dbg["dbg_qkv"][t * P:(t + 1) * P, :],
                    in_=qkvb.ap()[:, t * E3:(t + 1) * E3]))(t),
                  reads=[r_qkv[t]], writes=[r_dbg], dma=True, tag=f"dq{t}")

        sc3 = sc_tmp.ap().rearrange("p (h c) -> p h c", c=DK)
        sco3 = scoreb.ap().rearrange("p (h g) -> p h g", g=NH)
        pr3 = probb.ap().rearrange("p (h g) -> p h g", g=NH)
        acc3 = [acc_a.ap().rearrange("p (h c) -> p h c", c=DK),
                acc_b.ap().rearrange("p (h c) -> p h c", c=DK)]
        for t in range(TT):
            q3 = qkvb.ap()[:, t * E3:(t + 1) * E3].rearrange(
                "p (h c) -> p h c", c=3 * DK)
            xn3 = xnf.ap().rearrange("p (h c) -> p h c", c=DK)

            def sc_mult(g):
                buf = sc3 if g % 2 == 0 else xn3
                wres = r_att if g % 2 == 0 else r_tmpB
                eng = "V" if g % 3 == 1 else "G"
                A(eng, (lambda q3, g, buf: lambda e: e.tensor_mul(
                    buf, q3[:, :, 0:DK],
                    q3[:, g:g + 1, DK:2 * DK].broadcast_to([P, NH, DK])))(q3, g, buf),
                  reads=[r_qkv[t]], writes=[wres], tag=f"c{c}_scm{t}_{g}")

            def sc_red(g):
                buf = sc3 if g % 2 == 0 else xn3
                rres = r_att if g % 2 == 0 else r_tmpB
                A("V", (lambda g, buf: lambda e: e.tensor_reduce(
                    sco3[:, :, g:g + 1], buf, AX.X, ALU.add))(g, buf),
                  reads=[rres], writes=[r_scores], tag=f"c{c}_scr{t}_{g}")

            sc_mult(0)
            for g in range(1, NH):
                sc_mult(g)
                sc_red(g - 1)
            sc_red(NH - 1)
            A("V", lambda e: e.tensor_reduce(nm16.ap()[:].unsqueeze(2), sco3,
                                             AX.X, ALU.max, negate=True),
              reads=[r_scores], writes=[r_smx], tag=f"c{c}_nm{t}")
            A("V", lambda e: e.tensor_tensor(
                pr3, sco3, nm16.ap()[:].unsqueeze(2).broadcast_to([P, NH, NH]),
                ALU.add),
              reads=[r_scores, r_smx], writes=[r_probs], tag=f"c{c}_sub{t}")
            A("S", lambda e: e.activation(scoreb.ap()[:], probb.ap()[:], AF.Exp,
                                          scale=0.125),
              reads=[r_probs], writes=[r_scores], tag=f"c{c}_exp{t}")
            A("V", lambda e: e.tensor_copy(probb.ap()[:], scoreb.ap()[:]),
              reads=[r_scores], writes=[r_probs], tag=f"c{c}_pcp{t}")
            A("V", lambda e: e.tensor_reduce(den16.ap()[:].unsqueeze(2), sco3,
                                             AX.X, ALU.add),
              reads=[r_scores], writes=[r_smx], tag=f"c{c}_den{t}")
            A("V", lambda e: e.reciprocal(rec16.ap()[:], den16.ap()[:]),
              reads=[r_smx], writes=[r_smx], tag=f"c{c}_rc{t}")
            A("V", (lambda q3: lambda e: e.tensor_mul(
                acc3[0], pr3[:, :, 0:1].broadcast_to([P, NH, DK]),
                q3[:, 0:1, 2 * DK:3 * DK].broadcast_to([P, NH, DK])))(q3),
              reads=[r_probs, r_qkv[t]], writes=[r_acc], tag=f"c{c}_cx0{t}")
            accf = [acc_a.ap()[:], acc_b.ap()[:]]
            buff = {True: sc_tmp.ap()[:], False: xnf.ap()[:]}
            cur = 0
            for g in range(1, NH):
                buf = sc3 if g % 2 == 0 else xn3
                fbuf = buff[g % 2 == 0]
                bres = r_att if g % 2 == 0 else r_tmpB
                eng = "G" if g % 3 != 1 else "V"
                A(eng, (lambda q3, g, buf: lambda e: e.tensor_mul(
                    buf, pr3[:, :, g:g + 1].broadcast_to([P, NH, DK]),
                    q3[:, g:g + 1, 2 * DK:3 * DK].broadcast_to([P, NH, DK])))(q3, g, buf),
                  reads=[r_probs, r_qkv[t]], writes=[bres], tag=f"c{c}_cm{t}_{g}")
                A("V", (lambda cur, fbuf: lambda e: e.tensor_tensor(
                    accf[1 - cur], accf[cur], fbuf, ALU.add))(cur, fbuf),
                  reads=[bres, r_acc], writes=[r_acc], tag=f"c{c}_ca{t}_{g}")
                cur = 1 - cur
            A("V", (lambda cur: lambda e: e.tensor_tensor(
                acc3[1 - cur], acc3[cur],
                rec16.ap()[:].unsqueeze(2).broadcast_to([P, NH, DK]),
                ALU.mult))(cur),
              reads=[r_acc, r_smx], writes=[r_acc], tag=f"c{c}_cfs{t}")
            cur = 1 - cur
            acc_fin = acc_a if cur == 0 else acc_b
            if debug and c == 0:
                A("S", (lambda t, acc_fin: lambda e: e.dma_start(
                    out=dbg["dbg_ctx"][t * P:(t + 1) * P, :],
                    in_=acc_fin.ap()[:]))(t, acc_fin),
                  reads=[r_acc], writes=[r_dbg], dma=True, tag=f"dc{t}")
            A("V", (lambda t, acc_fin: lambda e: e.tensor_copy(
                tokh.ap()[:, t * D:(t + 1) * D], acc_fin.ap()[:]))(t, acc_fin),
              reads=[r_acc], writes=[r_tok[t]], tag=f"c{c}_chi{t}")
            A("V", (lambda t, acc_fin: lambda e: e.tensor_sub(
                tokl.ap()[:, t * D:(t + 1) * D], acc_fin.ap()[:],
                tokh.ap()[:, t * D:(t + 1) * D]))(t, acc_fin),
              reads=[r_acc], writes=[r_tok[t]], tag=f"c{c}_clo{t}")
            for hi in (True, False):
                tsrc = tokh if hi else tokl
                tdst = ctxT_hi if hi else ctxT_lo
                for jj in range(2):
                    bank = 2 + jj
                    pbf = pb[bank].ap().bitcast(BF16)
                    for j4 in range(4):
                        j = jj * 4 + j4
                        A("T", (lambda t, j, j4, pbf, tsrc: lambda e: e.transpose(
                            pbf[:, j4 * P:(j4 + 1) * P],
                            tsrc.ap()[:, t * D + j * P: t * D + (j + 1) * P],
                            ident.ap()[:]))(t, j, j4, pbf, tsrc),
                          reads=[r_tok[t], r_ident], writes=[r_pb[bank]],
                          tag=f"c{c}_ct{t}_{j}")
                    dst = tdst.ap().rearrange("p (j n) -> p j n", n=NB)[
                        :, jj * 4:(jj + 1) * 4, t * P:(t + 1) * P]
                    src = pbf[:, 0:4 * P].rearrange("p (j n) -> p j n", n=P)
                    A("S", (lambda dst, src: lambda e: e.copy(dst, src))(dst, src),
                      reads=[r_pb[bank]], writes=[r_ctxT[t]], tag=f"c{c}_ce{t}_{jj}")

        gi = 0
        for eb in range(D // EBW):
            slot = gi % NSLOT
            wv = wslab_slot(slot)
            A(ldq, (lambda eb, wv: lambda e: e.dma_start(
                out=wv[:, 0:KD * EBW].rearrange("p (k n) -> p k n", n=EBW),
                in_=s_out_hi[:, eb * EBW:(eb + 1) * EBW]
                .rearrange("(k p) n -> p k n", p=P)))(eb, wv),
              reads=[r_scr["out_hi"]], writes=[r_wslab[slot]], dma=True,
              tag=f"c{c}_owh{eb}")
            A(ldq, (lambda eb, wv: lambda e: e.dma_start(
                out=wv[:, KD * EBW:2 * KD * EBW].rearrange("p (k n) -> p k n", n=EBW),
                in_=s_out_lo[:, eb * EBW:(eb + 1) * EBW]
                .rearrange("(k p) n -> p k n", p=P)))(eb, wv),
              reads=[r_scr["out_lo"]], writes=[r_wslab[slot]], dma=True,
              tag=f"c{c}_owl{eb}")
            for t in range(TT):
                bank = (eb * TT + t) % 2
                pairs = [(ones2.ap()[:], b2_out.ap()[:, eb * EBW:(eb + 1) * EBW])]
                for k in range(KD):
                    ch_sl = ctxT_hi.ap()[:, k * NB + t * P: k * NB + (t + 1) * P]
                    cl_sl = ctxT_lo.ap()[:, k * NB + t * P: k * NB + (t + 1) * P]
                    whi = wv[:, k * EBW:(k + 1) * EBW]
                    wlo = wv[:, KD * EBW + k * EBW: KD * EBW + (k + 1) * EBW]
                    pairs += [(ch_sl, whi), (cl_sl, whi), (ch_sl, wlo)]
                mm_group_n(bank, EBW, pairs,
                           [r_ctxT[t], r_wslab[slot], r_ones2, r_b2o],
                           f"c{c}_ok{t}_{eb}")
                A("V", (lambda t, eb, bank: lambda e: e.tensor_add(
                    x2b.ap()[:, t * D + eb * EBW: t * D + (eb + 1) * EBW],
                    pb[bank].ap()[:, 0:EBW],
                    xb.ap()[:, t * D + eb * EBW: t * D + (eb + 1) * EBW]))(t, eb, bank),
                  reads=[r_pb[bank], r_xb[t]], writes=[r_x2[t]],
                  tag=f"c{c}_oe{t}_{eb}")
            gi += 1

        if debug and c == 0:
            for t in range(TT):
                A("S", (lambda t: lambda e: e.dma_start(
                    out=dbg["dbg_x2"][t * P:(t + 1) * P, :],
                    in_=x2b.ap()[:, t * D:(t + 1) * D]))(t),
                  reads=[r_x2[t]], writes=[r_dbg], dma=True, tag=f"dx2{t}")

        ln_stage(x2_tile, r_x2, f"c{c}L2")
        A("S", (lambda tok0: lambda e: e.dma_start(
            out=s_x2[tok0:tok0 + NB, :].rearrange("(i p) d -> p i d", p=P),
            in_=x2b.ap().rearrange("p (i d) -> p i d", d=D)))(tok0),
          reads=[r_x2[0], r_x2[1], r_x2[2], r_x2[3]], writes=[r_scr["x2sp"]],
          dma=True, tag=f"c{c}_sp")

        pl.alias_barrier(FFN_ARENA, ATTN_ARENA + [r_prepst[0], r_prepst[1]])
        OGW = 2
        for og in range(KH // OGW):
            slot = og % NSLOT
            wv = wslab_slot(slot)
            ocols = slice(og * OGW * P, (og + 1) * OGW * P)
            A(ldq, (lambda ocols, wv: lambda e: e.dma_start(
                out=wv[:, 0:KD * OGW * P].rearrange("p (k o) -> p k o", o=OGW * P),
                in_=s_t["w1"][:, ocols].rearrange("(k p) o -> p k o", p=P)))(
                    ocols, wv),
              reads=[r_scr["t_w1"]], writes=[r_wslab[slot]], dma=True,
              tag=f"c{c}_w1l{og}")
            A(ldq, (lambda ocols, wv: lambda e: e.dma_start(
                out=wv[:, KD * OGW * P:2 * KD * OGW * P]
                .rearrange("p (k o) -> p k o", o=OGW * P),
                in_=s_t["w2"][:, ocols].rearrange("(k p) o -> p k o", p=P)))(
                    ocols, wv),
              reads=[r_scr["t_w2"]], writes=[r_wslab[slot]], dma=True,
              tag=f"c{c}_w2l{og}")
            for ol in range(OGW):
                o = og * OGW + ol
                b1 = o % 2
                b2b = 4 + (o % 2)
                pairs1, pairs2 = [], []
                for k in range(KD):
                    w1sl = wv[:, k * OGW * P + ol * P: k * OGW * P + (ol + 1) * P]
                    w2sl = wv[:, KD * OGW * P + k * OGW * P + ol * P:
                              KD * OGW * P + k * OGW * P + (ol + 1) * P]
                    mh = xnT_hi.ap()[:, k * NB:(k + 1) * NB]
                    mlo = xnT_lo.ap()[:, k * NB:(k + 1) * NB]
                    pairs1 += [(w1sl, mh), (w1sl, mlo)]
                    pairs2 += [(w2sl, mh), (w2sl, mlo)]
                mm_group(b1, pairs1, [r_xnT, r_wslab[slot]], f"c{c}_t1_{o}")
                mm_group(b2b, pairs2, [r_xnT, r_wslab[slot]], f"c{c}_t2_{o}")
                tf = tf512 if o % 2 == 0 else tf512b
                rtf = r_tf if o % 2 == 0 else r_tfb
                A("S", (lambda o, b1, tf: lambda e: e.activation(
                    tf.ap()[:], pb[b1].ap()[:], AF.Silu,
                    scale=s1c.ap()[:, o:o + 1]))(o, b1, tf),
                  reads=[r_pb[b1], r_scl["s1"]], writes=[rtf], tag=f"c{c}_si{o}")
                A("V", (lambda o, b2b, tf: lambda e: e.scalar_tensor_tensor(
                    h0f.ap()[:], pb[b2b].ap()[:], s2c.ap()[:, o:o + 1],
                    tf.ap()[:], ALU.mult, ALU.mult))(o, b2b, tf),
                  reads=[r_pb[b2b], rtf, r_scl["s2"]], writes=[r_h0f],
                  tag=f"c{c}_h0f{o}")
                A("V", (lambda o: lambda e: e.tensor_copy(
                    h0_hi.ap()[:, o * NB:(o + 1) * NB], h0f.ap()[:]))(o),
                  reads=[r_h0f], writes=[r_h0], tag=f"c{c}_h0h{o}")
                A("V", (lambda o: lambda e: e.tensor_sub(
                    h0_lo.ap()[:, o * NB:(o + 1) * NB], h0f.ap()[:],
                    h0_hi.ap()[:, o * NB:(o + 1) * NB]))(o),
                  reads=[r_h0f], writes=[r_h0], tag=f"c{c}_h0l{o}")

        if debug and c == 0:
            A("S", lambda e: e.dma_start(
                out=dbg["dbg_h0"].rearrange("(k p) n -> p k n", p=P),
                in_=h0_hi.ap().rearrange("p (k n) -> p k n", n=NB)),
              reads=[r_h0], writes=[r_dbg], dma=True, tag="dh0")

        pl.alias_barrier([r_h1p], TOKSIDE)
        for sidx, wkey, scl, rscl, shi, slo, sres, dhi, dlo, dres in (
                (0, "h1", h1sc, r_scl["h1"],
                 lambda k: h0_hi.ap()[:, k * NB:(k + 1) * NB],
                 lambda k: h0_lo.ap()[:, k * NB:(k + 1) * NB], r_h0,
                 lambda o: h1p_slice(True, o), lambda o: h1p_slice(False, o), r_h1p),
                (1, "h2", h2sc, r_scl["h2"],
                 lambda k: h1p_slice(True, k), lambda k: h1p_slice(False, k), r_h1p,
                 lambda o: h0_hi.ap()[:, o * NB:(o + 1) * NB],
                 lambda o: h0_lo.ap()[:, o * NB:(o + 1) * NB], r_h0)):
            OGH = 2
            NPAIR = 1 if c == 0 else 2
            for og in range(KH // OGH):
                pairi = og % NPAIR
                wv = (wslab.ap()[:, 0:8192] if pairi == 0
                      else prepst_t.ap()[:, 0:8192])
                pres = ([r_wslab[0], r_wslab[1]] if pairi == 0
                        else [r_wslab[2], r_wslab[3]])
                ocols = slice(og * OGH * P, (og + 1) * OGH * P)
                A(ldq, (lambda wkey, ocols, wv: lambda e: e.dma_start(
                    out=wv[:, 0:8192].rearrange("p (k o) -> p k o", o=OGH * P),
                    in_=s_t[wkey][:, ocols].rearrange("(k p) o -> p k o", p=P)))(
                        wkey, ocols, wv),
                  reads=[r_scr["t_" + wkey]], writes=pres, dma=True,
                  tag=f"c{c}_{wkey}l{og}")
                for ol in range(OGH):
                    o = og * OGH + ol
                    bank = o % 2
                    tf = tf512 if o % 2 == 0 else tf512b
                    rtf = r_tf if o % 2 == 0 else r_tfb
                    pairs = []
                    for k in range(KH):
                        wsl = wv[:, k * OGH * P + ol * P: k * OGH * P + (ol + 1) * P]
                        pairs += [(wsl, shi(k)), (wsl, slo(k))]
                    mm_group(bank, pairs, [sres] + pres,
                             f"c{c}_{wkey}_{o}")
                    A("S", (lambda o, bank, scl, tf: lambda e: e.activation(
                        tf.ap()[:], pb[bank].ap()[:], AF.Gelu,
                        scale=scl.ap()[:, o:o + 1]))(o, bank, scl, tf),
                      reads=[r_pb[bank], rscl], writes=[rtf],
                      tag=f"c{c}_{wkey}g{o}")
                    A("V", (lambda o, dhi, tf: lambda e: e.tensor_copy(
                        dhi(o), tf.ap()[:]))(o, dhi, tf),
                      reads=[rtf], writes=[dres], tag=f"c{c}_{wkey}h{o}")
                    A("V", (lambda o, dhi, dlo, tf: lambda e: e.tensor_sub(
                        dlo(o), tf.ap()[:], dhi(o)))(o, dhi, dlo, tf),
                      reads=[rtf], writes=[dres], tag=f"c{c}_{wkey}lo{o}")

            if debug and c == 0 and sidx == 0:
                for pi in range(2):
                    A("S", (lambda pi: lambda e: e.dma_start(
                        out=dbg["dbg_h1p"][pi * 16 * P:(pi + 1) * 16 * P, :]
                        .rearrange("(k p) n -> p k n", p=P),
                        in_=h1p_hi_parts[pi].ap().rearrange("p (k n) -> p k n", n=NB)))(pi),
                      reads=[r_h1p], writes=[r_dbg], dma=True, tag=f"dh1p{pi}")

        pl.alias_barrier([r_x2[0], r_x2[1], r_x2[2], r_x2[3]], [r_h1p])
        A("S" if c == 0 else "Y", (lambda tok0: lambda e: e.dma_start(
            out=x2b.ap().rearrange("p (i d) -> p i d", d=D),
            in_=s_x2[tok0:tok0 + NB, :].rearrange("(i p) d -> p i d", p=P)))(tok0),
          reads=[r_scr["x2sp"]], writes=[r_x2[0], r_x2[1], r_x2[2], r_x2[3]],
          dma=True, tag=f"c{c}_x2r")
        NPIECE = 4
        for piece in range(KH // NPIECE):
            wslot = piece % 2
            w3v = w3slab.ap()[:, wslot * NPIECE * D:(wslot + 1) * NPIECE * D]
            A(ldq, (lambda piece, w3v: lambda e: e.dma_start(
                out=w3v.rearrange("p (k d) -> p k d", d=D),
                in_=s_t["w3"][piece * NPIECE * P:(piece + 1) * NPIECE * P, :]
                .rearrange("(k p) d -> p k d", p=P)))(piece, w3v),
              reads=[r_scr["t_w3"]], writes=[r_w3slab[wslot]], dma=True,
              tag=f"c{c}_w3l{piece}")
            for t in range(TT):
                for kl in range(NPIECE):
                    k = piece * NPIECE + kl
                    sh_ = h0_hi.ap()[:, k * NB + t * P: k * NB + (t + 1) * P]
                    sl_ = h0_lo.ap()[:, k * NB + t * P: k * NB + (t + 1) * P]
                    last = (piece == KH // NPIECE - 1 and kl == NPIECE - 1)
                    for db in range(2):
                        bank = t * 2 + db
                        mv = w3v[:, kl * D + db * NB: kl * D + (db + 1) * NB]
                        A("T", (lambda st, mv, bank, s0, s1: lambda e: e.matmul(
                            pb[bank].ap()[:], st, mv, start=s0, stop=s1))(
                                sh_, mv, bank, piece == 0 and kl == 0, False),
                          reads=[r_h0, r_w3slab[wslot]], writes=[r_pb[bank]],
                          tag=f"c{c}_w3h_p{piece}t{t}k{kl}d{db}")
                        A("T", (lambda st, mv, bank, s0, s1: lambda e: e.matmul(
                            pb[bank].ap()[:], st, mv, start=s0, stop=s1))(
                                sl_, mv, bank, False, last),
                          reads=[r_h0, r_w3slab[wslot]], writes=[r_pb[bank]],
                          tag=f"c{c}_w3lo_p{piece}t{t}k{kl}d{db}")
        pl.alias_barrier([r_tok[0], r_tok[1], r_tok[2], r_tok[3]], [r_h1p])
        ytokh = tokh.ap().bitcast(F32)
        ytokl = tokl.ap().bitcast(F32)
        def ystage(t, db):
            base = (t % 2) * D + db * NB
            return (ytokh if t < 2 else ytokl)[:, base:base + NB]
        for t in range(TT):
            for db in range(2):
                bank = t * 2 + db
                A("V", (lambda bank, db: lambda e: e.tensor_mul(
                    tf512.ap()[:], pb[bank].ap()[:],
                    s3_rep.ap()[:, db * NB:(db + 1) * NB]))(bank, db),
                  reads=[r_pb[bank], r_s3rep], writes=[r_tf],
                  tag=f"c{c}_w3s{t}_{db}")
                A("V", (lambda t, db: lambda e: e.tensor_add(
                    ystage(t, db), tf512.ap()[:],
                    x2b.ap()[:, t * D + db * NB: t * D + (db + 1) * NB]))(t, db),
                  reads=[r_tf, r_x2[t]], writes=[r_tok[t]], tag=f"c{c}_y{t}_{db}")
        A("S", (lambda tok0, ytokh: lambda e: e.dma_start(
            out=dy[tok0:tok0 + NB, :].rearrange("(i p) d -> p i d", p=P)[:, 0:2, :],
            in_=ytokh.rearrange("p (i d) -> p i d", d=D)))(tok0, ytokh),
          reads=[r_tok[0], r_tok[1]], writes=[r_ydram], dma=True,
          tag=f"c{c}_ysta")
        A("S", (lambda tok0, ytokl: lambda e: e.dma_start(
            out=dy[tok0:tok0 + NB, :].rearrange("(i p) d -> p i d", p=P)[:, 2:4, :],
            in_=ytokl.rearrange("p (i d) -> p i d", d=D)))(tok0, ytokl),
          reads=[r_tok[2], r_tok[3]], writes=[r_ydram], dma=True,
          tag=f"c{c}_ystb")

    A("S", lambda e: e.sem_inc(done_sem, 1), reads=[r_ydram, r_dbg], tag="fence")

    pl.finalize()
    with nc.Block() as block:
        pl.emit(block, sems)
    ctx_mgr.close()
    return nc, (list(dbg.keys()) if debug else []), pl


N_CORES = 8
T_FULL = 16384
T_CORE_DEFAULT = T_FULL // N_CORES


def kernel(L0_attn_x, L0_attn_W_qkv, L0_attn_b_qkv, L0_attn_W_out, L0_attn_b_out,
           L0_ffn_w1_w, L0_ffn_w1_s, L0_ffn_w2_w, L0_ffn_w2_s,
           L0_ffn_h1_w, L0_ffn_h1_s, L0_ffn_h2_w, L0_ffn_h2_s,
           L0_ffn_w3_w, L0_ffn_w3_s):
    x = np.ascontiguousarray(np.asarray(L0_attn_x).reshape(T_FULL, D))
    weights = {
        "W_qkv": np.ascontiguousarray(L0_attn_W_qkv),
        "b_qkv": np.ascontiguousarray(L0_attn_b_qkv),
        "W_out": np.ascontiguousarray(L0_attn_W_out),
        "b_out": np.ascontiguousarray(L0_attn_b_out),
        "w1_w": np.ascontiguousarray(L0_ffn_w1_w),
        "w1_s": np.ascontiguousarray(L0_ffn_w1_s),
        "w2_w": np.ascontiguousarray(L0_ffn_w2_w),
        "w2_s": np.ascontiguousarray(L0_ffn_w2_s),
        "h1_w": np.ascontiguousarray(L0_ffn_h1_w),
        "h1_s": np.ascontiguousarray(L0_ffn_h1_s),
        "h2_w": np.ascontiguousarray(L0_ffn_h2_w),
        "h2_s": np.ascontiguousarray(L0_ffn_h2_s),
        "w3_w": np.ascontiguousarray(L0_ffn_w3_w),
        "w3_s": np.ascontiguousarray(L0_ffn_w3_s),
    }
    nc, _dbg, _pl = build_core(T_CORE=T_CORE_DEFAULT, debug=False)
    ident = np.eye(P, dtype=ml_dtypes.bfloat16)
    in_maps = []
    for c in range(N_CORES):
        m = {"x": x[c * T_CORE_DEFAULT:(c + 1) * T_CORE_DEFAULT], "ident": ident}
        m.update(weights)
        in_maps.append(m)
    res = run_bass_kernel_spmd(nc, in_maps, core_ids=list(range(N_CORES)))
    y = np.concatenate([np.asarray(r["y"]) for r in res.results], axis=0)
    return y.reshape(1, T_FULL, D)


# version 10
# speedup vs baseline: 1.1445x; 1.1445x over previous
import numpy as np
import ml_dtypes
import concourse.bass as bass
import concourse.mybir as mybir
from concourse.bass_utils import run_bass_kernel_spmd
from contextlib import ExitStack

F32 = mybir.dt.float32
BF16 = mybir.dt.bfloat16
AF = mybir.ActivationFunctionType
ALU = mybir.AluOpType
AX = mybir.AxisListType

P = 128
D = 1024
E3 = 3072
HID = 4096
KD = D // P
KH = HID // P
NB = 512
NH = 16
DK = 64


class Res:
    __slots__ = ("name", "writers", "readers")

    def __init__(self, name):
        self.name = name
        self.writers = []
        self.readers = []


class Op:
    __slots__ = ("idx", "eng", "fn", "deps", "dma", "need_inc",
                 "sem_key", "sem_val", "waits", "tag", "eng_pos")

    def __init__(self, idx, eng, fn, dma, tag):
        self.idx = idx
        self.eng_pos = -1
        self.eng = eng
        self.fn = fn
        self.dma = dma
        self.deps = []
        self.need_inc = False
        self.sem_key = None
        self.sem_val = None
        self.waits = []
        self.tag = tag


class Plan:
    def __init__(self):
        self.ops = []
        self._eng_n = {}

    def add(self, eng, fn, reads=(), writes=(), dma=False, tag=""):
        op = Op(len(self.ops), eng, fn, dma, tag)
        op.eng_pos = self._eng_n.get(eng, 0)
        self._eng_n[eng] = op.eng_pos + 1
        deps = []
        for r in reads:
            deps.extend(r.writers)
        for w in writes:
            deps.extend(w.writers)
            deps.extend(w.readers)
        best = {}
        for d in deps:
            if d is op:
                continue
            key = ("Q" + d.eng) if d.dma else d.eng
            if key not in best or best[key].idx < d.idx:
                best[key] = d
        op.deps = list(best.values())
        for r in reads:
            r.readers.append(op)
        for w in writes:
            w.writers = [op]
            w.readers = []
        self.ops.append(op)
        return op

    def alias_barrier(self, dst_res, src_res):
        ops = []
        for r in src_res:
            ops.extend(r.writers)
            ops.extend(r.readers)
        for d in dst_res:
            d.readers.extend(ops)

    def finalize(self):
        for op in self.ops:
            resolved = []
            for d in op.deps:
                if d.dma:
                    resolved.append(d)
                elif d.eng != op.eng:
                    d.need_inc = True
                    resolved.append(d)
                elif (op.eng in ("V", "S") and not op.dma
                      and op.eng_pos - d.eng_pos == 1):
                    d.need_inc = True
                    resolved.append(d)
            op.deps = resolved
        ccnt, qcnt = {}, {}
        for op in self.ops:
            if op.dma:
                q = "Q" + op.eng
                qcnt[q] = qcnt.get(q, 0) + 16
                op.sem_key, op.sem_val = q, qcnt[q]
            elif op.need_inc:
                ccnt[op.eng] = ccnt.get(op.eng, 0) + 1
                op.sem_key, op.sem_val = op.eng, ccnt[op.eng]
        last = {}
        for op in self.ops:
            ws = {}
            for d in op.deps:
                k, v = d.sem_key, d.sem_val
                assert k is not None, (op.tag, d.tag)
                if last.get((op.eng, k), 0) < v and ws.get(k, 0) < v:
                    ws[k] = v
            for k, v in ws.items():
                last[(op.eng, k)] = v
            op.waits = sorted(ws.items())
        self.final_q = dict(qcnt)
        self.final_c = dict(ccnt)

    def emit(self, block, sems):
        per_eng = {}
        for op in self.ops:
            per_eng.setdefault(op.eng, []).append(op)

        def runner(ops):
            def go(eng):
                for op in ops:
                    for k, v in op.waits:
                        eng.wait_ge(sems[k], v)
                    inst = op.fn(eng)
                    if op.dma:
                        inst.then_inc(sems[op.sem_key], 16)
                    elif op.need_inc:
                        inst.then_inc(sems[op.sem_key], 1)
            return go

        for ename, attr in (("T", "tensor"), ("V", "vector"), ("S", "scalar"),
                            ("Y", "sync"), ("G", "gpsimd")):
            if ename in per_eng:
                getattr(block, attr)(runner(per_eng[ename]))


def build_core(T_CORE=2048, debug=False):
    C = T_CORE // NB
    TT = NB // P

    nc = bass.Bass("TRN2", target_bir_lowering=False)

    dx = nc.dram_tensor("x", [T_CORE, D], F32, kind="ExternalInput").ap()
    d_wqkv = nc.dram_tensor("W_qkv", [D, E3], F32, kind="ExternalInput").ap()
    d_bqkv = nc.dram_tensor("b_qkv", [E3], F32, kind="ExternalInput").ap()
    d_wout = nc.dram_tensor("W_out", [D, D], F32, kind="ExternalInput").ap()
    d_bout = nc.dram_tensor("b_out", [D], F32, kind="ExternalInput").ap()
    d_w1 = nc.dram_tensor("w1_w", [HID, D], F32, kind="ExternalInput").ap()
    d_w1s = nc.dram_tensor("w1_s", [HID], F32, kind="ExternalInput").ap()
    d_w2 = nc.dram_tensor("w2_w", [HID, D], F32, kind="ExternalInput").ap()
    d_w2s = nc.dram_tensor("w2_s", [HID], F32, kind="ExternalInput").ap()
    d_h1 = nc.dram_tensor("h1_w", [HID, HID], F32, kind="ExternalInput").ap()
    d_h1s = nc.dram_tensor("h1_s", [HID], F32, kind="ExternalInput").ap()
    d_h2 = nc.dram_tensor("h2_w", [HID, HID], F32, kind="ExternalInput").ap()
    d_h2s = nc.dram_tensor("h2_s", [HID], F32, kind="ExternalInput").ap()
    d_w3 = nc.dram_tensor("w3_w", [D, HID], F32, kind="ExternalInput").ap()
    d_w3s = nc.dram_tensor("w3_s", [D], F32, kind="ExternalInput").ap()
    d_ident = nc.dram_tensor("ident", [P, P], BF16, kind="ExternalInput").ap()
    dy = nc.dram_tensor("y", [T_CORE, D], F32, kind="ExternalOutput").ap()

    s_qkv_hi = nc.dram_tensor("s_qkv_hi", [D, E3], BF16, kind="Internal").ap()
    s_qkv_lo = nc.dram_tensor("s_qkv_lo", [D, E3], BF16, kind="Internal").ap()
    s_out_hi = nc.dram_tensor("s_out_hi", [D, D], BF16, kind="Internal").ap()
    s_out_lo = nc.dram_tensor("s_out_lo", [D, D], BF16, kind="Internal").ap()
    s_nat = {
        "w1": nc.dram_tensor("s_nat_w1", [HID, D], BF16, kind="Internal").ap(),
        "w2": nc.dram_tensor("s_nat_w2", [HID, D], BF16, kind="Internal").ap(),
        "h1": nc.dram_tensor("s_nat_h1", [HID, HID], BF16, kind="Internal").ap(),
        "h2": nc.dram_tensor("s_nat_h2", [HID, HID], BF16, kind="Internal").ap(),
        "w3": nc.dram_tensor("s_nat_w3", [D, HID], BF16, kind="Internal").ap(),
    }
    s_t = {
        "w1": nc.dram_tensor("s_t_w1", [D, HID], BF16, kind="Internal").ap(),
        "w2": nc.dram_tensor("s_t_w2", [D, HID], BF16, kind="Internal").ap(),
        "h1": nc.dram_tensor("s_t_h1", [HID, HID], BF16, kind="Internal").ap(),
        "h2": nc.dram_tensor("s_t_h2", [HID, HID], BF16, kind="Internal").ap(),
        "w3": nc.dram_tensor("s_t_w3", [HID, D], BF16, kind="Internal").ap(),
    }
    NROW = E3 + 2 * D
    s_bias = nc.dram_tensor("s_bias", [2, NROW], BF16, kind="Internal").ap()
    s_x2 = nc.dram_tensor("s_x2", [T_CORE, D], F32, kind="Internal").ap()

    dbg = {}
    if debug:
        for nm, shp, dt_ in (("dbg_qkv", [NB, E3], F32),
                             ("dbg_probs", [NB, NH * NH], F32),
                             ("dbg_ctx", [NB, D], F32),
                             ("dbg_x2", [NB, D], F32),
                             ("dbg_h0", [HID, NB], BF16),
                             ("dbg_h1p", [HID, NB], BF16),
                             ("dbg_xb", [P, 4 * D], F32),
                             ("dbg_stats", [P, 32], F32),
                             ("dbg_xnT", [P, KD * NB], BF16)):
            dbg[nm] = nc.dram_tensor(nm, shp, dt_, kind="ExternalOutput").ap()

    ctx_mgr = ExitStack()
    E = ctx_mgr.enter_context

    def addr_of(handle):
        name = handle.name
        for a in nc.m.functions[0].allocations:
            if isinstance(a, mybir.MemoryLocationSet):
                for ml in a.memorylocations:
                    if ml.name == name:
                        return ml.addr
        raise KeyError(name)

    ident = E(nc.sbuf_tensor("t_ident", [P, P], BF16))
    ones2 = E(nc.sbuf_tensor("t_ones2", [2, P], BF16))
    b2_qkv = E(nc.sbuf_tensor("t_b2qkv", [2, E3], BF16))
    b2_out = E(nc.sbuf_tensor("t_b2out", [2, D], BF16))
    s3pair = E(nc.sbuf_tensor("t_s3pair", [2, D], BF16))
    s3_rep = E(nc.sbuf_tensor("t_s3rep", [P, D], F32))
    s1c = E(nc.sbuf_tensor("t_s1c", [P, KH], F32))
    s2c = E(nc.sbuf_tensor("t_s2c", [P, KH], F32))
    h1sc = E(nc.sbuf_tensor("t_h1sc", [P, KH], F32))
    h2sc = E(nc.sbuf_tensor("t_h2sc", [P, KH], F32))
    stq = [E(nc.sbuf_tensor(f"t_st{i}", [P, TT], F32)) for i in range(8)]
    nm16 = E(nc.sbuf_tensor("t_nm16", [P, NH], F32))
    den16 = E(nc.sbuf_tensor("t_den16", [P, NH], F32))
    rec16 = E(nc.sbuf_tensor("t_rec16", [P, NH], F32))
    sqt = E(nc.sbuf_tensor("t_sq", [P, D], F32))
    xnf = E(nc.sbuf_tensor("t_xnf", [P, D], F32))
    tf512 = E(nc.sbuf_tensor("t_tf512", [P, NB], F32))
    tf512b = E(nc.sbuf_tensor("t_tf512b", [P, NB], F32))
    h0f = E(nc.sbuf_tensor("t_h0f", [P, NB], F32))

    xb = E(nc.sbuf_tensor("t_xb", [P, TT * D], F32))
    x2b = E(nc.sbuf_tensor("t_x2b", [P, TT * D], F32))
    tokh = E(nc.sbuf_tensor("t_tokh", [P, TT * D], BF16))
    tokl = E(nc.sbuf_tensor("t_tokl", [P, TT * D], BF16))
    xnT_hi = E(nc.sbuf_tensor("t_xnT_hi", [P, KD * NB], BF16))
    xnT_lo = E(nc.sbuf_tensor("t_xnT_lo", [P, KD * NB], BF16))
    wslab = E(nc.sbuf_tensor("t_wslab", [P, 2 * 4096], BF16))
    arena = E(nc.sbuf_tensor("t_arena", [P, 40960], BF16))
    abase = addr_of(arena)

    def at(name, shape, dtype, off):
        return nc.alloc_sbuf_tensor_at(name, shape, dtype, offset=abase + off)

    qkvb = at("v_qkv", [P, TT * E3], F32, 0)
    sc_tmp = at("v_sctmp", [P, NH * DK], F32, 49152)
    scoreb = at("v_score", [P, NH * NH], F32, 53248)
    probb = at("v_prob", [P, NH * NH], F32, 54272)
    acc_a = at("v_acca", [P, D], F32, 55296)
    acc_b = at("v_accb", [P, D], F32, 59392)
    ctxT_hi = at("v_ctxThi", [P, KD * NB], BF16, 63488)
    ctxT_lo = at("v_ctxTlo", [P, KD * NB], BF16, 71680)
    h0_hi = at("v_h0hi", [P, KH * NB], BF16, 0)
    h0_lo = at("v_h0lo", [P, KH * NB], BF16, 32768)
    w3slab = at("v_w3slab", [P, 8 * D], BF16, 65536)
    prepst_t = E(nc.sbuf_tensor("t_prepst", [P, 2 * HID], BF16))
    prepst = [prepst_t.ap()[:, 0:HID], prepst_t.ap()[:, HID:2 * HID]]

    def wslab_slot(slot):
        if slot < 2:
            return wslab.ap()[:, slot * 4096:(slot + 1) * 4096]
        return prepst_t.ap()[:, (slot - 2) * 4096:(slot - 1) * 4096]
    prepw = at("v_prepw", [P, E3], F32, 0)
    prep_hi = at("v_prephi", [P, E3], BF16, 12288)
    prep_lo = at("v_preplo", [P, E3], BF16, 18432)
    prow = at("v_prow", [1, NROW], F32, 24576)
    prow_b = at("v_prowb", [1, 2 * NROW], BF16, 45056)

    h1p_hi_parts = [
        nc.alloc_sbuf_tensor_at("v_h1phiA", [P, 8192], BF16, offset=addr_of(xb)),
        nc.alloc_sbuf_tensor_at("v_h1phiB", [P, 8192], BF16, offset=addr_of(x2b)),
    ]
    h1p_lo_parts = [
        nc.alloc_sbuf_tensor_at("v_h1ploA", [P, 4096], BF16, offset=addr_of(tokh)),
        nc.alloc_sbuf_tensor_at("v_h1ploB", [P, 4096], BF16, offset=addr_of(tokl)),
        nc.alloc_sbuf_tensor_at("v_h1ploC", [P, 4096], BF16, offset=addr_of(xnT_hi)),
        nc.alloc_sbuf_tensor_at("v_h1ploD", [P, 4096], BF16, offset=addr_of(xnT_lo)),
    ]

    def h1p_slice(hi, k):
        if hi:
            t = h1p_hi_parts[k // 16]
            k = k % 16
        else:
            t = h1p_lo_parts[k // 8]
            k = k % 8
        return t.ap()[:, k * NB:(k + 1) * NB]

    pb = [E(nc.psum_tensor(f"t_pb{i}", [P, NB], F32)) for i in range(8)]

    sems = {}
    for k in ("T", "V", "S", "G", "QY", "QG", "QS"):
        sems[k] = E(nc.semaphore("sem_" + k))
    done_sem = E(nc.semaphore("sem_done"))

    pl = Plan()
    A = pl.add
    r_pb = [Res(f"pb{i}") for i in range(8)]
    r_scr = {k: Res("scr_" + k) for k in
             ("qkv_hi", "qkv_lo", "out_hi", "out_lo",
              "nat_w1", "nat_w2", "nat_h1", "nat_h2", "nat_w3",
              "t_w1", "t_w2", "t_h1", "t_h2", "t_w3", "bias", "x2sp")}
    r_ident = Res("ident")
    r_ones2 = Res("ones2")
    r_b2q = Res("b2q")
    r_b2o = Res("b2o")
    r_s3p = Res("s3p")
    r_s3rep = Res("s3rep")
    r_scl = {k: Res("scl_" + k) for k in ("s1", "s2", "h1", "h2")}
    r_prepst = [Res("prepstA"), Res("prepstB")]
    r_prepw = Res("prepw")
    r_prow = Res("prow")
    r_xb = [Res(f"xb{t}") for t in range(4)]
    r_x2 = [Res(f"x2_{t}") for t in range(4)]
    r_tok = [Res(f"tok{t}") for t in range(4)]
    r_xnT = Res("xnT")
    r_qkv = [Res(f"qkv{t}") for t in range(4)]
    r_att = Res("att")
    r_scores = Res("scores")
    r_probs = Res("probs")
    r_smx = Res("smx")
    r_acc = Res("acc")
    r_ctxT = [Res(f"ctxT{t}") for t in range(4)]
    r_wslab = [Res("wslabA"), Res("wslabB"), Res("wslabC"), Res("wslabD")]
    r_w3slab = [Res("w3slabA"), Res("w3slabB")]
    r_h0 = Res("h0")
    r_h1p = Res("h1p")
    r_stats = Res("stats")
    r_tmpA = Res("tmpA")
    r_tmpB = Res("tmpB")
    r_tf = Res("tf")
    r_tfb = Res("tfb")
    r_h0f = Res("h0f")
    r_ydram = Res("ydram")
    r_dbg = Res("dbg")

    ATTN_ARENA = [r_qkv[0], r_qkv[1], r_qkv[2], r_qkv[3], r_att,
                  r_scores, r_probs, r_smx, r_acc,
                  r_ctxT[0], r_ctxT[1], r_ctxT[2], r_ctxT[3]]
    FFN_ARENA = [r_h0, r_w3slab[0], r_w3slab[1]]
    TOKSIDE = [r_xb[0], r_xb[1], r_xb[2], r_xb[3],
               r_x2[0], r_x2[1], r_x2[2], r_x2[3],
               r_tok[0], r_tok[1], r_tok[2], r_tok[3], r_xnT]

    A("Y", lambda e: e.dma_start(out=ident.ap()[:], in_=d_ident[:, :]),
      writes=[r_ident], dma=True, tag="ld_ident")
    A("V", lambda e: e.memset(ones2.ap()[:], 1.0), writes=[r_ones2], tag="ones2")
    for dst, src, rr in ((s1c, d_w1s, r_scl["s1"]), (s2c, d_w2s, r_scl["s2"]),
                         (h1sc, d_h1s, r_scl["h1"]), (h2sc, d_h2s, r_scl["h2"])):
        A("Y", (lambda dst, src: lambda e: e.dma_start(
            out=dst.ap()[:], in_=src.rearrange("(o p) -> p o", p=P),
            allow_slow_non_contiguous=True))(dst, src),
          writes=[rr], dma=True, tag="ld_scale")

    A("Y", lambda e: e.dma_start(out=prow.ap()[0:1, 0:E3], in_=d_bqkv.unsqueeze(0)),
      writes=[r_prow], dma=True, tag="ld_bqkv")
    A("Y", lambda e: e.dma_start(out=prow.ap()[0:1, E3:E3 + D], in_=d_bout.unsqueeze(0)),
      writes=[r_prow], dma=True, tag="ld_bout")
    A("Y", lambda e: e.dma_start(out=prow.ap()[0:1, E3 + D:NROW], in_=d_w3s.unsqueeze(0)),
      writes=[r_prow], dma=True, tag="ld_w3s")
    A("V", lambda e: e.tensor_copy(prow_b.ap()[0:1, 0:NROW], prow.ap()[0:1, 0:NROW]),
      reads=[r_prow], writes=[r_prow], tag="bias_hi")
    A("V", lambda e: e.tensor_sub(prow_b.ap()[0:1, NROW:2 * NROW],
                                  prow.ap()[0:1, 0:NROW], prow_b.ap()[0:1, 0:NROW]),
      reads=[r_prow], writes=[r_prow], tag="bias_lo")
    A("S", lambda e: e.dma_start(out=s_bias.rearrange("a b -> (a b)").unsqueeze(0),
                                 in_=prow_b.ap()[0:1, 0:2 * NROW]),
      reads=[r_prow], writes=[r_scr["bias"]], dma=True, tag="st_bias")
    A("Y", lambda e: e.dma_start(out=b2_qkv.ap()[:], in_=s_bias[:, 0:E3]),
      reads=[r_scr["bias"]], writes=[r_b2q], dma=True, tag="ld_b2qkv")
    A("Y", lambda e: e.dma_start(out=b2_out.ap()[:], in_=s_bias[:, E3:E3 + D]),
      reads=[r_scr["bias"]], writes=[r_b2o], dma=True, tag="ld_b2out")
    A("Y", lambda e: e.dma_start(out=s3pair.ap()[:], in_=s_bias[:, E3 + D:NROW]),
      reads=[r_scr["bias"]], writes=[r_s3p], dma=True, tag="ld_s3pair")
    for db in range(2):
        A("T", (lambda db: lambda e: e.matmul(
            pb[6 + db].ap()[:], ones2.ap()[:],
            s3pair.ap()[:, db * NB:(db + 1) * NB], start=True, stop=True))(db),
          reads=[r_ones2, r_s3p], writes=[r_pb[6 + db]], tag="s3rep_mm")
        A("S", (lambda db: lambda e: e.copy(
            s3_rep.ap()[:, db * NB:(db + 1) * NB], pb[6 + db].ap()[:]))(db),
          reads=[r_pb[6 + db]], writes=[r_s3rep], tag="s3rep_ev")

    for dsrc, dhi, dlo, width, rhi, rlo, tagn in (
            (d_wqkv, s_qkv_hi, s_qkv_lo, E3, r_scr["qkv_hi"], r_scr["qkv_lo"], "qkv"),
            (d_wout, s_out_hi, s_out_lo, D, r_scr["out_hi"], r_scr["out_lo"], "out")):
        for kt in range(KD):
            sl = slice(kt * P, (kt + 1) * P)
            A("G", (lambda dsrc, sl, width: lambda e: e.dma_start(
                out=prepw.ap()[:, 0:width], in_=dsrc[sl, :]))(dsrc, sl, width),
              writes=[r_prepw], dma=True, tag=f"p_{tagn}ld{kt}")
            A("V", (lambda width: lambda e: e.tensor_copy(
                prep_hi.ap()[:, 0:width], prepw.ap()[:, 0:width]))(width),
              reads=[r_prepw], writes=[r_prepw], tag=f"p_{tagn}hi{kt}")
            A("V", (lambda width: lambda e: e.tensor_sub(
                prep_lo.ap()[:, 0:width], prepw.ap()[:, 0:width],
                prep_hi.ap()[:, 0:width]))(width),
              reads=[r_prepw], writes=[r_prepw], tag=f"p_{tagn}lo{kt}")
            A("G", (lambda dhi, sl, width: lambda e: e.dma_start(
                out=dhi[sl, :], in_=prep_hi.ap()[:, 0:width]))(dhi, sl, width),
              reads=[r_prepw], writes=[rhi], dma=True, tag=f"p_{tagn}sh{kt}")
            A("G", (lambda dlo, sl, width: lambda e: e.dma_start(
                out=dlo[sl, :], in_=prep_lo.ap()[:, 0:width]))(dlo, sl, width),
              reads=[r_prepw], writes=[rlo], dma=True, tag=f"p_{tagn}sl{kt}")

    PREP_DIMS = {"w1": (HID, D), "w2": (HID, D), "h1": (HID, HID),
                 "h2": (HID, HID), "w3": (D, HID)}
    PREP_SRC = {"w1": d_w1, "w2": d_w2, "h1": d_h1, "h2": d_h2, "w3": d_w3}
    for wname in ("w1", "w2", "h1", "h2", "w3"):
        A("G", (lambda wname, dsrc: lambda e: e.dma_start(
            out=s_nat[wname][:, :], in_=dsrc[:, :]))(wname, PREP_SRC[wname]),
          writes=[r_scr["nat_" + wname]], dma=True, tag=f"p_cast_{wname}")

    def plan_prep_tr(wname):
        O, I = PREP_DIMS[wname]
        for it in range(I // P):
            sl = it % 2
            A("Y", (lambda wname, it, O, sl: lambda e: e.dma_start(
                out=prepst[sl][:, 0:O],
                in_=s_nat[wname][0:O, it * P:(it + 1) * P],
                transpose=True))(wname, it, O, sl),
              reads=[r_scr["nat_" + wname]], writes=[r_prepst[sl]], dma=True,
              tag=f"p_tld_{wname}_{it}")
            A("Y", (lambda wname, it, O, sl: lambda e: e.dma_start(
                out=s_t[wname][it * P:(it + 1) * P, 0:O],
                in_=prepst[sl][:, 0:O]))(wname, it, O, sl),
              reads=[r_prepst[sl]], writes=[r_scr["t_" + wname]], dma=True,
              tag=f"p_tst_{wname}_{it}")

    for _w in ("w1", "w2", "h1", "h2", "w3"):
        plan_prep_tr(_w)

    def mm_group(bank, pairs, extra_reads, tag):
        N = len(pairs)
        for i, (st, mv) in enumerate(pairs):
            A("T", (lambda st, mv, s0, s1: lambda e: e.matmul(
                pb[bank].ap()[:], st, mv, start=s0, stop=s1))(
                    st, mv, i == 0, i == N - 1),
              reads=list(extra_reads), writes=[r_pb[bank]], tag=f"{tag}_mm{i}")

    def mm_group_n(bank, width, pairs, extra_reads, tag):
        N = len(pairs)
        for i, (st, mv) in enumerate(pairs):
            A("T", (lambda st, mv, s0, s1: lambda e: e.matmul(
                pb[bank].ap()[:, 0:width], st, mv, start=s0, stop=s1))(
                    st, mv, i == 0, i == N - 1),
              reads=list(extra_reads), writes=[r_pb[bank]], tag=f"{tag}_mm{i}")

    def ln_stage(src_tile, src_res, tag):
        s_sx, s_sq, s_m, s_r = stq[0], stq[1], stq[2], stq[3]
        s_t1, s_t2, s_t3, s_t4 = stq[4], stq[5], stq[6], stq[7]
        for i in range(TT):
            A("S", (lambda i: lambda e: e.activation(
                sqt.ap()[:], src_tile(i), AF.Square))(i),
              reads=[src_res[i]], writes=[r_tmpA], tag=f"{tag}_sq{i}")
            A("V", (lambda i: lambda e: e.tensor_reduce(
                s_sx.ap()[:, i:i + 1], src_tile(i), AX.X, ALU.add))(i),
              reads=[src_res[i]], writes=[r_stats], tag=f"{tag}_sx{i}")
            A("V", (lambda i: lambda e: e.tensor_reduce(
                s_sq.ap()[:, i:i + 1], sqt.ap()[:], AX.X, ALU.add))(i),
              reads=[r_tmpA], writes=[r_stats], tag=f"{tag}_sqr{i}")
        A("V", lambda e: e.tensor_scalar_mul(s_m.ap()[:], s_sx.ap()[:], 1.0 / D),
          reads=[r_stats], writes=[r_stats], tag=f"{tag}_m")
        A("V", lambda e: e.tensor_scalar_mul(s_t1.ap()[:], s_sq.ap()[:], 1.0 / D),
          reads=[r_stats], writes=[r_stats], tag=f"{tag}_q")
        A("V", lambda e: e.tensor_mul(s_t2.ap()[:], s_m.ap()[:], s_m.ap()[:]),
          reads=[r_stats], writes=[r_stats], tag=f"{tag}_m2")
        A("V", lambda e: e.tensor_sub(s_t3.ap()[:], s_t1.ap()[:], s_t2.ap()[:]),
          reads=[r_stats], writes=[r_stats], tag=f"{tag}_v")
        A("V", lambda e: e.tensor_scalar_add(s_t4.ap()[:], s_t3.ap()[:], 1e-5),
          reads=[r_stats], writes=[r_stats], tag=f"{tag}_a")
        A("S", lambda e: e.activation(s_t1.ap()[:], s_t4.ap()[:], AF.Ln),
          reads=[r_stats], writes=[r_stats], tag=f"{tag}_ln")
        A("S", lambda e: e.activation(s_t2.ap()[:], s_t1.ap()[:], AF.Exp,
                                      scale=-0.5),
          reads=[r_stats], writes=[r_stats], tag=f"{tag}_y0")
        A("V", lambda e: e.tensor_mul(s_t1.ap()[:], s_t2.ap()[:], s_t2.ap()[:]),
          reads=[r_stats], writes=[r_stats], tag=f"{tag}_n1")
        A("V", lambda e: e.tensor_mul(s_t3.ap()[:], s_t1.ap()[:], s_t4.ap()[:]),
          reads=[r_stats], writes=[r_stats], tag=f"{tag}_n2")
        A("V", lambda e: e.tensor_scalar(s_t1.ap()[:], s_t3.ap()[:], -0.5, 1.5,
                                         ALU.mult, ALU.add),
          reads=[r_stats], writes=[r_stats], tag=f"{tag}_n3")
        A("V", lambda e: e.tensor_mul(s_r.ap()[:], s_t2.ap()[:], s_t1.ap()[:]),
          reads=[r_stats], writes=[r_stats], tag=f"{tag}_r")
        for i in range(TT):
            A("V", (lambda i: lambda e: e.tensor_scalar(
                xnf.ap()[:], src_tile(i), s_m.ap()[:, i:i + 1],
                s_r.ap()[:, i:i + 1], ALU.subtract, ALU.mult))(i),
              reads=[src_res[i], r_stats], writes=[r_tmpB], tag=f"{tag}_nm{i}")
            A("V", (lambda i: lambda e: e.tensor_copy(
                tokh.ap()[:, i * D:(i + 1) * D], xnf.ap()[:]))(i),
              reads=[r_tmpB], writes=[r_tok[i]], tag=f"{tag}_hi{i}")
            A("V", (lambda i: lambda e: e.tensor_sub(
                tokl.ap()[:, i * D:(i + 1) * D], xnf.ap()[:],
                tokh.ap()[:, i * D:(i + 1) * D]))(i),
              reads=[r_tmpB], writes=[r_tok[i]], tag=f"{tag}_lo{i}")
        for hi in (True, False):
            tsrc = tokh if hi else tokl
            tdst = xnT_hi if hi else xnT_lo
            for j in range(KD):
                bank = 2 + (j % 2)
                pbf = pb[bank].ap().bitcast(BF16)
                for i in range(TT):
                    A("T", (lambda i, j, pbf, tsrc: lambda e: e.transpose(
                        pbf[:, i * P:(i + 1) * P],
                        tsrc.ap()[:, i * D + j * P: i * D + (j + 1) * P],
                        ident.ap()[:]))(i, j, pbf, tsrc),
                      reads=[r_tok[i], r_ident], writes=[r_pb[bank]],
                      tag=f"{tag}_tr{j}_{i}")
                A("S", (lambda j, pbf, tdst: lambda e: e.copy(
                    tdst.ap()[:, j * NB:(j + 1) * NB], pbf[:, 0:NB]))(j, pbf, tdst),
                  reads=[r_pb[bank]], writes=[r_xnT], tag=f"{tag}_te{j}")

    xb_tile = lambda i: xb.ap()[:, i * D:(i + 1) * D]
    x2_tile = lambda i: x2b.ap()[:, i * D:(i + 1) * D]

    for c in range(C):
        tok0 = c * NB
        NSLOT = 2 if c == 0 else 4
        pl.alias_barrier(TOKSIDE, [r_h1p])
        pl.alias_barrier(ATTN_ARENA, FFN_ARENA + [r_prepw, r_prow])
        if c == 1:
            pl.alias_barrier([r_wslab[2], r_wslab[3]],
                             [r_prepst[0], r_prepst[1]])

        A("S" if c == 0 else "Y", (lambda tok0: lambda e: e.dma_start(
            out=xb.ap().rearrange("p (i d) -> p i d", d=D),
            in_=dx[tok0:tok0 + NB, :].rearrange("(i p) d -> p i d", p=P)))(tok0),
          writes=[r_xb[0], r_xb[1], r_xb[2], r_xb[3]], dma=True, tag=f"c{c}_x")

        ln_stage(xb_tile, r_xb, f"c{c}L1")

        if debug and c == 0:
            A("S", lambda e: e.dma_start(out=dbg["dbg_xb"][:, :], in_=xb.ap()[:]),
              reads=[r_xb[0], r_xb[1], r_xb[2], r_xb[3]], writes=[r_dbg], dma=True, tag="dxb")
            A("S", (lambda i: lambda e: e.dma_start(out=dbg["dbg_stats"][:, 0:4], in_=stq[0].ap()[:]))(0),
              reads=[r_stats], writes=[r_dbg], dma=True, tag="dst0")
            A("S", (lambda i: lambda e: e.dma_start(out=dbg["dbg_stats"][:, 4:8], in_=stq[1].ap()[:]))(1),
              reads=[r_stats], writes=[r_dbg], dma=True, tag="dst1")
            A("S", (lambda i: lambda e: e.dma_start(out=dbg["dbg_stats"][:, 8:12], in_=stq[2].ap()[:]))(2),
              reads=[r_stats], writes=[r_dbg], dma=True, tag="dst2")
            A("S", (lambda i: lambda e: e.dma_start(out=dbg["dbg_stats"][:, 12:16], in_=stq[3].ap()[:]))(3),
              reads=[r_stats], writes=[r_dbg], dma=True, tag="dst3")
            A("S", (lambda i: lambda e: e.dma_start(out=dbg["dbg_stats"][:, 16:20], in_=stq[4].ap()[:]))(4),
              reads=[r_stats], writes=[r_dbg], dma=True, tag="dst4")
            A("S", (lambda i: lambda e: e.dma_start(out=dbg["dbg_stats"][:, 20:24], in_=stq[5].ap()[:]))(5),
              reads=[r_stats], writes=[r_dbg], dma=True, tag="dst5")
            A("S", (lambda i: lambda e: e.dma_start(out=dbg["dbg_stats"][:, 24:28], in_=stq[6].ap()[:]))(6),
              reads=[r_stats], writes=[r_dbg], dma=True, tag="dst6")
            A("S", (lambda i: lambda e: e.dma_start(out=dbg["dbg_stats"][:, 28:32], in_=stq[7].ap()[:]))(7),
              reads=[r_stats], writes=[r_dbg], dma=True, tag="dst7")
            A("S", lambda e: e.dma_start(out=dbg["dbg_xnT"][:, :], in_=xnT_hi.ap()[:]),
              reads=[r_xnT], writes=[r_dbg], dma=True, tag="dxnT")

        EBW = 256
        ldq = "S" if c == 0 else "Y"
        gi = 0
        for eb in range(E3 // EBW):
            slot = gi % NSLOT
            wv = wslab_slot(slot)
            A(ldq, (lambda eb, wv: lambda e: e.dma_start(
                out=wv[:, 0:KD * EBW].rearrange("p (k n) -> p k n", n=EBW),
                in_=s_qkv_hi[:, eb * EBW:(eb + 1) * EBW]
                .rearrange("(k p) n -> p k n", p=P)))(eb, wv),
              reads=[r_scr["qkv_hi"]], writes=[r_wslab[slot]], dma=True,
              tag=f"c{c}_qwh{eb}")
            A(ldq, (lambda eb, wv: lambda e: e.dma_start(
                out=wv[:, KD * EBW:2 * KD * EBW].rearrange("p (k n) -> p k n", n=EBW),
                in_=s_qkv_lo[:, eb * EBW:(eb + 1) * EBW]
                .rearrange("(k p) n -> p k n", p=P)))(eb, wv),
              reads=[r_scr["qkv_lo"]], writes=[r_wslab[slot]], dma=True,
              tag=f"c{c}_qwl{eb}")
            for t in range(TT):
                bank = (eb * TT + t) % 2
                pairs = [(ones2.ap()[:], b2_qkv.ap()[:, eb * EBW:(eb + 1) * EBW])]
                for k in range(KD):
                    xh_sl = xnT_hi.ap()[:, k * NB + t * P: k * NB + (t + 1) * P]
                    xl_sl = xnT_lo.ap()[:, k * NB + t * P: k * NB + (t + 1) * P]
                    whi = wv[:, k * EBW:(k + 1) * EBW]
                    wlo = wv[:, KD * EBW + k * EBW: KD * EBW + (k + 1) * EBW]
                    pairs += [(xh_sl, whi), (xl_sl, whi), (xh_sl, wlo)]
                mm_group_n(bank, EBW, pairs,
                           [r_xnT, r_wslab[slot], r_ones2, r_b2q], f"c{c}_qk{t}_{eb}")
                A("S", (lambda t, eb, bank: lambda e: e.copy(
                    qkvb.ap()[:, t * E3 + eb * EBW: t * E3 + (eb + 1) * EBW],
                    pb[bank].ap()[:, 0:EBW]))(t, eb, bank),
                  reads=[r_pb[bank]], writes=[r_qkv[t]], tag=f"c{c}_qe{t}_{eb}")
            gi += 1

        if debug and c == 0:
            for t in range(TT):
                A("S", (lambda t: lambda e: e.dma_start(
                    out=dbg["dbg_qkv"][t * P:(t + 1) * P, :],
                    in_=qkvb.ap()[:, t * E3:(t + 1) * E3]))(t),
                  reads=[r_qkv[t]], writes=[r_dbg], dma=True, tag=f"dq{t}")

        sc3 = sc_tmp.ap().rearrange("p (h c) -> p h c", c=DK)
        sco3 = scoreb.ap().rearrange("p (h g) -> p h g", g=NH)
        pr3 = probb.ap().rearrange("p (h g) -> p h g", g=NH)
        acc3 = [acc_a.ap().rearrange("p (h c) -> p h c", c=DK),
                acc_b.ap().rearrange("p (h c) -> p h c", c=DK)]
        for t in range(TT):
            q3 = qkvb.ap()[:, t * E3:(t + 1) * E3].rearrange(
                "p (h c) -> p h c", c=3 * DK)
            xn3 = xnf.ap().rearrange("p (h c) -> p h c", c=DK)

            def sc_mult(g):
                buf = sc3 if g % 2 == 0 else xn3
                eng = "V" if g % 2 == 0 else "G"
                wres = r_att if g % 2 == 0 else r_tmpB
                A(eng, (lambda q3, g, buf: lambda e: e.tensor_mul(
                    buf, q3[:, :, 0:DK],
                    q3[:, g:g + 1, DK:2 * DK].broadcast_to([P, NH, DK])))(q3, g, buf),
                  reads=[r_qkv[t]], writes=[wres], tag=f"c{c}_scm{t}_{g}")

            def sc_red(g):
                buf = sc3 if g % 2 == 0 else xn3
                rres = r_att if g % 2 == 0 else r_tmpB
                A("V", (lambda g, buf: lambda e: e.tensor_reduce(
                    sco3[:, :, g:g + 1], buf, AX.X, ALU.add))(g, buf),
                  reads=[rres], writes=[r_scores], tag=f"c{c}_scr{t}_{g}")

            sc_mult(0)
            for g in range(1, NH):
                sc_mult(g)
                sc_red(g - 1)
            sc_red(NH - 1)
            A("V", lambda e: e.tensor_reduce(nm16.ap()[:].unsqueeze(2), sco3,
                                             AX.X, ALU.max, negate=True),
              reads=[r_scores], writes=[r_smx], tag=f"c{c}_nm{t}")
            A("V", lambda e: e.tensor_tensor(
                pr3, sco3, nm16.ap()[:].unsqueeze(2).broadcast_to([P, NH, NH]),
                ALU.add),
              reads=[r_scores, r_smx], writes=[r_probs], tag=f"c{c}_sub{t}")
            A("S", lambda e: e.activation(scoreb.ap()[:], probb.ap()[:], AF.Exp,
                                          scale=0.125),
              reads=[r_probs], writes=[r_scores], tag=f"c{c}_exp{t}")
            A("V", lambda e: e.tensor_reduce(den16.ap()[:].unsqueeze(2), sco3,
                                             AX.X, ALU.add),
              reads=[r_scores], writes=[r_smx], tag=f"c{c}_den{t}")
            A("S", lambda e: e.activation(nm16.ap()[:], den16.ap()[:], AF.Ln),
              reads=[r_smx], writes=[r_smx], tag=f"c{c}_ld{t}")
            A("S", lambda e: e.activation(rec16.ap()[:], nm16.ap()[:], AF.Exp,
                                          scale=-1.0),
              reads=[r_smx], writes=[r_smx], tag=f"c{c}_rc{t}")
            A("V", lambda e: e.tensor_tensor(
                pr3, sco3, rec16.ap()[:].unsqueeze(2).broadcast_to([P, NH, NH]),
                ALU.mult),
              reads=[r_scores, r_smx], writes=[r_probs], tag=f"c{c}_dv{t}")
            if debug and c == 0:
                A("S", (lambda t: lambda e: e.dma_start(
                    out=dbg["dbg_probs"][t * P:(t + 1) * P, :], in_=probb.ap()[:]))(t),
                  reads=[r_probs], writes=[r_dbg], dma=True, tag=f"dp{t}")
            A("V", (lambda q3: lambda e: e.tensor_mul(
                acc3[0], pr3[:, :, 0:1].broadcast_to([P, NH, DK]),
                q3[:, 0:1, 2 * DK:3 * DK].broadcast_to([P, NH, DK])))(q3),
              reads=[r_probs, r_qkv[t]], writes=[r_acc], tag=f"c{c}_cx0{t}")
            cur = 0
            for g in range(1, NH):
                buf = sc3 if g % 2 == 0 else xn3
                bres = r_att if g % 2 == 0 else r_tmpB
                eng = "V" if g % 3 == 0 else "G"
                A(eng, (lambda q3, g, buf: lambda e: e.tensor_mul(
                    buf, pr3[:, :, g:g + 1].broadcast_to([P, NH, DK]),
                    q3[:, g:g + 1, 2 * DK:3 * DK].broadcast_to([P, NH, DK])))(q3, g, buf),
                  reads=[r_probs, r_qkv[t]], writes=[bres], tag=f"c{c}_cm{t}_{g}")
                A("V", (lambda cur, buf: lambda e: e.tensor_tensor(
                    acc3[1 - cur], acc3[cur], buf, ALU.add))(cur, buf),
                  reads=[bres, r_acc], writes=[r_acc], tag=f"c{c}_ca{t}_{g}")
                cur = 1 - cur
            acc_fin = acc_a if cur == 0 else acc_b
            if debug and c == 0:
                A("S", (lambda t, acc_fin: lambda e: e.dma_start(
                    out=dbg["dbg_ctx"][t * P:(t + 1) * P, :],
                    in_=acc_fin.ap()[:]))(t, acc_fin),
                  reads=[r_acc], writes=[r_dbg], dma=True, tag=f"dc{t}")
            A("V", (lambda t, acc_fin: lambda e: e.tensor_copy(
                tokh.ap()[:, t * D:(t + 1) * D], acc_fin.ap()[:]))(t, acc_fin),
              reads=[r_acc], writes=[r_tok[t]], tag=f"c{c}_chi{t}")
            A("V", (lambda t, acc_fin: lambda e: e.tensor_sub(
                tokl.ap()[:, t * D:(t + 1) * D], acc_fin.ap()[:],
                tokh.ap()[:, t * D:(t + 1) * D]))(t, acc_fin),
              reads=[r_acc], writes=[r_tok[t]], tag=f"c{c}_clo{t}")
            for hi in (True, False):
                tsrc = tokh if hi else tokl
                tdst = ctxT_hi if hi else ctxT_lo
                for jj in range(2):
                    bank = 2 + jj
                    pbf = pb[bank].ap().bitcast(BF16)
                    for j4 in range(4):
                        j = jj * 4 + j4
                        A("T", (lambda t, j, j4, pbf, tsrc: lambda e: e.transpose(
                            pbf[:, j4 * P:(j4 + 1) * P],
                            tsrc.ap()[:, t * D + j * P: t * D + (j + 1) * P],
                            ident.ap()[:]))(t, j, j4, pbf, tsrc),
                          reads=[r_tok[t], r_ident], writes=[r_pb[bank]],
                          tag=f"c{c}_ct{t}_{j}")
                    dst = tdst.ap().rearrange("p (j n) -> p j n", n=NB)[
                        :, jj * 4:(jj + 1) * 4, t * P:(t + 1) * P]
                    src = pbf[:, 0:4 * P].rearrange("p (j n) -> p j n", n=P)
                    A("S", (lambda dst, src: lambda e: e.copy(dst, src))(dst, src),
                      reads=[r_pb[bank]], writes=[r_ctxT[t]], tag=f"c{c}_ce{t}_{jj}")

        gi = 0
        for eb in range(D // EBW):
            slot = gi % NSLOT
            wv = wslab_slot(slot)
            A(ldq, (lambda eb, wv: lambda e: e.dma_start(
                out=wv[:, 0:KD * EBW].rearrange("p (k n) -> p k n", n=EBW),
                in_=s_out_hi[:, eb * EBW:(eb + 1) * EBW]
                .rearrange("(k p) n -> p k n", p=P)))(eb, wv),
              reads=[r_scr["out_hi"]], writes=[r_wslab[slot]], dma=True,
              tag=f"c{c}_owh{eb}")
            A(ldq, (lambda eb, wv: lambda e: e.dma_start(
                out=wv[:, KD * EBW:2 * KD * EBW].rearrange("p (k n) -> p k n", n=EBW),
                in_=s_out_lo[:, eb * EBW:(eb + 1) * EBW]
                .rearrange("(k p) n -> p k n", p=P)))(eb, wv),
              reads=[r_scr["out_lo"]], writes=[r_wslab[slot]], dma=True,
              tag=f"c{c}_owl{eb}")
            for t in range(TT):
                bank = (eb * TT + t) % 2
                pairs = [(ones2.ap()[:], b2_out.ap()[:, eb * EBW:(eb + 1) * EBW])]
                for k in range(KD):
                    ch_sl = ctxT_hi.ap()[:, k * NB + t * P: k * NB + (t + 1) * P]
                    cl_sl = ctxT_lo.ap()[:, k * NB + t * P: k * NB + (t + 1) * P]
                    whi = wv[:, k * EBW:(k + 1) * EBW]
                    wlo = wv[:, KD * EBW + k * EBW: KD * EBW + (k + 1) * EBW]
                    pairs += [(ch_sl, whi), (cl_sl, whi), (ch_sl, wlo)]
                mm_group_n(bank, EBW, pairs,
                           [r_ctxT[t], r_wslab[slot], r_ones2, r_b2o],
                           f"c{c}_ok{t}_{eb}")
                A("V", (lambda t, eb, bank: lambda e: e.tensor_add(
                    x2b.ap()[:, t * D + eb * EBW: t * D + (eb + 1) * EBW],
                    pb[bank].ap()[:, 0:EBW],
                    xb.ap()[:, t * D + eb * EBW: t * D + (eb + 1) * EBW]))(t, eb, bank),
                  reads=[r_pb[bank], r_xb[t]], writes=[r_x2[t]],
                  tag=f"c{c}_oe{t}_{eb}")
            gi += 1

        if debug and c == 0:
            for t in range(TT):
                A("S", (lambda t: lambda e: e.dma_start(
                    out=dbg["dbg_x2"][t * P:(t + 1) * P, :],
                    in_=x2b.ap()[:, t * D:(t + 1) * D]))(t),
                  reads=[r_x2[t]], writes=[r_dbg], dma=True, tag=f"dx2{t}")

        ln_stage(x2_tile, r_x2, f"c{c}L2")
        A("S", (lambda tok0: lambda e: e.dma_start(
            out=s_x2[tok0:tok0 + NB, :].rearrange("(i p) d -> p i d", p=P),
            in_=x2b.ap().rearrange("p (i d) -> p i d", d=D)))(tok0),
          reads=[r_x2[0], r_x2[1], r_x2[2], r_x2[3]], writes=[r_scr["x2sp"]],
          dma=True, tag=f"c{c}_sp")

        ldq = "Y"

        pl.alias_barrier(FFN_ARENA, ATTN_ARENA + [r_prepst[0], r_prepst[1]])
        if c == 0:
            pl.alias_barrier([r_wslab[2], r_wslab[3]],
                             [r_prepst[0], r_prepst[1]])
        OGW = 2
        for og in range(KH // OGW):
            slot = og % 4
            wv = wslab_slot(slot)
            ocols = slice(og * OGW * P, (og + 1) * OGW * P)
            A(ldq, (lambda ocols, wv: lambda e: e.dma_start(
                out=wv[:, 0:KD * OGW * P].rearrange("p (k o) -> p k o", o=OGW * P),
                in_=s_t["w1"][:, ocols].rearrange("(k p) o -> p k o", p=P)))(
                    ocols, wv),
              reads=[r_scr["t_w1"]], writes=[r_wslab[slot]], dma=True,
              tag=f"c{c}_w1l{og}")
            A(ldq, (lambda ocols, wv: lambda e: e.dma_start(
                out=wv[:, KD * OGW * P:2 * KD * OGW * P]
                .rearrange("p (k o) -> p k o", o=OGW * P),
                in_=s_t["w2"][:, ocols].rearrange("(k p) o -> p k o", p=P)))(
                    ocols, wv),
              reads=[r_scr["t_w2"]], writes=[r_wslab[slot]], dma=True,
              tag=f"c{c}_w2l{og}")
            for ol in range(OGW):
                o = og * OGW + ol
                b1 = o % 2
                b2b = 4 + (o % 2)
                pairs1, pairs2 = [], []
                for k in range(KD):
                    w1sl = wv[:, k * OGW * P + ol * P: k * OGW * P + (ol + 1) * P]
                    w2sl = wv[:, KD * OGW * P + k * OGW * P + ol * P:
                              KD * OGW * P + k * OGW * P + (ol + 1) * P]
                    mh = xnT_hi.ap()[:, k * NB:(k + 1) * NB]
                    mlo = xnT_lo.ap()[:, k * NB:(k + 1) * NB]
                    pairs1 += [(w1sl, mh), (w1sl, mlo)]
                    pairs2 += [(w2sl, mh), (w2sl, mlo)]
                mm_group(b1, pairs1, [r_xnT, r_wslab[slot]], f"c{c}_t1_{o}")
                mm_group(b2b, pairs2, [r_xnT, r_wslab[slot]], f"c{c}_t2_{o}")
                tf = tf512 if o % 2 == 0 else tf512b
                rtf = r_tf if o % 2 == 0 else r_tfb
                A("S", (lambda o, b1, tf: lambda e: e.activation(
                    tf.ap()[:], pb[b1].ap()[:], AF.Silu,
                    scale=s1c.ap()[:, o:o + 1]))(o, b1, tf),
                  reads=[r_pb[b1], r_scl["s1"]], writes=[rtf], tag=f"c{c}_si{o}")
                A("V", (lambda o, b2b, tf: lambda e: e.scalar_tensor_tensor(
                    h0f.ap()[:], pb[b2b].ap()[:], s2c.ap()[:, o:o + 1],
                    tf.ap()[:], ALU.mult, ALU.mult))(o, b2b, tf),
                  reads=[r_pb[b2b], rtf, r_scl["s2"]], writes=[r_h0f],
                  tag=f"c{c}_h0f{o}")
                A("V", (lambda o: lambda e: e.tensor_copy(
                    h0_hi.ap()[:, o * NB:(o + 1) * NB], h0f.ap()[:]))(o),
                  reads=[r_h0f], writes=[r_h0], tag=f"c{c}_h0h{o}")
                A("V", (lambda o: lambda e: e.tensor_sub(
                    h0_lo.ap()[:, o * NB:(o + 1) * NB], h0f.ap()[:],
                    h0_hi.ap()[:, o * NB:(o + 1) * NB]))(o),
                  reads=[r_h0f], writes=[r_h0], tag=f"c{c}_h0l{o}")

        if debug and c == 0:
            A("S", lambda e: e.dma_start(
                out=dbg["dbg_h0"].rearrange("(k p) n -> p k n", p=P),
                in_=h0_hi.ap().rearrange("p (k n) -> p k n", n=NB)),
              reads=[r_h0], writes=[r_dbg], dma=True, tag="dh0")

        pl.alias_barrier([r_h1p], TOKSIDE)
        for sidx, wkey, scl, rscl, shi, slo, sres, dhi, dlo, dres in (
                (0, "h1", h1sc, r_scl["h1"],
                 lambda k: h0_hi.ap()[:, k * NB:(k + 1) * NB],
                 lambda k: h0_lo.ap()[:, k * NB:(k + 1) * NB], r_h0,
                 lambda o: h1p_slice(True, o), lambda o: h1p_slice(False, o), r_h1p),
                (1, "h2", h2sc, r_scl["h2"],
                 lambda k: h1p_slice(True, k), lambda k: h1p_slice(False, k), r_h1p,
                 lambda o: h0_hi.ap()[:, o * NB:(o + 1) * NB],
                 lambda o: h0_lo.ap()[:, o * NB:(o + 1) * NB], r_h0)):
            OGH = 2
            NPAIR = 2
            if c == 0:
                pl.alias_barrier([r_wslab[2], r_wslab[3]],
                                 [r_prepst[0], r_prepst[1]])
            for og in range(KH // OGH):
                pairi = og % NPAIR
                wv = (wslab.ap()[:, 0:8192] if pairi == 0
                      else prepst_t.ap()[:, 0:8192])
                pres = ([r_wslab[0], r_wslab[1]] if pairi == 0
                        else [r_wslab[2], r_wslab[3]])
                ocols = slice(og * OGH * P, (og + 1) * OGH * P)
                A(ldq, (lambda wkey, ocols, wv: lambda e: e.dma_start(
                    out=wv[:, 0:8192].rearrange("p (k o) -> p k o", o=OGH * P),
                    in_=s_t[wkey][:, ocols].rearrange("(k p) o -> p k o", p=P)))(
                        wkey, ocols, wv),
                  reads=[r_scr["t_" + wkey]], writes=pres, dma=True,
                  tag=f"c{c}_{wkey}l{og}")
                for ol in range(OGH):
                    o = og * OGH + ol
                    bank = o % 2
                    tf = tf512 if o % 2 == 0 else tf512b
                    rtf = r_tf if o % 2 == 0 else r_tfb
                    pairs = []
                    for k in range(KH):
                        wsl = wv[:, k * OGH * P + ol * P: k * OGH * P + (ol + 1) * P]
                        pairs += [(wsl, shi(k)), (wsl, slo(k))]
                    mm_group(bank, pairs, [sres] + pres,
                             f"c{c}_{wkey}_{o}")
                    A("S", (lambda o, bank, scl, tf: lambda e: e.activation(
                        tf.ap()[:], pb[bank].ap()[:], AF.Gelu,
                        scale=scl.ap()[:, o:o + 1]))(o, bank, scl, tf),
                      reads=[r_pb[bank], rscl], writes=[rtf],
                      tag=f"c{c}_{wkey}g{o}")
                    A("V", (lambda o, dhi, tf: lambda e: e.tensor_copy(
                        dhi(o), tf.ap()[:]))(o, dhi, tf),
                      reads=[rtf], writes=[dres], tag=f"c{c}_{wkey}h{o}")
                    A("V", (lambda o, dhi, dlo, tf: lambda e: e.tensor_sub(
                        dlo(o), tf.ap()[:], dhi(o)))(o, dhi, dlo, tf),
                      reads=[rtf], writes=[dres], tag=f"c{c}_{wkey}lo{o}")

            if debug and c == 0 and sidx == 0:
                for pi in range(2):
                    A("S", (lambda pi: lambda e: e.dma_start(
                        out=dbg["dbg_h1p"][pi * 16 * P:(pi + 1) * 16 * P, :]
                        .rearrange("(k p) n -> p k n", p=P),
                        in_=h1p_hi_parts[pi].ap().rearrange("p (k n) -> p k n", n=NB)))(pi),
                      reads=[r_h1p], writes=[r_dbg], dma=True, tag=f"dh1p{pi}")

        pl.alias_barrier([r_x2[0], r_x2[1], r_x2[2], r_x2[3]], [r_h1p])
        A("S" if c == 0 else "Y", (lambda tok0: lambda e: e.dma_start(
            out=x2b.ap().rearrange("p (i d) -> p i d", d=D),
            in_=s_x2[tok0:tok0 + NB, :].rearrange("(i p) d -> p i d", p=P)))(tok0),
          reads=[r_scr["x2sp"]], writes=[r_x2[0], r_x2[1], r_x2[2], r_x2[3]],
          dma=True, tag=f"c{c}_x2r")
        NPIECE = 4
        for piece in range(KH // NPIECE):
            wslot = piece % 2
            w3v = w3slab.ap()[:, wslot * NPIECE * D:(wslot + 1) * NPIECE * D]
            A(ldq, (lambda piece, w3v: lambda e: e.dma_start(
                out=w3v.rearrange("p (k d) -> p k d", d=D),
                in_=s_t["w3"][piece * NPIECE * P:(piece + 1) * NPIECE * P, :]
                .rearrange("(k p) d -> p k d", p=P)))(piece, w3v),
              reads=[r_scr["t_w3"]], writes=[r_w3slab[wslot]], dma=True,
              tag=f"c{c}_w3l{piece}")
            for t in range(TT):
                for kl in range(NPIECE):
                    k = piece * NPIECE + kl
                    sh_ = h0_hi.ap()[:, k * NB + t * P: k * NB + (t + 1) * P]
                    sl_ = h0_lo.ap()[:, k * NB + t * P: k * NB + (t + 1) * P]
                    last = (piece == KH // NPIECE - 1 and kl == NPIECE - 1)
                    for db in range(2):
                        bank = t * 2 + db
                        mv = w3v[:, kl * D + db * NB: kl * D + (db + 1) * NB]
                        A("T", (lambda st, mv, bank, s0, s1: lambda e: e.matmul(
                            pb[bank].ap()[:], st, mv, start=s0, stop=s1))(
                                sh_, mv, bank, piece == 0 and kl == 0, False),
                          reads=[r_h0, r_w3slab[wslot]], writes=[r_pb[bank]],
                          tag=f"c{c}_w3h_p{piece}t{t}k{kl}d{db}")
                        A("T", (lambda st, mv, bank, s0, s1: lambda e: e.matmul(
                            pb[bank].ap()[:], st, mv, start=s0, stop=s1))(
                                sl_, mv, bank, False, last),
                          reads=[r_h0, r_w3slab[wslot]], writes=[r_pb[bank]],
                          tag=f"c{c}_w3lo_p{piece}t{t}k{kl}d{db}")
        pl.alias_barrier([r_tok[0], r_tok[1], r_tok[2], r_tok[3]], [r_h1p])
        ytokh = tokh.ap().bitcast(F32)
        ytokl = tokl.ap().bitcast(F32)
        def ystage(t, db):
            base = (t % 2) * D + db * NB
            return (ytokh if t < 2 else ytokl)[:, base:base + NB]
        for t in range(TT):
            for db in range(2):
                bank = t * 2 + db
                A("V", (lambda bank, db: lambda e: e.tensor_mul(
                    tf512.ap()[:], pb[bank].ap()[:],
                    s3_rep.ap()[:, db * NB:(db + 1) * NB]))(bank, db),
                  reads=[r_pb[bank], r_s3rep], writes=[r_tf],
                  tag=f"c{c}_w3s{t}_{db}")
                A("V", (lambda t, db: lambda e: e.tensor_add(
                    ystage(t, db), tf512.ap()[:],
                    x2b.ap()[:, t * D + db * NB: t * D + (db + 1) * NB]))(t, db),
                  reads=[r_tf, r_x2[t]], writes=[r_tok[t]], tag=f"c{c}_y{t}_{db}")
        A("S", (lambda tok0, ytokh: lambda e: e.dma_start(
            out=dy[tok0:tok0 + NB, :].rearrange("(i p) d -> p i d", p=P)[:, 0:2, :],
            in_=ytokh.rearrange("p (i d) -> p i d", d=D)))(tok0, ytokh),
          reads=[r_tok[0], r_tok[1]], writes=[r_ydram], dma=True,
          tag=f"c{c}_ysta")
        A("S", (lambda tok0, ytokl: lambda e: e.dma_start(
            out=dy[tok0:tok0 + NB, :].rearrange("(i p) d -> p i d", p=P)[:, 2:4, :],
            in_=ytokl.rearrange("p (i d) -> p i d", d=D)))(tok0, ytokl),
          reads=[r_tok[2], r_tok[3]], writes=[r_ydram], dma=True,
          tag=f"c{c}_ystb")

    A("S", lambda e: e.sem_inc(done_sem, 1), reads=[r_ydram, r_dbg], tag="fence")

    pl.finalize()
    with nc.Block() as block:
        pl.emit(block, sems)
    ctx_mgr.close()
    return nc, (list(dbg.keys()) if debug else []), pl


N_CORES = 8
T_FULL = 16384
T_CORE_DEFAULT = T_FULL // N_CORES


def kernel(L0_attn_x, L0_attn_W_qkv, L0_attn_b_qkv, L0_attn_W_out, L0_attn_b_out,
           L0_ffn_w1_w, L0_ffn_w1_s, L0_ffn_w2_w, L0_ffn_w2_s,
           L0_ffn_h1_w, L0_ffn_h1_s, L0_ffn_h2_w, L0_ffn_h2_s,
           L0_ffn_w3_w, L0_ffn_w3_s):
    x = np.ascontiguousarray(np.asarray(L0_attn_x).reshape(T_FULL, D))
    weights = {
        "W_qkv": np.ascontiguousarray(L0_attn_W_qkv),
        "b_qkv": np.ascontiguousarray(L0_attn_b_qkv),
        "W_out": np.ascontiguousarray(L0_attn_W_out),
        "b_out": np.ascontiguousarray(L0_attn_b_out),
        "w1_w": np.ascontiguousarray(L0_ffn_w1_w),
        "w1_s": np.ascontiguousarray(L0_ffn_w1_s),
        "w2_w": np.ascontiguousarray(L0_ffn_w2_w),
        "w2_s": np.ascontiguousarray(L0_ffn_w2_s),
        "h1_w": np.ascontiguousarray(L0_ffn_h1_w),
        "h1_s": np.ascontiguousarray(L0_ffn_h1_s),
        "h2_w": np.ascontiguousarray(L0_ffn_h2_w),
        "h2_s": np.ascontiguousarray(L0_ffn_h2_s),
        "w3_w": np.ascontiguousarray(L0_ffn_w3_w),
        "w3_s": np.ascontiguousarray(L0_ffn_w3_s),
    }
    nc, _dbg, _pl = build_core(T_CORE=T_CORE_DEFAULT, debug=False)
    ident = np.eye(P, dtype=ml_dtypes.bfloat16)
    in_maps = []
    for c in range(N_CORES):
        m = {"x": x[c * T_CORE_DEFAULT:(c + 1) * T_CORE_DEFAULT], "ident": ident}
        m.update(weights)
        in_maps.append(m)
    res = run_bass_kernel_spmd(nc, in_maps, core_ids=list(range(N_CORES)))
    y = np.concatenate([np.asarray(r["y"]) for r in res.results], axis=0)
    return y.reshape(1, T_FULL, D)


# version 11 (final)
# speedup vs baseline: 1.1483x; 1.0033x over previous
import numpy as np
import ml_dtypes
import concourse.bass as bass
import concourse.mybir as mybir
from concourse.bass_utils import run_bass_kernel_spmd
from contextlib import ExitStack

F32 = mybir.dt.float32
BF16 = mybir.dt.bfloat16
AF = mybir.ActivationFunctionType
ALU = mybir.AluOpType
AX = mybir.AxisListType

P = 128
D = 1024
E3 = 3072
HID = 4096
KD = D // P
KH = HID // P
NB = 512
NH = 16
DK = 64


class Res:
    __slots__ = ("name", "writers", "readers")

    def __init__(self, name):
        self.name = name
        self.writers = []
        self.readers = []


class Op:
    __slots__ = ("idx", "eng", "fn", "deps", "dma", "need_inc",
                 "sem_key", "sem_val", "waits", "tag", "eng_pos")

    def __init__(self, idx, eng, fn, dma, tag):
        self.idx = idx
        self.eng_pos = -1
        self.eng = eng
        self.fn = fn
        self.dma = dma
        self.deps = []
        self.need_inc = False
        self.sem_key = None
        self.sem_val = None
        self.waits = []
        self.tag = tag


class Plan:
    def __init__(self):
        self.ops = []
        self._eng_n = {}

    def add(self, eng, fn, reads=(), writes=(), dma=False, tag=""):
        op = Op(len(self.ops), eng, fn, dma, tag)
        op.eng_pos = self._eng_n.get(eng, 0)
        self._eng_n[eng] = op.eng_pos + 1
        deps = []
        for r in reads:
            deps.extend(r.writers)
        for w in writes:
            deps.extend(w.writers)
            deps.extend(w.readers)
        best = {}
        for d in deps:
            if d is op:
                continue
            key = ("Q" + d.eng) if d.dma else d.eng
            if key not in best or best[key].idx < d.idx:
                best[key] = d
        op.deps = list(best.values())
        for r in reads:
            r.readers.append(op)
        for w in writes:
            w.writers = [op]
            w.readers = []
        self.ops.append(op)
        return op

    def alias_barrier(self, dst_res, src_res):
        ops = []
        for r in src_res:
            ops.extend(r.writers)
            ops.extend(r.readers)
        for d in dst_res:
            d.readers.extend(ops)

    def finalize(self):
        for op in self.ops:
            resolved = []
            for d in op.deps:
                if d.dma:
                    resolved.append(d)
                elif d.eng != op.eng:
                    d.need_inc = True
                    resolved.append(d)
                elif (op.eng in ("V", "S") and not op.dma
                      and op.eng_pos - d.eng_pos == 1):
                    d.need_inc = True
                    resolved.append(d)
            op.deps = resolved
        ccnt, qcnt = {}, {}
        for op in self.ops:
            if op.dma:
                q = "Q" + op.eng
                qcnt[q] = qcnt.get(q, 0) + 16
                op.sem_key, op.sem_val = q, qcnt[q]
            elif op.need_inc:
                ccnt[op.eng] = ccnt.get(op.eng, 0) + 1
                op.sem_key, op.sem_val = op.eng, ccnt[op.eng]
        last = {}
        for op in self.ops:
            ws = {}
            for d in op.deps:
                k, v = d.sem_key, d.sem_val
                assert k is not None, (op.tag, d.tag)
                if last.get((op.eng, k), 0) < v and ws.get(k, 0) < v:
                    ws[k] = v
            for k, v in ws.items():
                last[(op.eng, k)] = v
            op.waits = sorted(ws.items())
        self.final_q = dict(qcnt)
        self.final_c = dict(ccnt)

    def emit(self, block, sems):
        per_eng = {}
        for op in self.ops:
            per_eng.setdefault(op.eng, []).append(op)

        def runner(ops):
            def go(eng):
                for op in ops:
                    for k, v in op.waits:
                        eng.wait_ge(sems[k], v)
                    inst = op.fn(eng)
                    if op.dma:
                        inst.then_inc(sems[op.sem_key], 16)
                    elif op.need_inc:
                        inst.then_inc(sems[op.sem_key], 1)
            return go

        for ename, attr in (("T", "tensor"), ("V", "vector"), ("S", "scalar"),
                            ("Y", "sync"), ("G", "gpsimd")):
            if ename in per_eng:
                getattr(block, attr)(runner(per_eng[ename]))


def build_core(T_CORE=2048, debug=False):
    C = T_CORE // NB
    TT = NB // P

    nc = bass.Bass("TRN2", target_bir_lowering=False)

    dx = nc.dram_tensor("x", [T_CORE, D], F32, kind="ExternalInput").ap()
    d_wqkv = nc.dram_tensor("W_qkv", [D, E3], F32, kind="ExternalInput").ap()
    d_bqkv = nc.dram_tensor("b_qkv", [E3], F32, kind="ExternalInput").ap()
    d_wout = nc.dram_tensor("W_out", [D, D], F32, kind="ExternalInput").ap()
    d_bout = nc.dram_tensor("b_out", [D], F32, kind="ExternalInput").ap()
    d_w1 = nc.dram_tensor("w1_w", [HID, D], F32, kind="ExternalInput").ap()
    d_w1s = nc.dram_tensor("w1_s", [HID], F32, kind="ExternalInput").ap()
    d_w2 = nc.dram_tensor("w2_w", [HID, D], F32, kind="ExternalInput").ap()
    d_w2s = nc.dram_tensor("w2_s", [HID], F32, kind="ExternalInput").ap()
    d_h1 = nc.dram_tensor("h1_w", [HID, HID], F32, kind="ExternalInput").ap()
    d_h1s = nc.dram_tensor("h1_s", [HID], F32, kind="ExternalInput").ap()
    d_h2 = nc.dram_tensor("h2_w", [HID, HID], F32, kind="ExternalInput").ap()
    d_h2s = nc.dram_tensor("h2_s", [HID], F32, kind="ExternalInput").ap()
    d_w3 = nc.dram_tensor("w3_w", [D, HID], F32, kind="ExternalInput").ap()
    d_w3s = nc.dram_tensor("w3_s", [D], F32, kind="ExternalInput").ap()
    d_ident = nc.dram_tensor("ident", [P, P], BF16, kind="ExternalInput").ap()
    dy = nc.dram_tensor("y", [T_CORE, D], F32, kind="ExternalOutput").ap()

    s_qkv_hi = nc.dram_tensor("s_qkv_hi", [D, E3], BF16, kind="Internal").ap()
    s_qkv_lo = nc.dram_tensor("s_qkv_lo", [D, E3], BF16, kind="Internal").ap()
    s_out_hi = nc.dram_tensor("s_out_hi", [D, D], BF16, kind="Internal").ap()
    s_out_lo = nc.dram_tensor("s_out_lo", [D, D], BF16, kind="Internal").ap()
    s_nat = {
        "w1": nc.dram_tensor("s_nat_w1", [HID, D], BF16, kind="Internal").ap(),
        "w2": nc.dram_tensor("s_nat_w2", [HID, D], BF16, kind="Internal").ap(),
        "h1": nc.dram_tensor("s_nat_h1", [HID, HID], BF16, kind="Internal").ap(),
        "h2": nc.dram_tensor("s_nat_h2", [HID, HID], BF16, kind="Internal").ap(),
        "w3": nc.dram_tensor("s_nat_w3", [D, HID], BF16, kind="Internal").ap(),
    }
    s_t = {
        "w1": nc.dram_tensor("s_t_w1", [D, HID], BF16, kind="Internal").ap(),
        "w2": nc.dram_tensor("s_t_w2", [D, HID], BF16, kind="Internal").ap(),
        "h1": nc.dram_tensor("s_t_h1", [HID, HID], BF16, kind="Internal").ap(),
        "h2": nc.dram_tensor("s_t_h2", [HID, HID], BF16, kind="Internal").ap(),
        "w3": nc.dram_tensor("s_t_w3", [HID, D], BF16, kind="Internal").ap(),
    }
    NROW = E3 + 2 * D
    s_bias = nc.dram_tensor("s_bias", [2, NROW], BF16, kind="Internal").ap()
    s_x2 = nc.dram_tensor("s_x2", [T_CORE, D], F32, kind="Internal").ap()

    dbg = {}
    if debug:
        for nm, shp, dt_ in (("dbg_qkv", [NB, E3], F32),
                             ("dbg_probs", [NB, NH * NH], F32),
                             ("dbg_ctx", [NB, D], F32),
                             ("dbg_x2", [NB, D], F32),
                             ("dbg_h0", [HID, NB], BF16),
                             ("dbg_h1p", [HID, NB], BF16),
                             ("dbg_xb", [P, 4 * D], F32),
                             ("dbg_stats", [P, 32], F32),
                             ("dbg_xnT", [P, KD * NB], BF16)):
            dbg[nm] = nc.dram_tensor(nm, shp, dt_, kind="ExternalOutput").ap()

    ctx_mgr = ExitStack()
    E = ctx_mgr.enter_context

    def addr_of(handle):
        name = handle.name
        for a in nc.m.functions[0].allocations:
            if isinstance(a, mybir.MemoryLocationSet):
                for ml in a.memorylocations:
                    if ml.name == name:
                        return ml.addr
        raise KeyError(name)

    ident = E(nc.sbuf_tensor("t_ident", [P, P], BF16))
    ones2 = E(nc.sbuf_tensor("t_ones2", [2, P], BF16))
    b2_qkv = E(nc.sbuf_tensor("t_b2qkv", [2, E3], BF16))
    b2_out = E(nc.sbuf_tensor("t_b2out", [2, D], BF16))
    s3pair = E(nc.sbuf_tensor("t_s3pair", [2, D], BF16))
    s3_rep = E(nc.sbuf_tensor("t_s3rep", [P, D], F32))
    s1c = E(nc.sbuf_tensor("t_s1c", [P, KH], F32))
    s2c = E(nc.sbuf_tensor("t_s2c", [P, KH], F32))
    h1sc = E(nc.sbuf_tensor("t_h1sc", [P, KH], F32))
    h2sc = E(nc.sbuf_tensor("t_h2sc", [P, KH], F32))
    stq = [E(nc.sbuf_tensor(f"t_st{i}", [P, TT], F32)) for i in range(8)]
    nm16 = E(nc.sbuf_tensor("t_nm16", [P, NH], F32))
    den16 = E(nc.sbuf_tensor("t_den16", [P, NH], F32))
    rec16 = E(nc.sbuf_tensor("t_rec16", [P, NH], F32))
    sqt = E(nc.sbuf_tensor("t_sq", [P, D], F32))
    xnf = E(nc.sbuf_tensor("t_xnf", [P, D], F32))
    tf512 = E(nc.sbuf_tensor("t_tf512", [P, NB], F32))
    tf512b = E(nc.sbuf_tensor("t_tf512b", [P, NB], F32))
    h0f = E(nc.sbuf_tensor("t_h0f", [P, NB], F32))

    xb = E(nc.sbuf_tensor("t_xb", [P, TT * D], F32))
    x2b = E(nc.sbuf_tensor("t_x2b", [P, TT * D], F32))
    tokh = E(nc.sbuf_tensor("t_tokh", [P, TT * D], BF16))
    tokl = E(nc.sbuf_tensor("t_tokl", [P, TT * D], BF16))
    xnT_hi = E(nc.sbuf_tensor("t_xnT_hi", [P, KD * NB], BF16))
    xnT_lo = E(nc.sbuf_tensor("t_xnT_lo", [P, KD * NB], BF16))
    wslab = E(nc.sbuf_tensor("t_wslab", [P, 2 * 4096], BF16))
    arena = E(nc.sbuf_tensor("t_arena", [P, 40960], BF16))
    abase = addr_of(arena)

    def at(name, shape, dtype, off):
        return nc.alloc_sbuf_tensor_at(name, shape, dtype, offset=abase + off)

    qkvb = at("v_qkv", [P, TT * E3], F32, 0)
    sc_tmp = at("v_sctmp", [P, NH * DK], F32, 49152)
    scoreb = at("v_score", [P, NH * NH], F32, 53248)
    probb = at("v_prob", [P, NH * NH], F32, 54272)
    acc_a = at("v_acca", [P, D], F32, 55296)
    acc_b = at("v_accb", [P, D], F32, 59392)
    ctxT_hi = at("v_ctxThi", [P, KD * NB], BF16, 63488)
    ctxT_lo = at("v_ctxTlo", [P, KD * NB], BF16, 71680)
    h0_hi = at("v_h0hi", [P, KH * NB], BF16, 0)
    h0_lo = at("v_h0lo", [P, KH * NB], BF16, 32768)
    w3slab = at("v_w3slab", [P, 8 * D], BF16, 65536)
    prepst_t = E(nc.sbuf_tensor("t_prepst", [P, 2 * HID], BF16))
    prepst = [prepst_t.ap()[:, 0:HID], prepst_t.ap()[:, HID:2 * HID]]

    def wslab_slot(slot):
        if slot < 2:
            return wslab.ap()[:, slot * 4096:(slot + 1) * 4096]
        return prepst_t.ap()[:, (slot - 2) * 4096:(slot - 1) * 4096]
    prepw = at("v_prepw", [P, E3], F32, 0)
    prep_hi = at("v_prephi", [P, E3], BF16, 12288)
    prep_lo = at("v_preplo", [P, E3], BF16, 18432)
    prow = at("v_prow", [1, NROW], F32, 24576)
    prow_b = at("v_prowb", [1, 2 * NROW], BF16, 45056)

    h1p_hi_parts = [
        nc.alloc_sbuf_tensor_at("v_h1phiA", [P, 8192], BF16, offset=addr_of(xb)),
        nc.alloc_sbuf_tensor_at("v_h1phiB", [P, 8192], BF16, offset=addr_of(x2b)),
    ]
    h1p_lo_parts = [
        nc.alloc_sbuf_tensor_at("v_h1ploA", [P, 4096], BF16, offset=addr_of(tokh)),
        nc.alloc_sbuf_tensor_at("v_h1ploB", [P, 4096], BF16, offset=addr_of(tokl)),
        nc.alloc_sbuf_tensor_at("v_h1ploC", [P, 4096], BF16, offset=addr_of(xnT_hi)),
        nc.alloc_sbuf_tensor_at("v_h1ploD", [P, 4096], BF16, offset=addr_of(xnT_lo)),
    ]

    def h1p_slice(hi, k):
        if hi:
            t = h1p_hi_parts[k // 16]
            k = k % 16
        else:
            t = h1p_lo_parts[k // 8]
            k = k % 8
        return t.ap()[:, k * NB:(k + 1) * NB]

    pb = [E(nc.psum_tensor(f"t_pb{i}", [P, NB], F32)) for i in range(8)]

    sems = {}
    for k in ("T", "V", "S", "G", "QY", "QG", "QS"):
        sems[k] = E(nc.semaphore("sem_" + k))
    done_sem = E(nc.semaphore("sem_done"))

    pl = Plan()
    A = pl.add
    r_pb = [Res(f"pb{i}") for i in range(8)]
    r_scr = {k: Res("scr_" + k) for k in
             ("qkv_hi", "qkv_lo", "out_hi", "out_lo",
              "nat_w1", "nat_w2", "nat_h1", "nat_h2", "nat_w3",
              "t_w1", "t_w2", "t_h1", "t_h2", "t_w3", "bias", "x2sp")}
    r_ident = Res("ident")
    r_ones2 = Res("ones2")
    r_b2q = Res("b2q")
    r_b2o = Res("b2o")
    r_s3p = Res("s3p")
    r_s3rep = Res("s3rep")
    r_scl = {k: Res("scl_" + k) for k in ("s1", "s2", "h1", "h2")}
    r_prepst = [Res("prepstA"), Res("prepstB")]
    r_prepw = Res("prepw")
    r_prow = Res("prow")
    r_xb = [Res(f"xb{t}") for t in range(4)]
    r_x2 = [Res(f"x2_{t}") for t in range(4)]
    r_tok = [Res(f"tok{t}") for t in range(4)]
    r_xnT = Res("xnT")
    r_qkv = [Res(f"qkv{t}") for t in range(4)]
    r_att = Res("att")
    r_scores = Res("scores")
    r_probs = Res("probs")
    r_smx = Res("smx")
    r_acc = Res("acc")
    r_ctxT = [Res(f"ctxT{t}") for t in range(4)]
    r_wslab = [Res("wslabA"), Res("wslabB"), Res("wslabC"), Res("wslabD")]
    r_w3slab = [Res("w3slabA"), Res("w3slabB")]
    r_h0 = Res("h0")
    r_h1p = Res("h1p")
    r_stats = Res("stats")
    r_tmpA = Res("tmpA")
    r_tmpB = Res("tmpB")
    r_tf = Res("tf")
    r_tfb = Res("tfb")
    r_h0f = Res("h0f")
    r_ydram = Res("ydram")
    r_dbg = Res("dbg")

    ATTN_ARENA = [r_qkv[0], r_qkv[1], r_qkv[2], r_qkv[3], r_att,
                  r_scores, r_probs, r_smx, r_acc,
                  r_ctxT[0], r_ctxT[1], r_ctxT[2], r_ctxT[3]]
    FFN_ARENA = [r_h0, r_w3slab[0], r_w3slab[1]]
    TOKSIDE = [r_xb[0], r_xb[1], r_xb[2], r_xb[3],
               r_x2[0], r_x2[1], r_x2[2], r_x2[3],
               r_tok[0], r_tok[1], r_tok[2], r_tok[3], r_xnT]

    A("Y", lambda e: e.dma_start(out=ident.ap()[:], in_=d_ident[:, :]),
      writes=[r_ident], dma=True, tag="ld_ident")
    A("V", lambda e: e.memset(ones2.ap()[:], 1.0), writes=[r_ones2], tag="ones2")
    for dst, src, rr in ((s1c, d_w1s, r_scl["s1"]), (s2c, d_w2s, r_scl["s2"]),
                         (h1sc, d_h1s, r_scl["h1"]), (h2sc, d_h2s, r_scl["h2"])):
        A("Y", (lambda dst, src: lambda e: e.dma_start(
            out=dst.ap()[:], in_=src.rearrange("(o p) -> p o", p=P),
            allow_slow_non_contiguous=True))(dst, src),
          writes=[rr], dma=True, tag="ld_scale")

    A("Y", lambda e: e.dma_start(out=prow.ap()[0:1, 0:E3], in_=d_bqkv.unsqueeze(0)),
      writes=[r_prow], dma=True, tag="ld_bqkv")
    A("Y", lambda e: e.dma_start(out=prow.ap()[0:1, E3:E3 + D], in_=d_bout.unsqueeze(0)),
      writes=[r_prow], dma=True, tag="ld_bout")
    A("Y", lambda e: e.dma_start(out=prow.ap()[0:1, E3 + D:NROW], in_=d_w3s.unsqueeze(0)),
      writes=[r_prow], dma=True, tag="ld_w3s")
    A("V", lambda e: e.tensor_copy(prow_b.ap()[0:1, 0:NROW], prow.ap()[0:1, 0:NROW]),
      reads=[r_prow], writes=[r_prow], tag="bias_hi")
    A("V", lambda e: e.tensor_sub(prow_b.ap()[0:1, NROW:2 * NROW],
                                  prow.ap()[0:1, 0:NROW], prow_b.ap()[0:1, 0:NROW]),
      reads=[r_prow], writes=[r_prow], tag="bias_lo")
    A("S", lambda e: e.dma_start(out=s_bias.rearrange("a b -> (a b)").unsqueeze(0),
                                 in_=prow_b.ap()[0:1, 0:2 * NROW]),
      reads=[r_prow], writes=[r_scr["bias"]], dma=True, tag="st_bias")
    A("Y", lambda e: e.dma_start(out=b2_qkv.ap()[:], in_=s_bias[:, 0:E3]),
      reads=[r_scr["bias"]], writes=[r_b2q], dma=True, tag="ld_b2qkv")
    A("Y", lambda e: e.dma_start(out=b2_out.ap()[:], in_=s_bias[:, E3:E3 + D]),
      reads=[r_scr["bias"]], writes=[r_b2o], dma=True, tag="ld_b2out")
    A("Y", lambda e: e.dma_start(out=s3pair.ap()[:], in_=s_bias[:, E3 + D:NROW]),
      reads=[r_scr["bias"]], writes=[r_s3p], dma=True, tag="ld_s3pair")
    for db in range(2):
        A("T", (lambda db: lambda e: e.matmul(
            pb[6 + db].ap()[:], ones2.ap()[:],
            s3pair.ap()[:, db * NB:(db + 1) * NB], start=True, stop=True))(db),
          reads=[r_ones2, r_s3p], writes=[r_pb[6 + db]], tag="s3rep_mm")
        A("S", (lambda db: lambda e: e.copy(
            s3_rep.ap()[:, db * NB:(db + 1) * NB], pb[6 + db].ap()[:]))(db),
          reads=[r_pb[6 + db]], writes=[r_s3rep], tag="s3rep_ev")

    for dsrc, dhi, dlo, width, rhi, rlo, tagn in (
            (d_wqkv, s_qkv_hi, s_qkv_lo, E3, r_scr["qkv_hi"], r_scr["qkv_lo"], "qkv"),
            (d_wout, s_out_hi, s_out_lo, D, r_scr["out_hi"], r_scr["out_lo"], "out")):
        for kt in range(KD):
            sl = slice(kt * P, (kt + 1) * P)
            A("G", (lambda dsrc, sl, width: lambda e: e.dma_start(
                out=prepw.ap()[:, 0:width], in_=dsrc[sl, :]))(dsrc, sl, width),
              writes=[r_prepw], dma=True, tag=f"p_{tagn}ld{kt}")
            A("V", (lambda width: lambda e: e.tensor_copy(
                prep_hi.ap()[:, 0:width], prepw.ap()[:, 0:width]))(width),
              reads=[r_prepw], writes=[r_prepw], tag=f"p_{tagn}hi{kt}")
            A("V", (lambda width: lambda e: e.tensor_sub(
                prep_lo.ap()[:, 0:width], prepw.ap()[:, 0:width],
                prep_hi.ap()[:, 0:width]))(width),
              reads=[r_prepw], writes=[r_prepw], tag=f"p_{tagn}lo{kt}")
            A("Y", (lambda dhi, sl, width: lambda e: e.dma_start(
                out=dhi[sl, :], in_=prep_hi.ap()[:, 0:width]))(dhi, sl, width),
              reads=[r_prepw], writes=[rhi], dma=True, tag=f"p_{tagn}sh{kt}")
            A("Y", (lambda dlo, sl, width: lambda e: e.dma_start(
                out=dlo[sl, :], in_=prep_lo.ap()[:, 0:width]))(dlo, sl, width),
              reads=[r_prepw], writes=[rlo], dma=True, tag=f"p_{tagn}sl{kt}")

    PREP_DIMS = {"w1": (HID, D), "w2": (HID, D), "h1": (HID, HID),
                 "h2": (HID, HID), "w3": (D, HID)}
    PREP_SRC = {"w1": d_w1, "w2": d_w2, "h1": d_h1, "h2": d_h2, "w3": d_w3}
    for wname in ("w1", "w2", "h1", "h2", "w3"):
        A("G", (lambda wname, dsrc: lambda e: e.dma_start(
            out=s_nat[wname][:, :], in_=dsrc[:, :]))(wname, PREP_SRC[wname]),
          writes=[r_scr["nat_" + wname]], dma=True, tag=f"p_cast_{wname}")

    def plan_prep_tr(wname):
        O, I = PREP_DIMS[wname]
        for it in range(I // P):
            sl = it % 2
            A("Y", (lambda wname, it, O, sl: lambda e: e.dma_start(
                out=prepst[sl][:, 0:O],
                in_=s_nat[wname][0:O, it * P:(it + 1) * P],
                transpose=True))(wname, it, O, sl),
              reads=[r_scr["nat_" + wname]], writes=[r_prepst[sl]], dma=True,
              tag=f"p_tld_{wname}_{it}")
            A("Y", (lambda wname, it, O, sl: lambda e: e.dma_start(
                out=s_t[wname][it * P:(it + 1) * P, 0:O],
                in_=prepst[sl][:, 0:O]))(wname, it, O, sl),
              reads=[r_prepst[sl]], writes=[r_scr["t_" + wname]], dma=True,
              tag=f"p_tst_{wname}_{it}")

    for _w in ("w1", "w2", "h1", "h2", "w3"):
        plan_prep_tr(_w)

    def mm_group(bank, pairs, extra_reads, tag):
        N = len(pairs)
        for i, (st, mv) in enumerate(pairs):
            A("T", (lambda st, mv, s0, s1: lambda e: e.matmul(
                pb[bank].ap()[:], st, mv, start=s0, stop=s1))(
                    st, mv, i == 0, i == N - 1),
              reads=list(extra_reads), writes=[r_pb[bank]], tag=f"{tag}_mm{i}")

    def mm_group_n(bank, width, pairs, extra_reads, tag):
        N = len(pairs)
        for i, (st, mv) in enumerate(pairs):
            A("T", (lambda st, mv, s0, s1: lambda e: e.matmul(
                pb[bank].ap()[:, 0:width], st, mv, start=s0, stop=s1))(
                    st, mv, i == 0, i == N - 1),
              reads=list(extra_reads), writes=[r_pb[bank]], tag=f"{tag}_mm{i}")

    def ln_stage(src_tile, src_res, tag):
        s_sx, s_sq, s_m, s_r = stq[0], stq[1], stq[2], stq[3]
        s_t1, s_t2, s_t3, s_t4 = stq[4], stq[5], stq[6], stq[7]
        for i in range(TT):
            A("S", (lambda i: lambda e: e.activation(
                sqt.ap()[:], src_tile(i), AF.Square))(i),
              reads=[src_res[i]], writes=[r_tmpA], tag=f"{tag}_sq{i}")
            A("V", (lambda i: lambda e: e.tensor_reduce(
                s_sx.ap()[:, i:i + 1], src_tile(i), AX.X, ALU.add))(i),
              reads=[src_res[i]], writes=[r_stats], tag=f"{tag}_sx{i}")
            A("V", (lambda i: lambda e: e.tensor_reduce(
                s_sq.ap()[:, i:i + 1], sqt.ap()[:], AX.X, ALU.add))(i),
              reads=[r_tmpA], writes=[r_stats], tag=f"{tag}_sqr{i}")
        A("V", lambda e: e.tensor_scalar_mul(s_m.ap()[:], s_sx.ap()[:], 1.0 / D),
          reads=[r_stats], writes=[r_stats], tag=f"{tag}_m")
        A("V", lambda e: e.tensor_scalar_mul(s_t1.ap()[:], s_sq.ap()[:], 1.0 / D),
          reads=[r_stats], writes=[r_stats], tag=f"{tag}_q")
        A("V", lambda e: e.tensor_mul(s_t2.ap()[:], s_m.ap()[:], s_m.ap()[:]),
          reads=[r_stats], writes=[r_stats], tag=f"{tag}_m2")
        A("V", lambda e: e.tensor_sub(s_t3.ap()[:], s_t1.ap()[:], s_t2.ap()[:]),
          reads=[r_stats], writes=[r_stats], tag=f"{tag}_v")
        A("V", lambda e: e.tensor_scalar_add(s_t4.ap()[:], s_t3.ap()[:], 1e-5),
          reads=[r_stats], writes=[r_stats], tag=f"{tag}_a")
        A("S", lambda e: e.activation(s_t1.ap()[:], s_t4.ap()[:], AF.Ln),
          reads=[r_stats], writes=[r_stats], tag=f"{tag}_ln")
        A("S", lambda e: e.activation(s_t2.ap()[:], s_t1.ap()[:], AF.Exp,
                                      scale=-0.5),
          reads=[r_stats], writes=[r_stats], tag=f"{tag}_y0")
        A("V", lambda e: e.tensor_mul(s_t1.ap()[:], s_t2.ap()[:], s_t2.ap()[:]),
          reads=[r_stats], writes=[r_stats], tag=f"{tag}_n1")
        A("V", lambda e: e.tensor_mul(s_t3.ap()[:], s_t1.ap()[:], s_t4.ap()[:]),
          reads=[r_stats], writes=[r_stats], tag=f"{tag}_n2")
        A("V", lambda e: e.tensor_scalar(s_t1.ap()[:], s_t3.ap()[:], -0.5, 1.5,
                                         ALU.mult, ALU.add),
          reads=[r_stats], writes=[r_stats], tag=f"{tag}_n3")
        A("V", lambda e: e.tensor_mul(s_r.ap()[:], s_t2.ap()[:], s_t1.ap()[:]),
          reads=[r_stats], writes=[r_stats], tag=f"{tag}_r")
        for i in range(TT):
            A("V", (lambda i: lambda e: e.tensor_scalar(
                xnf.ap()[:], src_tile(i), s_m.ap()[:, i:i + 1],
                s_r.ap()[:, i:i + 1], ALU.subtract, ALU.mult))(i),
              reads=[src_res[i], r_stats], writes=[r_tmpB], tag=f"{tag}_nm{i}")
            A("V", (lambda i: lambda e: e.tensor_copy(
                tokh.ap()[:, i * D:(i + 1) * D], xnf.ap()[:]))(i),
              reads=[r_tmpB], writes=[r_tok[i]], tag=f"{tag}_hi{i}")
            A("V", (lambda i: lambda e: e.tensor_sub(
                tokl.ap()[:, i * D:(i + 1) * D], xnf.ap()[:],
                tokh.ap()[:, i * D:(i + 1) * D]))(i),
              reads=[r_tmpB], writes=[r_tok[i]], tag=f"{tag}_lo{i}")
        for hi in (True, False):
            tsrc = tokh if hi else tokl
            tdst = xnT_hi if hi else xnT_lo
            for j in range(KD):
                bank = 2 + (j % 2)
                pbf = pb[bank].ap().bitcast(BF16)
                for i in range(TT):
                    A("T", (lambda i, j, pbf, tsrc: lambda e: e.transpose(
                        pbf[:, i * P:(i + 1) * P],
                        tsrc.ap()[:, i * D + j * P: i * D + (j + 1) * P],
                        ident.ap()[:]))(i, j, pbf, tsrc),
                      reads=[r_tok[i], r_ident], writes=[r_pb[bank]],
                      tag=f"{tag}_tr{j}_{i}")
                A("S", (lambda j, pbf, tdst: lambda e: e.copy(
                    tdst.ap()[:, j * NB:(j + 1) * NB], pbf[:, 0:NB]))(j, pbf, tdst),
                  reads=[r_pb[bank]], writes=[r_xnT], tag=f"{tag}_te{j}")

    xb_tile = lambda i: xb.ap()[:, i * D:(i + 1) * D]
    x2_tile = lambda i: x2b.ap()[:, i * D:(i + 1) * D]

    for c in range(C):
        tok0 = c * NB
        NSLOT = 2 if c == 0 else 4
        pl.alias_barrier(TOKSIDE, [r_h1p])
        pl.alias_barrier(ATTN_ARENA, FFN_ARENA + [r_prepw, r_prow])
        if c == 1:
            pl.alias_barrier([r_wslab[2], r_wslab[3]],
                             [r_prepst[0], r_prepst[1]])

        A("S" if c == 0 else "Y", (lambda tok0: lambda e: e.dma_start(
            out=xb.ap().rearrange("p (i d) -> p i d", d=D),
            in_=dx[tok0:tok0 + NB, :].rearrange("(i p) d -> p i d", p=P)))(tok0),
          writes=[r_xb[0], r_xb[1], r_xb[2], r_xb[3]], dma=True, tag=f"c{c}_x")

        ln_stage(xb_tile, r_xb, f"c{c}L1")

        if debug and c == 0:
            A("S", lambda e: e.dma_start(out=dbg["dbg_xb"][:, :], in_=xb.ap()[:]),
              reads=[r_xb[0], r_xb[1], r_xb[2], r_xb[3]], writes=[r_dbg], dma=True, tag="dxb")
            A("S", (lambda i: lambda e: e.dma_start(out=dbg["dbg_stats"][:, 0:4], in_=stq[0].ap()[:]))(0),
              reads=[r_stats], writes=[r_dbg], dma=True, tag="dst0")
            A("S", (lambda i: lambda e: e.dma_start(out=dbg["dbg_stats"][:, 4:8], in_=stq[1].ap()[:]))(1),
              reads=[r_stats], writes=[r_dbg], dma=True, tag="dst1")
            A("S", (lambda i: lambda e: e.dma_start(out=dbg["dbg_stats"][:, 8:12], in_=stq[2].ap()[:]))(2),
              reads=[r_stats], writes=[r_dbg], dma=True, tag="dst2")
            A("S", (lambda i: lambda e: e.dma_start(out=dbg["dbg_stats"][:, 12:16], in_=stq[3].ap()[:]))(3),
              reads=[r_stats], writes=[r_dbg], dma=True, tag="dst3")
            A("S", (lambda i: lambda e: e.dma_start(out=dbg["dbg_stats"][:, 16:20], in_=stq[4].ap()[:]))(4),
              reads=[r_stats], writes=[r_dbg], dma=True, tag="dst4")
            A("S", (lambda i: lambda e: e.dma_start(out=dbg["dbg_stats"][:, 20:24], in_=stq[5].ap()[:]))(5),
              reads=[r_stats], writes=[r_dbg], dma=True, tag="dst5")
            A("S", (lambda i: lambda e: e.dma_start(out=dbg["dbg_stats"][:, 24:28], in_=stq[6].ap()[:]))(6),
              reads=[r_stats], writes=[r_dbg], dma=True, tag="dst6")
            A("S", (lambda i: lambda e: e.dma_start(out=dbg["dbg_stats"][:, 28:32], in_=stq[7].ap()[:]))(7),
              reads=[r_stats], writes=[r_dbg], dma=True, tag="dst7")
            A("S", lambda e: e.dma_start(out=dbg["dbg_xnT"][:, :], in_=xnT_hi.ap()[:]),
              reads=[r_xnT], writes=[r_dbg], dma=True, tag="dxnT")

        EBW = 256
        ldq = "S" if c == 0 else "Y"
        gi = 0
        for eb in range(E3 // EBW):
            slot = gi % NSLOT
            wv = wslab_slot(slot)
            A(ldq, (lambda eb, wv: lambda e: e.dma_start(
                out=wv[:, 0:KD * EBW].rearrange("p (k n) -> p k n", n=EBW),
                in_=s_qkv_hi[:, eb * EBW:(eb + 1) * EBW]
                .rearrange("(k p) n -> p k n", p=P)))(eb, wv),
              reads=[r_scr["qkv_hi"]], writes=[r_wslab[slot]], dma=True,
              tag=f"c{c}_qwh{eb}")
            A(ldq, (lambda eb, wv: lambda e: e.dma_start(
                out=wv[:, KD * EBW:2 * KD * EBW].rearrange("p (k n) -> p k n", n=EBW),
                in_=s_qkv_lo[:, eb * EBW:(eb + 1) * EBW]
                .rearrange("(k p) n -> p k n", p=P)))(eb, wv),
              reads=[r_scr["qkv_lo"]], writes=[r_wslab[slot]], dma=True,
              tag=f"c{c}_qwl{eb}")
            for t in range(TT):
                bank = (eb * TT + t) % 2
                pairs = [(ones2.ap()[:], b2_qkv.ap()[:, eb * EBW:(eb + 1) * EBW])]
                for k in range(KD):
                    xh_sl = xnT_hi.ap()[:, k * NB + t * P: k * NB + (t + 1) * P]
                    xl_sl = xnT_lo.ap()[:, k * NB + t * P: k * NB + (t + 1) * P]
                    whi = wv[:, k * EBW:(k + 1) * EBW]
                    wlo = wv[:, KD * EBW + k * EBW: KD * EBW + (k + 1) * EBW]
                    pairs += [(xh_sl, whi), (xl_sl, whi), (xh_sl, wlo)]
                mm_group_n(bank, EBW, pairs,
                           [r_xnT, r_wslab[slot], r_ones2, r_b2q], f"c{c}_qk{t}_{eb}")
                A("S", (lambda t, eb, bank: lambda e: e.copy(
                    qkvb.ap()[:, t * E3 + eb * EBW: t * E3 + (eb + 1) * EBW],
                    pb[bank].ap()[:, 0:EBW]))(t, eb, bank),
                  reads=[r_pb[bank]], writes=[r_qkv[t]], tag=f"c{c}_qe{t}_{eb}")
            gi += 1

        if debug and c == 0:
            for t in range(TT):
                A("S", (lambda t: lambda e: e.dma_start(
                    out=dbg["dbg_qkv"][t * P:(t + 1) * P, :],
                    in_=qkvb.ap()[:, t * E3:(t + 1) * E3]))(t),
                  reads=[r_qkv[t]], writes=[r_dbg], dma=True, tag=f"dq{t}")

        sc3 = sc_tmp.ap().rearrange("p (h c) -> p h c", c=DK)
        sco3 = scoreb.ap().rearrange("p (h g) -> p h g", g=NH)
        pr3 = probb.ap().rearrange("p (h g) -> p h g", g=NH)
        acc3 = [acc_a.ap().rearrange("p (h c) -> p h c", c=DK),
                acc_b.ap().rearrange("p (h c) -> p h c", c=DK)]
        for t in range(TT):
            q3 = qkvb.ap()[:, t * E3:(t + 1) * E3].rearrange(
                "p (h c) -> p h c", c=3 * DK)
            xn3 = xnf.ap().rearrange("p (h c) -> p h c", c=DK)

            def sc_mult(g):
                buf = sc3 if g % 2 == 0 else xn3
                eng = "V" if g % 2 == 0 else "G"
                wres = r_att if g % 2 == 0 else r_tmpB
                A(eng, (lambda q3, g, buf: lambda e: e.tensor_mul(
                    buf, q3[:, :, 0:DK],
                    q3[:, g:g + 1, DK:2 * DK].broadcast_to([P, NH, DK])))(q3, g, buf),
                  reads=[r_qkv[t]], writes=[wres], tag=f"c{c}_scm{t}_{g}")

            def sc_red(g):
                buf = sc3 if g % 2 == 0 else xn3
                rres = r_att if g % 2 == 0 else r_tmpB
                A("V", (lambda g, buf: lambda e: e.tensor_reduce(
                    sco3[:, :, g:g + 1], buf, AX.X, ALU.add))(g, buf),
                  reads=[rres], writes=[r_scores], tag=f"c{c}_scr{t}_{g}")

            sc_mult(0)
            for g in range(1, NH):
                sc_mult(g)
                sc_red(g - 1)
            sc_red(NH - 1)
            A("V", lambda e: e.tensor_reduce(nm16.ap()[:].unsqueeze(2), sco3,
                                             AX.X, ALU.max, negate=True),
              reads=[r_scores], writes=[r_smx], tag=f"c{c}_nm{t}")
            A("V", lambda e: e.tensor_tensor(
                pr3, sco3, nm16.ap()[:].unsqueeze(2).broadcast_to([P, NH, NH]),
                ALU.add),
              reads=[r_scores, r_smx], writes=[r_probs], tag=f"c{c}_sub{t}")
            A("S", lambda e: e.activation(scoreb.ap()[:], probb.ap()[:], AF.Exp,
                                          scale=0.125),
              reads=[r_probs], writes=[r_scores], tag=f"c{c}_exp{t}")
            A("V", lambda e: e.tensor_reduce(den16.ap()[:].unsqueeze(2), sco3,
                                             AX.X, ALU.add),
              reads=[r_scores], writes=[r_smx], tag=f"c{c}_den{t}")
            A("S", lambda e: e.activation(nm16.ap()[:], den16.ap()[:], AF.Ln),
              reads=[r_smx], writes=[r_smx], tag=f"c{c}_ld{t}")
            A("S", lambda e: e.activation(rec16.ap()[:], nm16.ap()[:], AF.Exp,
                                          scale=-1.0),
              reads=[r_smx], writes=[r_smx], tag=f"c{c}_rc{t}")
            A("V", lambda e: e.tensor_tensor(
                pr3, sco3, rec16.ap()[:].unsqueeze(2).broadcast_to([P, NH, NH]),
                ALU.mult),
              reads=[r_scores, r_smx], writes=[r_probs], tag=f"c{c}_dv{t}")
            if debug and c == 0:
                A("S", (lambda t: lambda e: e.dma_start(
                    out=dbg["dbg_probs"][t * P:(t + 1) * P, :], in_=probb.ap()[:]))(t),
                  reads=[r_probs], writes=[r_dbg], dma=True, tag=f"dp{t}")
            A("V", (lambda q3: lambda e: e.tensor_mul(
                acc3[0], pr3[:, :, 0:1].broadcast_to([P, NH, DK]),
                q3[:, 0:1, 2 * DK:3 * DK].broadcast_to([P, NH, DK])))(q3),
              reads=[r_probs, r_qkv[t]], writes=[r_acc], tag=f"c{c}_cx0{t}")
            cur = 0
            for g in range(1, NH):
                buf = sc3 if g % 2 == 0 else xn3
                bres = r_att if g % 2 == 0 else r_tmpB
                eng = "V" if g % 3 == 0 else "G"
                A(eng, (lambda q3, g, buf: lambda e: e.tensor_mul(
                    buf, pr3[:, :, g:g + 1].broadcast_to([P, NH, DK]),
                    q3[:, g:g + 1, 2 * DK:3 * DK].broadcast_to([P, NH, DK])))(q3, g, buf),
                  reads=[r_probs, r_qkv[t]], writes=[bres], tag=f"c{c}_cm{t}_{g}")
                A("V", (lambda cur, buf: lambda e: e.tensor_tensor(
                    acc3[1 - cur], acc3[cur], buf, ALU.add))(cur, buf),
                  reads=[bres, r_acc], writes=[r_acc], tag=f"c{c}_ca{t}_{g}")
                cur = 1 - cur
            acc_fin = acc_a if cur == 0 else acc_b
            if debug and c == 0:
                A("S", (lambda t, acc_fin: lambda e: e.dma_start(
                    out=dbg["dbg_ctx"][t * P:(t + 1) * P, :],
                    in_=acc_fin.ap()[:]))(t, acc_fin),
                  reads=[r_acc], writes=[r_dbg], dma=True, tag=f"dc{t}")
            A("V", (lambda t, acc_fin: lambda e: e.tensor_copy(
                tokh.ap()[:, t * D:(t + 1) * D], acc_fin.ap()[:]))(t, acc_fin),
              reads=[r_acc], writes=[r_tok[t]], tag=f"c{c}_chi{t}")
            A("V", (lambda t, acc_fin: lambda e: e.tensor_sub(
                tokl.ap()[:, t * D:(t + 1) * D], acc_fin.ap()[:],
                tokh.ap()[:, t * D:(t + 1) * D]))(t, acc_fin),
              reads=[r_acc], writes=[r_tok[t]], tag=f"c{c}_clo{t}")
            for hi in (True, False):
                tsrc = tokh if hi else tokl
                tdst = ctxT_hi if hi else ctxT_lo
                for jj in range(2):
                    bank = 2 + jj
                    pbf = pb[bank].ap().bitcast(BF16)
                    for j4 in range(4):
                        j = jj * 4 + j4
                        A("T", (lambda t, j, j4, pbf, tsrc: lambda e: e.transpose(
                            pbf[:, j4 * P:(j4 + 1) * P],
                            tsrc.ap()[:, t * D + j * P: t * D + (j + 1) * P],
                            ident.ap()[:]))(t, j, j4, pbf, tsrc),
                          reads=[r_tok[t], r_ident], writes=[r_pb[bank]],
                          tag=f"c{c}_ct{t}_{j}")
                    dst = tdst.ap().rearrange("p (j n) -> p j n", n=NB)[
                        :, jj * 4:(jj + 1) * 4, t * P:(t + 1) * P]
                    src = pbf[:, 0:4 * P].rearrange("p (j n) -> p j n", n=P)
                    A("S", (lambda dst, src: lambda e: e.copy(dst, src))(dst, src),
                      reads=[r_pb[bank]], writes=[r_ctxT[t]], tag=f"c{c}_ce{t}_{jj}")

        gi = 0
        for eb in range(D // EBW):
            slot = gi % NSLOT
            wv = wslab_slot(slot)
            A(ldq, (lambda eb, wv: lambda e: e.dma_start(
                out=wv[:, 0:KD * EBW].rearrange("p (k n) -> p k n", n=EBW),
                in_=s_out_hi[:, eb * EBW:(eb + 1) * EBW]
                .rearrange("(k p) n -> p k n", p=P)))(eb, wv),
              reads=[r_scr["out_hi"]], writes=[r_wslab[slot]], dma=True,
              tag=f"c{c}_owh{eb}")
            A(ldq, (lambda eb, wv: lambda e: e.dma_start(
                out=wv[:, KD * EBW:2 * KD * EBW].rearrange("p (k n) -> p k n", n=EBW),
                in_=s_out_lo[:, eb * EBW:(eb + 1) * EBW]
                .rearrange("(k p) n -> p k n", p=P)))(eb, wv),
              reads=[r_scr["out_lo"]], writes=[r_wslab[slot]], dma=True,
              tag=f"c{c}_owl{eb}")
            for t in range(TT):
                bank = (eb * TT + t) % 2
                pairs = [(ones2.ap()[:], b2_out.ap()[:, eb * EBW:(eb + 1) * EBW])]
                for k in range(KD):
                    ch_sl = ctxT_hi.ap()[:, k * NB + t * P: k * NB + (t + 1) * P]
                    cl_sl = ctxT_lo.ap()[:, k * NB + t * P: k * NB + (t + 1) * P]
                    whi = wv[:, k * EBW:(k + 1) * EBW]
                    wlo = wv[:, KD * EBW + k * EBW: KD * EBW + (k + 1) * EBW]
                    pairs += [(ch_sl, whi), (cl_sl, whi), (ch_sl, wlo)]
                mm_group_n(bank, EBW, pairs,
                           [r_ctxT[t], r_wslab[slot], r_ones2, r_b2o],
                           f"c{c}_ok{t}_{eb}")
                A("V", (lambda t, eb, bank: lambda e: e.tensor_add(
                    x2b.ap()[:, t * D + eb * EBW: t * D + (eb + 1) * EBW],
                    pb[bank].ap()[:, 0:EBW],
                    xb.ap()[:, t * D + eb * EBW: t * D + (eb + 1) * EBW]))(t, eb, bank),
                  reads=[r_pb[bank], r_xb[t]], writes=[r_x2[t]],
                  tag=f"c{c}_oe{t}_{eb}")
            gi += 1

        if debug and c == 0:
            for t in range(TT):
                A("S", (lambda t: lambda e: e.dma_start(
                    out=dbg["dbg_x2"][t * P:(t + 1) * P, :],
                    in_=x2b.ap()[:, t * D:(t + 1) * D]))(t),
                  reads=[r_x2[t]], writes=[r_dbg], dma=True, tag=f"dx2{t}")

        ln_stage(x2_tile, r_x2, f"c{c}L2")
        A("S", (lambda tok0: lambda e: e.dma_start(
            out=s_x2[tok0:tok0 + NB, :].rearrange("(i p) d -> p i d", p=P),
            in_=x2b.ap().rearrange("p (i d) -> p i d", d=D)))(tok0),
          reads=[r_x2[0], r_x2[1], r_x2[2], r_x2[3]], writes=[r_scr["x2sp"]],
          dma=True, tag=f"c{c}_sp")

        ldq = "Y"

        pl.alias_barrier(FFN_ARENA, ATTN_ARENA + [r_prepst[0], r_prepst[1]])
        if c == 0:
            pl.alias_barrier([r_wslab[2], r_wslab[3]],
                             [r_prepst[0], r_prepst[1]])
        OGW = 2
        for og in range(KH // OGW):
            slot = og % 4
            wv = wslab_slot(slot)
            ocols = slice(og * OGW * P, (og + 1) * OGW * P)
            A(ldq, (lambda ocols, wv: lambda e: e.dma_start(
                out=wv[:, 0:KD * OGW * P].rearrange("p (k o) -> p k o", o=OGW * P),
                in_=s_t["w1"][:, ocols].rearrange("(k p) o -> p k o", p=P)))(
                    ocols, wv),
              reads=[r_scr["t_w1"]], writes=[r_wslab[slot]], dma=True,
              tag=f"c{c}_w1l{og}")
            A(ldq, (lambda ocols, wv: lambda e: e.dma_start(
                out=wv[:, KD * OGW * P:2 * KD * OGW * P]
                .rearrange("p (k o) -> p k o", o=OGW * P),
                in_=s_t["w2"][:, ocols].rearrange("(k p) o -> p k o", p=P)))(
                    ocols, wv),
              reads=[r_scr["t_w2"]], writes=[r_wslab[slot]], dma=True,
              tag=f"c{c}_w2l{og}")
            for ol in range(OGW):
                o = og * OGW + ol
                b1 = o % 2
                b2b = 4 + (o % 2)
                pairs1, pairs2 = [], []
                for k in range(KD):
                    w1sl = wv[:, k * OGW * P + ol * P: k * OGW * P + (ol + 1) * P]
                    w2sl = wv[:, KD * OGW * P + k * OGW * P + ol * P:
                              KD * OGW * P + k * OGW * P + (ol + 1) * P]
                    mh = xnT_hi.ap()[:, k * NB:(k + 1) * NB]
                    mlo = xnT_lo.ap()[:, k * NB:(k + 1) * NB]
                    pairs1 += [(w1sl, mh), (w1sl, mlo)]
                    pairs2 += [(w2sl, mh), (w2sl, mlo)]
                mm_group(b1, pairs1, [r_xnT, r_wslab[slot]], f"c{c}_t1_{o}")
                mm_group(b2b, pairs2, [r_xnT, r_wslab[slot]], f"c{c}_t2_{o}")
                tf = tf512 if o % 2 == 0 else tf512b
                rtf = r_tf if o % 2 == 0 else r_tfb
                A("S", (lambda o, b1, tf: lambda e: e.activation(
                    tf.ap()[:], pb[b1].ap()[:], AF.Silu,
                    scale=s1c.ap()[:, o:o + 1]))(o, b1, tf),
                  reads=[r_pb[b1], r_scl["s1"]], writes=[rtf], tag=f"c{c}_si{o}")
                A("V", (lambda o, b2b, tf: lambda e: e.scalar_tensor_tensor(
                    h0f.ap()[:], pb[b2b].ap()[:], s2c.ap()[:, o:o + 1],
                    tf.ap()[:], ALU.mult, ALU.mult))(o, b2b, tf),
                  reads=[r_pb[b2b], rtf, r_scl["s2"]], writes=[r_h0f],
                  tag=f"c{c}_h0f{o}")
                A("V", (lambda o: lambda e: e.tensor_copy(
                    h0_hi.ap()[:, o * NB:(o + 1) * NB], h0f.ap()[:]))(o),
                  reads=[r_h0f], writes=[r_h0], tag=f"c{c}_h0h{o}")
                A("V", (lambda o: lambda e: e.tensor_sub(
                    h0_lo.ap()[:, o * NB:(o + 1) * NB], h0f.ap()[:],
                    h0_hi.ap()[:, o * NB:(o + 1) * NB]))(o),
                  reads=[r_h0f], writes=[r_h0], tag=f"c{c}_h0l{o}")

        if debug and c == 0:
            A("S", lambda e: e.dma_start(
                out=dbg["dbg_h0"].rearrange("(k p) n -> p k n", p=P),
                in_=h0_hi.ap().rearrange("p (k n) -> p k n", n=NB)),
              reads=[r_h0], writes=[r_dbg], dma=True, tag="dh0")

        pl.alias_barrier([r_h1p], TOKSIDE)
        for sidx, wkey, scl, rscl, shi, slo, sres, dhi, dlo, dres in (
                (0, "h1", h1sc, r_scl["h1"],
                 lambda k: h0_hi.ap()[:, k * NB:(k + 1) * NB],
                 lambda k: h0_lo.ap()[:, k * NB:(k + 1) * NB], r_h0,
                 lambda o: h1p_slice(True, o), lambda o: h1p_slice(False, o), r_h1p),
                (1, "h2", h2sc, r_scl["h2"],
                 lambda k: h1p_slice(True, k), lambda k: h1p_slice(False, k), r_h1p,
                 lambda o: h0_hi.ap()[:, o * NB:(o + 1) * NB],
                 lambda o: h0_lo.ap()[:, o * NB:(o + 1) * NB], r_h0)):
            OGH = 2
            NPAIR = 2
            if c == 0:
                pl.alias_barrier([r_wslab[2], r_wslab[3]],
                                 [r_prepst[0], r_prepst[1]])
            for og in range(KH // OGH):
                pairi = og % NPAIR
                wv = (wslab.ap()[:, 0:8192] if pairi == 0
                      else prepst_t.ap()[:, 0:8192])
                pres = ([r_wslab[0], r_wslab[1]] if pairi == 0
                        else [r_wslab[2], r_wslab[3]])
                ocols = slice(og * OGH * P, (og + 1) * OGH * P)
                A(ldq, (lambda wkey, ocols, wv: lambda e: e.dma_start(
                    out=wv[:, 0:8192].rearrange("p (k o) -> p k o", o=OGH * P),
                    in_=s_t[wkey][:, ocols].rearrange("(k p) o -> p k o", p=P)))(
                        wkey, ocols, wv),
                  reads=[r_scr["t_" + wkey]], writes=pres, dma=True,
                  tag=f"c{c}_{wkey}l{og}")
                for ol in range(OGH):
                    o = og * OGH + ol
                    bank = o % 2
                    tf = tf512 if o % 2 == 0 else tf512b
                    rtf = r_tf if o % 2 == 0 else r_tfb
                    pairs = []
                    for k in range(KH):
                        wsl = wv[:, k * OGH * P + ol * P: k * OGH * P + (ol + 1) * P]
                        pairs += [(wsl, shi(k)), (wsl, slo(k))]
                    mm_group(bank, pairs, [sres] + pres,
                             f"c{c}_{wkey}_{o}")
                    A("S", (lambda o, bank, scl, tf: lambda e: e.activation(
                        tf.ap()[:], pb[bank].ap()[:], AF.Gelu,
                        scale=scl.ap()[:, o:o + 1]))(o, bank, scl, tf),
                      reads=[r_pb[bank], rscl], writes=[rtf],
                      tag=f"c{c}_{wkey}g{o}")
                    A("V", (lambda o, dhi, tf: lambda e: e.tensor_copy(
                        dhi(o), tf.ap()[:]))(o, dhi, tf),
                      reads=[rtf], writes=[dres], tag=f"c{c}_{wkey}h{o}")
                    A("V", (lambda o, dhi, dlo, tf: lambda e: e.tensor_sub(
                        dlo(o), tf.ap()[:], dhi(o)))(o, dhi, dlo, tf),
                      reads=[rtf], writes=[dres], tag=f"c{c}_{wkey}lo{o}")

            if debug and c == 0 and sidx == 0:
                for pi in range(2):
                    A("S", (lambda pi: lambda e: e.dma_start(
                        out=dbg["dbg_h1p"][pi * 16 * P:(pi + 1) * 16 * P, :]
                        .rearrange("(k p) n -> p k n", p=P),
                        in_=h1p_hi_parts[pi].ap().rearrange("p (k n) -> p k n", n=NB)))(pi),
                      reads=[r_h1p], writes=[r_dbg], dma=True, tag=f"dh1p{pi}")

        pl.alias_barrier([r_x2[0], r_x2[1], r_x2[2], r_x2[3]], [r_h1p])
        A("S" if c == 0 else "Y", (lambda tok0: lambda e: e.dma_start(
            out=x2b.ap().rearrange("p (i d) -> p i d", d=D),
            in_=s_x2[tok0:tok0 + NB, :].rearrange("(i p) d -> p i d", p=P)))(tok0),
          reads=[r_scr["x2sp"]], writes=[r_x2[0], r_x2[1], r_x2[2], r_x2[3]],
          dma=True, tag=f"c{c}_x2r")
        NPIECE = 4
        for piece in range(KH // NPIECE):
            wslot = piece % 2
            w3v = w3slab.ap()[:, wslot * NPIECE * D:(wslot + 1) * NPIECE * D]
            A(ldq, (lambda piece, w3v: lambda e: e.dma_start(
                out=w3v.rearrange("p (k d) -> p k d", d=D),
                in_=s_t["w3"][piece * NPIECE * P:(piece + 1) * NPIECE * P, :]
                .rearrange("(k p) d -> p k d", p=P)))(piece, w3v),
              reads=[r_scr["t_w3"]], writes=[r_w3slab[wslot]], dma=True,
              tag=f"c{c}_w3l{piece}")
            for t in range(TT):
                for kl in range(NPIECE):
                    k = piece * NPIECE + kl
                    sh_ = h0_hi.ap()[:, k * NB + t * P: k * NB + (t + 1) * P]
                    sl_ = h0_lo.ap()[:, k * NB + t * P: k * NB + (t + 1) * P]
                    last = (piece == KH // NPIECE - 1 and kl == NPIECE - 1)
                    for db in range(2):
                        bank = t * 2 + db
                        mv = w3v[:, kl * D + db * NB: kl * D + (db + 1) * NB]
                        A("T", (lambda st, mv, bank, s0, s1: lambda e: e.matmul(
                            pb[bank].ap()[:], st, mv, start=s0, stop=s1))(
                                sh_, mv, bank, piece == 0 and kl == 0, False),
                          reads=[r_h0, r_w3slab[wslot]], writes=[r_pb[bank]],
                          tag=f"c{c}_w3h_p{piece}t{t}k{kl}d{db}")
                        A("T", (lambda st, mv, bank, s0, s1: lambda e: e.matmul(
                            pb[bank].ap()[:], st, mv, start=s0, stop=s1))(
                                sl_, mv, bank, False, last),
                          reads=[r_h0, r_w3slab[wslot]], writes=[r_pb[bank]],
                          tag=f"c{c}_w3lo_p{piece}t{t}k{kl}d{db}")
        pl.alias_barrier([r_tok[0], r_tok[1], r_tok[2], r_tok[3]], [r_h1p])
        ytokh = tokh.ap().bitcast(F32)
        ytokl = tokl.ap().bitcast(F32)
        def ystage(t, db):
            base = (t % 2) * D + db * NB
            return (ytokh if t < 2 else ytokl)[:, base:base + NB]
        for t in range(TT):
            for db in range(2):
                bank = t * 2 + db
                A("V", (lambda bank, db: lambda e: e.tensor_mul(
                    tf512.ap()[:], pb[bank].ap()[:],
                    s3_rep.ap()[:, db * NB:(db + 1) * NB]))(bank, db),
                  reads=[r_pb[bank], r_s3rep], writes=[r_tf],
                  tag=f"c{c}_w3s{t}_{db}")
                A("V", (lambda t, db: lambda e: e.tensor_add(
                    ystage(t, db), tf512.ap()[:],
                    x2b.ap()[:, t * D + db * NB: t * D + (db + 1) * NB]))(t, db),
                  reads=[r_tf, r_x2[t]], writes=[r_tok[t]], tag=f"c{c}_y{t}_{db}")
        A("S", (lambda tok0, ytokh: lambda e: e.dma_start(
            out=dy[tok0:tok0 + NB, :].rearrange("(i p) d -> p i d", p=P)[:, 0:2, :],
            in_=ytokh.rearrange("p (i d) -> p i d", d=D)))(tok0, ytokh),
          reads=[r_tok[0], r_tok[1]], writes=[r_ydram], dma=True,
          tag=f"c{c}_ysta")
        A("S", (lambda tok0, ytokl: lambda e: e.dma_start(
            out=dy[tok0:tok0 + NB, :].rearrange("(i p) d -> p i d", p=P)[:, 2:4, :],
            in_=ytokl.rearrange("p (i d) -> p i d", d=D)))(tok0, ytokl),
          reads=[r_tok[2], r_tok[3]], writes=[r_ydram], dma=True,
          tag=f"c{c}_ystb")

    A("S", lambda e: e.sem_inc(done_sem, 1), reads=[r_ydram, r_dbg], tag="fence")

    pl.finalize()
    with nc.Block() as block:
        pl.emit(block, sems)
    ctx_mgr.close()
    return nc, (list(dbg.keys()) if debug else []), pl


N_CORES = 8
T_FULL = 16384
T_CORE_DEFAULT = T_FULL // N_CORES


def kernel(L0_attn_x, L0_attn_W_qkv, L0_attn_b_qkv, L0_attn_W_out, L0_attn_b_out,
           L0_ffn_w1_w, L0_ffn_w1_s, L0_ffn_w2_w, L0_ffn_w2_s,
           L0_ffn_h1_w, L0_ffn_h1_s, L0_ffn_h2_w, L0_ffn_h2_s,
           L0_ffn_w3_w, L0_ffn_w3_s):
    x = np.ascontiguousarray(np.asarray(L0_attn_x).reshape(T_FULL, D))
    weights = {
        "W_qkv": np.ascontiguousarray(L0_attn_W_qkv),
        "b_qkv": np.ascontiguousarray(L0_attn_b_qkv),
        "W_out": np.ascontiguousarray(L0_attn_W_out),
        "b_out": np.ascontiguousarray(L0_attn_b_out),
        "w1_w": np.ascontiguousarray(L0_ffn_w1_w),
        "w1_s": np.ascontiguousarray(L0_ffn_w1_s),
        "w2_w": np.ascontiguousarray(L0_ffn_w2_w),
        "w2_s": np.ascontiguousarray(L0_ffn_w2_s),
        "h1_w": np.ascontiguousarray(L0_ffn_h1_w),
        "h1_s": np.ascontiguousarray(L0_ffn_h1_s),
        "h2_w": np.ascontiguousarray(L0_ffn_h2_w),
        "h2_s": np.ascontiguousarray(L0_ffn_h2_s),
        "w3_w": np.ascontiguousarray(L0_ffn_w3_w),
        "w3_s": np.ascontiguousarray(L0_ffn_w3_s),
    }
    nc, _dbg, _pl = build_core(T_CORE=T_CORE_DEFAULT, debug=False)
    ident = np.eye(P, dtype=ml_dtypes.bfloat16)
    in_maps = []
    for c in range(N_CORES):
        m = {"x": x[c * T_CORE_DEFAULT:(c + 1) * T_CORE_DEFAULT], "ident": ident}
        m.update(weights)
        in_maps.append(m)
    res = run_bass_kernel_spmd(nc, in_maps, core_ids=list(range(N_CORES)))
    y = np.concatenate([np.asarray(r["y"]) for r in res.results], axis=0)
    return y.reshape(1, T_FULL, D)


# version 12
# speedup vs baseline: 1.1837x; 1.0308x over previous
import numpy as np
import ml_dtypes
import concourse.bass as bass
import concourse.mybir as mybir
from concourse.bass_utils import run_bass_kernel_spmd
from contextlib import ExitStack

F32 = mybir.dt.float32
BF16 = mybir.dt.bfloat16
AF = mybir.ActivationFunctionType
ALU = mybir.AluOpType
AX = mybir.AxisListType

P = 128
D = 1024
E3 = 3072
HID = 4096
KD = D // P
KH = HID // P
NB = 512
NH = 16
DK = 64


class Res:
    __slots__ = ("name", "writers", "readers")

    def __init__(self, name):
        self.name = name
        self.writers = []
        self.readers = []


class Op:
    __slots__ = ("idx", "eng", "fn", "deps", "dma", "need_inc",
                 "sem_key", "sem_val", "waits", "tag", "eng_pos")

    def __init__(self, idx, eng, fn, dma, tag):
        self.idx = idx
        self.eng_pos = -1
        self.eng = eng
        self.fn = fn
        self.dma = dma
        self.deps = []
        self.need_inc = False
        self.sem_key = None
        self.sem_val = None
        self.waits = []
        self.tag = tag


class Plan:
    def __init__(self):
        self.ops = []
        self._eng_n = {}

    def add(self, eng, fn, reads=(), writes=(), dma=False, tag=""):
        op = Op(len(self.ops), eng, fn, dma, tag)
        op.eng_pos = self._eng_n.get(eng, 0)
        self._eng_n[eng] = op.eng_pos + 1
        deps = []
        for r in reads:
            deps.extend(r.writers)
        for w in writes:
            deps.extend(w.writers)
            deps.extend(w.readers)
        best = {}
        for d in deps:
            if d is op:
                continue
            key = ("Q" + d.eng) if d.dma else d.eng
            if key not in best or best[key].idx < d.idx:
                best[key] = d
        op.deps = list(best.values())
        for r in reads:
            r.readers.append(op)
        for w in writes:
            w.writers = [op]
            w.readers = []
        self.ops.append(op)
        return op

    def alias_barrier(self, dst_res, src_res):
        ops = []
        for r in src_res:
            ops.extend(r.writers)
            ops.extend(r.readers)
        for d in dst_res:
            d.readers.extend(ops)

    def finalize(self):
        for op in self.ops:
            resolved = []
            for d in op.deps:
                if d.dma:
                    resolved.append(d)
                elif d.eng != op.eng:
                    d.need_inc = True
                    resolved.append(d)
                elif (op.eng in ("V", "S") and not op.dma
                      and op.eng_pos - d.eng_pos == 1):
                    d.need_inc = True
                    resolved.append(d)
            op.deps = resolved
        ccnt, qcnt = {}, {}
        for op in self.ops:
            if op.dma:
                q = "Q" + op.eng
                qcnt[q] = qcnt.get(q, 0) + 16
                op.sem_key, op.sem_val = q, qcnt[q]
            elif op.need_inc:
                ccnt[op.eng] = ccnt.get(op.eng, 0) + 1
                op.sem_key, op.sem_val = op.eng, ccnt[op.eng]
        last = {}
        for op in self.ops:
            ws = {}
            for d in op.deps:
                k, v = d.sem_key, d.sem_val
                assert k is not None, (op.tag, d.tag)
                if last.get((op.eng, k), 0) < v and ws.get(k, 0) < v:
                    ws[k] = v
            for k, v in ws.items():
                last[(op.eng, k)] = v
            op.waits = sorted(ws.items())
        self.final_q = dict(qcnt)
        self.final_c = dict(ccnt)

    def emit(self, block, sems):
        per_eng = {}
        for op in self.ops:
            per_eng.setdefault(op.eng, []).append(op)

        def runner(ops):
            def go(eng):
                for op in ops:
                    for k, v in op.waits:
                        eng.wait_ge(sems[k], v)
                    inst = op.fn(eng)
                    if op.dma:
                        inst.then_inc(sems[op.sem_key], 16)
                    elif op.need_inc:
                        inst.then_inc(sems[op.sem_key], 1)
            return go

        for ename, attr in (("T", "tensor"), ("V", "vector"), ("S", "scalar"),
                            ("Y", "sync"), ("G", "gpsimd")):
            if ename in per_eng:
                getattr(block, attr)(runner(per_eng[ename]))


def build_core(T_CORE=2048, debug=False):
    C = T_CORE // NB
    TT = NB // P

    nc = bass.Bass("TRN2", target_bir_lowering=False)

    dx = nc.dram_tensor("x", [T_CORE, D], F32, kind="ExternalInput").ap()
    d_wqkv = nc.dram_tensor("W_qkv", [D, E3], F32, kind="ExternalInput").ap()
    d_bqkv = nc.dram_tensor("b_qkv", [E3], F32, kind="ExternalInput").ap()
    d_wout = nc.dram_tensor("W_out", [D, D], F32, kind="ExternalInput").ap()
    d_bout = nc.dram_tensor("b_out", [D], F32, kind="ExternalInput").ap()
    d_w1 = nc.dram_tensor("w1_w", [HID, D], F32, kind="ExternalInput").ap()
    d_w1s = nc.dram_tensor("w1_s", [HID], F32, kind="ExternalInput").ap()
    d_w2 = nc.dram_tensor("w2_w", [HID, D], F32, kind="ExternalInput").ap()
    d_w2s = nc.dram_tensor("w2_s", [HID], F32, kind="ExternalInput").ap()
    d_h1 = nc.dram_tensor("h1_w", [HID, HID], F32, kind="ExternalInput").ap()
    d_h1s = nc.dram_tensor("h1_s", [HID], F32, kind="ExternalInput").ap()
    d_h2 = nc.dram_tensor("h2_w", [HID, HID], F32, kind="ExternalInput").ap()
    d_h2s = nc.dram_tensor("h2_s", [HID], F32, kind="ExternalInput").ap()
    d_w3 = nc.dram_tensor("w3_w", [D, HID], F32, kind="ExternalInput").ap()
    d_w3s = nc.dram_tensor("w3_s", [D], F32, kind="ExternalInput").ap()
    d_ident = nc.dram_tensor("ident", [P, P], BF16, kind="ExternalInput").ap()
    dy = nc.dram_tensor("y", [T_CORE, D], F32, kind="ExternalOutput").ap()

    s_qkv_hi = nc.dram_tensor("s_qkv_hi", [D, E3], BF16, kind="Internal").ap()
    s_qkv_lo = nc.dram_tensor("s_qkv_lo", [D, E3], BF16, kind="Internal").ap()
    s_out_hi = nc.dram_tensor("s_out_hi", [D, D], BF16, kind="Internal").ap()
    s_out_lo = nc.dram_tensor("s_out_lo", [D, D], BF16, kind="Internal").ap()
    s_nat = {
        "w1": nc.dram_tensor("s_nat_w1", [HID, D], BF16, kind="Internal").ap(),
        "w2": nc.dram_tensor("s_nat_w2", [HID, D], BF16, kind="Internal").ap(),
        "h1": nc.dram_tensor("s_nat_h1", [HID, HID], BF16, kind="Internal").ap(),
        "h2": nc.dram_tensor("s_nat_h2", [HID, HID], BF16, kind="Internal").ap(),
        "w3": nc.dram_tensor("s_nat_w3", [D, HID], BF16, kind="Internal").ap(),
    }
    s_t = {
        "w1": nc.dram_tensor("s_t_w1", [D, HID], BF16, kind="Internal").ap(),
        "w2": nc.dram_tensor("s_t_w2", [D, HID], BF16, kind="Internal").ap(),
        "h1": nc.dram_tensor("s_t_h1", [HID, HID], BF16, kind="Internal").ap(),
        "h2": nc.dram_tensor("s_t_h2", [HID, HID], BF16, kind="Internal").ap(),
        "w3": nc.dram_tensor("s_t_w3", [HID, D], BF16, kind="Internal").ap(),
    }
    NROW = E3 + 2 * D
    s_bias = nc.dram_tensor("s_bias", [2, NROW], BF16, kind="Internal").ap()
    s_x2 = nc.dram_tensor("s_x2", [T_CORE, D], F32, kind="Internal").ap()

    dbg = {}
    if debug:
        for nm, shp, dt_ in (("dbg_qkv", [NB, E3], F32),
                             ("dbg_probs", [NB, NH * NH], F32),
                             ("dbg_ctx", [NB, D], F32),
                             ("dbg_x2", [NB, D], F32),
                             ("dbg_h0", [HID, NB], BF16),
                             ("dbg_h1p", [HID, NB], BF16),
                             ("dbg_xb", [P, 4 * D], F32),
                             ("dbg_stats", [P, 32], F32),
                             ("dbg_xnT", [P, KD * NB], BF16)):
            dbg[nm] = nc.dram_tensor(nm, shp, dt_, kind="ExternalOutput").ap()

    ctx_mgr = ExitStack()
    E = ctx_mgr.enter_context

    def addr_of(handle):
        name = handle.name
        for a in nc.m.functions[0].allocations:
            if isinstance(a, mybir.MemoryLocationSet):
                for ml in a.memorylocations:
                    if ml.name == name:
                        return ml.addr
        raise KeyError(name)

    ident = E(nc.sbuf_tensor("t_ident", [P, P], BF16))
    ones2 = E(nc.sbuf_tensor("t_ones2", [2, P], BF16))
    b2_qkv = E(nc.sbuf_tensor("t_b2qkv", [2, E3], BF16))
    b2_out = E(nc.sbuf_tensor("t_b2out", [2, D], BF16))

    s3_rep = E(nc.sbuf_tensor("t_s3rep", [P, D], F32))
    s1c = E(nc.sbuf_tensor("t_s1c", [P, KH], F32))
    s2c = E(nc.sbuf_tensor("t_s2c", [P, KH], F32))
    h1sc = E(nc.sbuf_tensor("t_h1sc", [P, KH], F32))
    h2sc = E(nc.sbuf_tensor("t_h2sc", [P, KH], F32))
    stq = [E(nc.sbuf_tensor(f"t_st{i}", [P, TT], F32)) for i in range(8)]
    nm16 = E(nc.sbuf_tensor("t_nm16", [P, NH], F32))
    den16 = E(nc.sbuf_tensor("t_den16", [P, NH], F32))
    rec16 = E(nc.sbuf_tensor("t_rec16", [P, NH], F32))
    sqt = E(nc.sbuf_tensor("t_sq", [P, D], F32))
    xnf = E(nc.sbuf_tensor("t_xnf", [P, D], F32))
    tf512 = E(nc.sbuf_tensor("t_tf512", [P, NB], F32))
    tf512b = E(nc.sbuf_tensor("t_tf512b", [P, NB], F32))
    h0f = E(nc.sbuf_tensor("t_h0f", [P, NB], F32))
    dbuf2 = E(nc.sbuf_tensor("t_dbuf2", [P, D], F32))

    xb = E(nc.sbuf_tensor("t_xb", [P, TT * D], F32))
    x2b = E(nc.sbuf_tensor("t_x2b", [P, TT * D], F32))
    tokh = E(nc.sbuf_tensor("t_tokh", [P, TT * D], BF16))
    tokl = E(nc.sbuf_tensor("t_tokl", [P, TT * D], BF16))
    xnT_hi = E(nc.sbuf_tensor("t_xnT_hi", [P, KD * NB], BF16))
    xnT_lo = E(nc.sbuf_tensor("t_xnT_lo", [P, KD * NB], BF16))
    wslab = E(nc.sbuf_tensor("t_wslab", [P, 2 * 4096], BF16))
    arena = E(nc.sbuf_tensor("t_arena", [P, 40960], BF16))
    abase = addr_of(arena)

    def at(name, shape, dtype, off):
        return nc.alloc_sbuf_tensor_at(name, shape, dtype, offset=abase + off)

    qkvb = at("v_qkv", [P, TT * E3], F32, 0)
    sc_tmp = at("v_sctmp", [P, NH * DK], F32, 49152)
    scoreb = at("v_score", [P, NH * NH], F32, 53248)
    probb = at("v_prob", [P, NH * NH], F32, 54272)
    acc_a = at("v_acca", [P, D], F32, 55296)
    acc_b = at("v_accb", [P, D], F32, 59392)
    ctxT_hi = at("v_ctxThi", [P, KD * NB], BF16, 63488)
    ctxT_lo = at("v_ctxTlo", [P, KD * NB], BF16, 71680)
    h0_hi = at("v_h0hi", [P, KH * NB], BF16, 0)
    h0_lo = at("v_h0lo", [P, KH * NB], BF16, 32768)
    w3slab = at("v_w3slab", [P, 8 * D], BF16, 65536)
    prepst_t = E(nc.sbuf_tensor("t_prepst", [P, 2 * HID], BF16))
    prepst = [prepst_t.ap()[:, 0:HID], prepst_t.ap()[:, HID:2 * HID]]

    def wslab_slot(slot):
        if slot < 2:
            return wslab.ap()[:, slot * 4096:(slot + 1) * 4096]
        return prepst_t.ap()[:, (slot - 2) * 4096:(slot - 1) * 4096]
    prepw = at("v_prepw", [P, E3], F32, 0)
    prep_hi = at("v_prephi", [P, E3], BF16, 12288)
    prep_lo = at("v_preplo", [P, E3], BF16, 18432)
    prow = at("v_prow", [1, NROW], F32, 24576)
    prow_b = at("v_prowb", [1, 2 * NROW], BF16, 45056)
    s3pair = at("v_s3pair", [2, D], BF16, 24576)

    h1p_hi_parts = [
        nc.alloc_sbuf_tensor_at("v_h1phiA", [P, 8192], BF16, offset=addr_of(xb)),
        nc.alloc_sbuf_tensor_at("v_h1phiB", [P, 8192], BF16, offset=addr_of(x2b)),
    ]
    h1p_lo_parts = [
        nc.alloc_sbuf_tensor_at("v_h1ploA", [P, 4096], BF16, offset=addr_of(tokh)),
        nc.alloc_sbuf_tensor_at("v_h1ploB", [P, 4096], BF16, offset=addr_of(tokl)),
        nc.alloc_sbuf_tensor_at("v_h1ploC", [P, 4096], BF16, offset=addr_of(xnT_hi)),
        nc.alloc_sbuf_tensor_at("v_h1ploD", [P, 4096], BF16, offset=addr_of(xnT_lo)),
    ]

    def h1p_slice(hi, k):
        if hi:
            t = h1p_hi_parts[k // 16]
            k = k % 16
        else:
            t = h1p_lo_parts[k // 8]
            k = k % 8
        return t.ap()[:, k * NB:(k + 1) * NB]

    pb = [E(nc.psum_tensor(f"t_pb{i}", [P, NB], F32)) for i in range(8)]

    sems = {}
    for k in ("T", "V", "S", "G", "QY", "QG", "QS"):
        sems[k] = E(nc.semaphore("sem_" + k))
    done_sem = E(nc.semaphore("sem_done"))

    pl = Plan()
    A = pl.add
    r_pb = [Res(f"pb{i}") for i in range(8)]
    r_scr = {k: Res("scr_" + k) for k in
             ("qkv_hi", "qkv_lo", "out_hi", "out_lo",
              "nat_w1", "nat_w2", "nat_h1", "nat_h2", "nat_w3",
              "t_w1", "t_w2", "t_h1", "t_h2", "t_w3", "bias", "x2sp")}
    r_ident = Res("ident")
    r_ones2 = Res("ones2")
    r_b2q = Res("b2q")
    r_b2o = Res("b2o")
    r_s3p = Res("s3p")
    r_s3rep = Res("s3rep")
    r_scl = {k: Res("scl_" + k) for k in ("s1", "s2", "h1", "h2")}
    r_prepst = [Res("prepstA"), Res("prepstB")]
    r_prepw = Res("prepw")
    r_prow = Res("prow")
    r_xb = [Res(f"xb{t}") for t in range(4)]
    r_x2 = [Res(f"x2_{t}") for t in range(4)]
    r_tok = [Res(f"tok{t}") for t in range(4)]
    r_xnT = Res("xnT")
    r_qkv = [Res(f"qkv{t}") for t in range(4)]
    r_att = Res("att")
    r_scores = Res("scores")
    r_probs = Res("probs")
    r_smx = Res("smx")
    r_acc = Res("acc")
    r_ctxT = [Res(f"ctxT{t}") for t in range(4)]
    r_wslab = [Res("wslabA"), Res("wslabB"), Res("wslabC"), Res("wslabD")]
    r_w3slab = [Res("w3slabA"), Res("w3slabB")]
    r_h0 = Res("h0")
    r_h1p = Res("h1p")
    r_stats = Res("stats")
    r_tmpA = Res("tmpA")
    r_tmpB = Res("tmpB")
    r_tf = Res("tf")
    r_tfb = Res("tfb")
    r_h0f = Res("h0f")
    r_dbuf2 = Res("dbuf2")
    r_ydram = Res("ydram")
    r_dbg = Res("dbg")

    ATTN_ARENA = [r_qkv[0], r_qkv[1], r_qkv[2], r_qkv[3], r_att,
                  r_scores, r_probs, r_smx, r_acc,
                  r_ctxT[0], r_ctxT[1], r_ctxT[2], r_ctxT[3]]
    FFN_ARENA = [r_h0, r_w3slab[0], r_w3slab[1]]
    TOKSIDE = [r_xb[0], r_xb[1], r_xb[2], r_xb[3],
               r_x2[0], r_x2[1], r_x2[2], r_x2[3],
               r_tok[0], r_tok[1], r_tok[2], r_tok[3], r_xnT]

    A("Y", lambda e: e.dma_start(out=ident.ap()[:], in_=d_ident[:, :]),
      writes=[r_ident], dma=True, tag="ld_ident")
    A("V", lambda e: e.memset(ones2.ap()[:], 1.0), writes=[r_ones2], tag="ones2")
    for dst, src, rr in ((s1c, d_w1s, r_scl["s1"]), (s2c, d_w2s, r_scl["s2"]),
                         (h1sc, d_h1s, r_scl["h1"]), (h2sc, d_h2s, r_scl["h2"])):
        A("Y", (lambda dst, src: lambda e: e.dma_start(
            out=dst.ap()[:], in_=src.rearrange("(o p) -> p o", p=P),
            allow_slow_non_contiguous=True))(dst, src),
          writes=[rr], dma=True, tag="ld_scale")

    A("Y", lambda e: e.dma_start(out=prow.ap()[0:1, 0:E3], in_=d_bqkv.unsqueeze(0)),
      writes=[r_prow], dma=True, tag="ld_bqkv")
    A("Y", lambda e: e.dma_start(out=prow.ap()[0:1, E3:E3 + D], in_=d_bout.unsqueeze(0)),
      writes=[r_prow], dma=True, tag="ld_bout")
    A("Y", lambda e: e.dma_start(out=prow.ap()[0:1, E3 + D:NROW], in_=d_w3s.unsqueeze(0)),
      writes=[r_prow], dma=True, tag="ld_w3s")
    A("V", lambda e: e.tensor_copy(prow_b.ap()[0:1, 0:NROW], prow.ap()[0:1, 0:NROW]),
      reads=[r_prow], writes=[r_prow], tag="bias_hi")
    A("V", lambda e: e.tensor_sub(prow_b.ap()[0:1, NROW:2 * NROW],
                                  prow.ap()[0:1, 0:NROW], prow_b.ap()[0:1, 0:NROW]),
      reads=[r_prow], writes=[r_prow], tag="bias_lo")
    A("S", lambda e: e.dma_start(out=s_bias.rearrange("a b -> (a b)").unsqueeze(0),
                                 in_=prow_b.ap()[0:1, 0:2 * NROW]),
      reads=[r_prow], writes=[r_scr["bias"]], dma=True, tag="st_bias")
    A("Y", lambda e: e.dma_start(out=b2_qkv.ap()[:], in_=s_bias[:, 0:E3]),
      reads=[r_scr["bias"]], writes=[r_b2q], dma=True, tag="ld_b2qkv")
    A("Y", lambda e: e.dma_start(out=b2_out.ap()[:], in_=s_bias[:, E3:E3 + D]),
      reads=[r_scr["bias"]], writes=[r_b2o], dma=True, tag="ld_b2out")
    A("Y", lambda e: e.dma_start(out=s3pair.ap()[:], in_=s_bias[:, E3 + D:NROW]),
      reads=[r_scr["bias"]], writes=[r_s3p], dma=True, tag="ld_s3pair")
    for db in range(2):
        A("T", (lambda db: lambda e: e.matmul(
            pb[6 + db].ap()[:], ones2.ap()[:],
            s3pair.ap()[:, db * NB:(db + 1) * NB], start=True, stop=True))(db),
          reads=[r_ones2, r_s3p], writes=[r_pb[6 + db]], tag="s3rep_mm")
        A("S", (lambda db: lambda e: e.copy(
            s3_rep.ap()[:, db * NB:(db + 1) * NB], pb[6 + db].ap()[:]))(db),
          reads=[r_pb[6 + db]], writes=[r_s3rep], tag="s3rep_ev")

    for dsrc, dhi, dlo, width, rhi, rlo, tagn in (
            (d_wqkv, s_qkv_hi, s_qkv_lo, E3, r_scr["qkv_hi"], r_scr["qkv_lo"], "qkv"),
            (d_wout, s_out_hi, s_out_lo, D, r_scr["out_hi"], r_scr["out_lo"], "out")):
        for kt in range(KD):
            sl = slice(kt * P, (kt + 1) * P)
            A("G", (lambda dsrc, sl, width: lambda e: e.dma_start(
                out=prepw.ap()[:, 0:width], in_=dsrc[sl, :]))(dsrc, sl, width),
              writes=[r_prepw], dma=True, tag=f"p_{tagn}ld{kt}")
            A("V", (lambda width: lambda e: e.tensor_copy(
                prep_hi.ap()[:, 0:width], prepw.ap()[:, 0:width]))(width),
              reads=[r_prepw], writes=[r_prepw], tag=f"p_{tagn}hi{kt}")
            A("V", (lambda width: lambda e: e.tensor_sub(
                prep_lo.ap()[:, 0:width], prepw.ap()[:, 0:width],
                prep_hi.ap()[:, 0:width]))(width),
              reads=[r_prepw], writes=[r_prepw], tag=f"p_{tagn}lo{kt}")
            A("Y", (lambda dhi, sl, width: lambda e: e.dma_start(
                out=dhi[sl, :], in_=prep_hi.ap()[:, 0:width]))(dhi, sl, width),
              reads=[r_prepw], writes=[rhi], dma=True, tag=f"p_{tagn}sh{kt}")
            A("Y", (lambda dlo, sl, width: lambda e: e.dma_start(
                out=dlo[sl, :], in_=prep_lo.ap()[:, 0:width]))(dlo, sl, width),
              reads=[r_prepw], writes=[rlo], dma=True, tag=f"p_{tagn}sl{kt}")

    PREP_DIMS = {"w1": (HID, D), "w2": (HID, D), "h1": (HID, HID),
                 "h2": (HID, HID), "w3": (D, HID)}
    PREP_SRC = {"w1": d_w1, "w2": d_w2, "h1": d_h1, "h2": d_h2, "w3": d_w3}
    for wname in ("w1", "w2", "h1", "h2", "w3"):
        A("G", (lambda wname, dsrc: lambda e: e.dma_start(
            out=s_nat[wname][:, :], in_=dsrc[:, :]))(wname, PREP_SRC[wname]),
          writes=[r_scr["nat_" + wname]], dma=True, tag=f"p_cast_{wname}")

    def plan_prep_tr(wname):
        O, I = PREP_DIMS[wname]
        for it in range(I // P):
            sl = it % 2
            A("Y", (lambda wname, it, O, sl: lambda e: e.dma_start(
                out=prepst[sl][:, 0:O],
                in_=s_nat[wname][0:O, it * P:(it + 1) * P],
                transpose=True))(wname, it, O, sl),
              reads=[r_scr["nat_" + wname]], writes=[r_prepst[sl]], dma=True,
              tag=f"p_tld_{wname}_{it}")
            A("Y", (lambda wname, it, O, sl: lambda e: e.dma_start(
                out=s_t[wname][it * P:(it + 1) * P, 0:O],
                in_=prepst[sl][:, 0:O]))(wname, it, O, sl),
              reads=[r_prepst[sl]], writes=[r_scr["t_" + wname]], dma=True,
              tag=f"p_tst_{wname}_{it}")

    for _w in ("w1", "w2", "h1", "h2", "w3"):
        plan_prep_tr(_w)

    def mm_group(bank, pairs, extra_reads, tag):
        N = len(pairs)
        for i, (st, mv) in enumerate(pairs):
            A("T", (lambda st, mv, s0, s1: lambda e: e.matmul(
                pb[bank].ap()[:], st, mv, start=s0, stop=s1))(
                    st, mv, i == 0, i == N - 1),
              reads=list(extra_reads), writes=[r_pb[bank]], tag=f"{tag}_mm{i}")

    def mm_group_n(bank, width, pairs, extra_reads, tag):
        N = len(pairs)
        for i, (st, mv) in enumerate(pairs):
            A("T", (lambda st, mv, s0, s1: lambda e: e.matmul(
                pb[bank].ap()[:, 0:width], st, mv, start=s0, stop=s1))(
                    st, mv, i == 0, i == N - 1),
              reads=list(extra_reads), writes=[r_pb[bank]], tag=f"{tag}_mm{i}")

    def ln_stage(src_tile, src_res, tag):
        s_sx, s_sq, s_m, s_r = stq[0], stq[1], stq[2], stq[3]
        s_t1, s_t2, s_t3, s_t4 = stq[4], stq[5], stq[6], stq[7]
        for i in range(TT):
            A("S", (lambda i: lambda e: e.activation(
                sqt.ap()[:], src_tile(i), AF.Square))(i),
              reads=[src_res[i]], writes=[r_tmpA], tag=f"{tag}_sq{i}")
            A("V", (lambda i: lambda e: e.tensor_reduce(
                s_sx.ap()[:, i:i + 1], src_tile(i), AX.X, ALU.add))(i),
              reads=[src_res[i]], writes=[r_stats], tag=f"{tag}_sx{i}")
            A("V", (lambda i: lambda e: e.tensor_reduce(
                s_sq.ap()[:, i:i + 1], sqt.ap()[:], AX.X, ALU.add))(i),
              reads=[r_tmpA], writes=[r_stats], tag=f"{tag}_sqr{i}")
        A("V", lambda e: e.tensor_scalar_mul(s_m.ap()[:], s_sx.ap()[:], 1.0 / D),
          reads=[r_stats], writes=[r_stats], tag=f"{tag}_m")
        A("V", lambda e: e.tensor_scalar_mul(s_t1.ap()[:], s_sq.ap()[:], 1.0 / D),
          reads=[r_stats], writes=[r_stats], tag=f"{tag}_q")
        A("V", lambda e: e.tensor_mul(s_t2.ap()[:], s_m.ap()[:], s_m.ap()[:]),
          reads=[r_stats], writes=[r_stats], tag=f"{tag}_m2")
        A("V", lambda e: e.tensor_sub(s_t3.ap()[:], s_t1.ap()[:], s_t2.ap()[:]),
          reads=[r_stats], writes=[r_stats], tag=f"{tag}_v")
        A("V", lambda e: e.tensor_scalar_add(s_t4.ap()[:], s_t3.ap()[:], 1e-5),
          reads=[r_stats], writes=[r_stats], tag=f"{tag}_a")
        A("S", lambda e: e.activation(s_t1.ap()[:], s_t4.ap()[:], AF.Ln),
          reads=[r_stats], writes=[r_stats], tag=f"{tag}_ln")
        A("S", lambda e: e.activation(s_t2.ap()[:], s_t1.ap()[:], AF.Exp,
                                      scale=-0.5),
          reads=[r_stats], writes=[r_stats], tag=f"{tag}_y0")
        A("V", lambda e: e.tensor_mul(s_t1.ap()[:], s_t2.ap()[:], s_t2.ap()[:]),
          reads=[r_stats], writes=[r_stats], tag=f"{tag}_n1")
        A("V", lambda e: e.tensor_mul(s_t3.ap()[:], s_t1.ap()[:], s_t4.ap()[:]),
          reads=[r_stats], writes=[r_stats], tag=f"{tag}_n2")
        A("V", lambda e: e.tensor_scalar(s_t1.ap()[:], s_t3.ap()[:], -0.5, 1.5,
                                         ALU.mult, ALU.add),
          reads=[r_stats], writes=[r_stats], tag=f"{tag}_n3")
        A("V", lambda e: e.tensor_mul(s_r.ap()[:], s_t2.ap()[:], s_t1.ap()[:]),
          reads=[r_stats], writes=[r_stats], tag=f"{tag}_r")
        for i in range(TT):
            A("V", (lambda i: lambda e: e.tensor_scalar(
                xnf.ap()[:], src_tile(i), s_m.ap()[:, i:i + 1],
                s_r.ap()[:, i:i + 1], ALU.subtract, ALU.mult))(i),
              reads=[src_res[i], r_stats], writes=[r_tmpB], tag=f"{tag}_nm{i}")
            A("V", (lambda i: lambda e: e.tensor_copy(
                tokh.ap()[:, i * D:(i + 1) * D], xnf.ap()[:]))(i),
              reads=[r_tmpB], writes=[r_tok[i]], tag=f"{tag}_hi{i}")
            A("V", (lambda i: lambda e: e.tensor_sub(
                tokl.ap()[:, i * D:(i + 1) * D], xnf.ap()[:],
                tokh.ap()[:, i * D:(i + 1) * D]))(i),
              reads=[r_tmpB], writes=[r_tok[i]], tag=f"{tag}_lo{i}")
        for hi in (True, False):
            tsrc = tokh if hi else tokl
            tdst = xnT_hi if hi else xnT_lo
            for j in range(KD):
                bank = 2 + (j % 2)
                pbf = pb[bank].ap().bitcast(BF16)
                for i in range(TT):
                    A("T", (lambda i, j, pbf, tsrc: lambda e: e.transpose(
                        pbf[:, i * P:(i + 1) * P],
                        tsrc.ap()[:, i * D + j * P: i * D + (j + 1) * P],
                        ident.ap()[:]))(i, j, pbf, tsrc),
                      reads=[r_tok[i], r_ident], writes=[r_pb[bank]],
                      tag=f"{tag}_tr{j}_{i}")
                A("S", (lambda j, pbf, tdst: lambda e: e.copy(
                    tdst.ap()[:, j * NB:(j + 1) * NB], pbf[:, 0:NB]))(j, pbf, tdst),
                  reads=[r_pb[bank]], writes=[r_xnT], tag=f"{tag}_te{j}")

    xb_tile = lambda i: xb.ap()[:, i * D:(i + 1) * D]
    x2_tile = lambda i: x2b.ap()[:, i * D:(i + 1) * D]

    for c in range(C):
        tok0 = c * NB
        NSLOT = 2 if c == 0 else 4
        pl.alias_barrier(TOKSIDE, [r_h1p])
        pl.alias_barrier(ATTN_ARENA, FFN_ARENA + [r_prepw, r_prow, r_s3p])
        if c == 1:
            pl.alias_barrier([r_wslab[2], r_wslab[3]],
                             [r_prepst[0], r_prepst[1]])

        A("S" if c == 0 else "Y", (lambda tok0: lambda e: e.dma_start(
            out=xb.ap().rearrange("p (i d) -> p i d", d=D),
            in_=dx[tok0:tok0 + NB, :].rearrange("(i p) d -> p i d", p=P)))(tok0),
          writes=[r_xb[0], r_xb[1], r_xb[2], r_xb[3]], dma=True, tag=f"c{c}_x")

        ln_stage(xb_tile, r_xb, f"c{c}L1")

        if debug and c == 0:
            A("S", lambda e: e.dma_start(out=dbg["dbg_xb"][:, :], in_=xb.ap()[:]),
              reads=[r_xb[0], r_xb[1], r_xb[2], r_xb[3]], writes=[r_dbg], dma=True, tag="dxb")
            A("S", (lambda i: lambda e: e.dma_start(out=dbg["dbg_stats"][:, 0:4], in_=stq[0].ap()[:]))(0),
              reads=[r_stats], writes=[r_dbg], dma=True, tag="dst0")
            A("S", (lambda i: lambda e: e.dma_start(out=dbg["dbg_stats"][:, 4:8], in_=stq[1].ap()[:]))(1),
              reads=[r_stats], writes=[r_dbg], dma=True, tag="dst1")
            A("S", (lambda i: lambda e: e.dma_start(out=dbg["dbg_stats"][:, 8:12], in_=stq[2].ap()[:]))(2),
              reads=[r_stats], writes=[r_dbg], dma=True, tag="dst2")
            A("S", (lambda i: lambda e: e.dma_start(out=dbg["dbg_stats"][:, 12:16], in_=stq[3].ap()[:]))(3),
              reads=[r_stats], writes=[r_dbg], dma=True, tag="dst3")
            A("S", (lambda i: lambda e: e.dma_start(out=dbg["dbg_stats"][:, 16:20], in_=stq[4].ap()[:]))(4),
              reads=[r_stats], writes=[r_dbg], dma=True, tag="dst4")
            A("S", (lambda i: lambda e: e.dma_start(out=dbg["dbg_stats"][:, 20:24], in_=stq[5].ap()[:]))(5),
              reads=[r_stats], writes=[r_dbg], dma=True, tag="dst5")
            A("S", (lambda i: lambda e: e.dma_start(out=dbg["dbg_stats"][:, 24:28], in_=stq[6].ap()[:]))(6),
              reads=[r_stats], writes=[r_dbg], dma=True, tag="dst6")
            A("S", (lambda i: lambda e: e.dma_start(out=dbg["dbg_stats"][:, 28:32], in_=stq[7].ap()[:]))(7),
              reads=[r_stats], writes=[r_dbg], dma=True, tag="dst7")
            A("S", lambda e: e.dma_start(out=dbg["dbg_xnT"][:, :], in_=xnT_hi.ap()[:]),
              reads=[r_xnT], writes=[r_dbg], dma=True, tag="dxnT")

        EBW = 256
        ldq = "S" if c == 0 else "Y"
        gi = 0
        for eb in range(E3 // EBW):
            slot = gi % NSLOT
            wv = wslab_slot(slot)
            A(ldq, (lambda eb, wv: lambda e: e.dma_start(
                out=wv[:, 0:KD * EBW].rearrange("p (k n) -> p k n", n=EBW),
                in_=s_qkv_hi[:, eb * EBW:(eb + 1) * EBW]
                .rearrange("(k p) n -> p k n", p=P)))(eb, wv),
              reads=[r_scr["qkv_hi"]], writes=[r_wslab[slot]], dma=True,
              tag=f"c{c}_qwh{eb}")
            A(ldq, (lambda eb, wv: lambda e: e.dma_start(
                out=wv[:, KD * EBW:2 * KD * EBW].rearrange("p (k n) -> p k n", n=EBW),
                in_=s_qkv_lo[:, eb * EBW:(eb + 1) * EBW]
                .rearrange("(k p) n -> p k n", p=P)))(eb, wv),
              reads=[r_scr["qkv_lo"]], writes=[r_wslab[slot]], dma=True,
              tag=f"c{c}_qwl{eb}")
            for t in range(TT):
                bank = (eb * TT + t) % 2
                pairs = [(ones2.ap()[:], b2_qkv.ap()[:, eb * EBW:(eb + 1) * EBW])]
                for k in range(KD):
                    xh_sl = xnT_hi.ap()[:, k * NB + t * P: k * NB + (t + 1) * P]
                    xl_sl = xnT_lo.ap()[:, k * NB + t * P: k * NB + (t + 1) * P]
                    whi = wv[:, k * EBW:(k + 1) * EBW]
                    wlo = wv[:, KD * EBW + k * EBW: KD * EBW + (k + 1) * EBW]
                    pairs += [(xh_sl, whi), (xl_sl, whi), (xh_sl, wlo)]
                mm_group_n(bank, EBW, pairs,
                           [r_xnT, r_wslab[slot], r_ones2, r_b2q], f"c{c}_qk{t}_{eb}")
                A("S", (lambda t, eb, bank: lambda e: e.copy(
                    qkvb.ap()[:, t * E3 + eb * EBW: t * E3 + (eb + 1) * EBW],
                    pb[bank].ap()[:, 0:EBW]))(t, eb, bank),
                  reads=[r_pb[bank]], writes=[r_qkv[t]], tag=f"c{c}_qe{t}_{eb}")
            gi += 1

        if debug and c == 0:
            for t in range(TT):
                A("S", (lambda t: lambda e: e.dma_start(
                    out=dbg["dbg_qkv"][t * P:(t + 1) * P, :],
                    in_=qkvb.ap()[:, t * E3:(t + 1) * E3]))(t),
                  reads=[r_qkv[t]], writes=[r_dbg], dma=True, tag=f"dq{t}")

        sc3 = sc_tmp.ap().rearrange("p (h c) -> p h c", c=DK)
        sco3 = scoreb.ap().rearrange("p (h g) -> p h g", g=NH)
        pr3 = probb.ap().rearrange("p (h g) -> p h g", g=NH)
        acc3 = [acc_a.ap().rearrange("p (h c) -> p h c", c=DK),
                acc_b.ap().rearrange("p (h c) -> p h c", c=DK)]
        for t in range(TT):
            q3 = qkvb.ap()[:, t * E3:(t + 1) * E3].rearrange(
                "p (h c) -> p h c", c=3 * DK)
            if t % 2 == 0:
                bufA, bufB = sc_tmp, xnf
                resA, resB = r_att, r_tmpB
            else:
                bufA, bufB = sqt, dbuf2
                resA, resB = r_tmpA, r_dbuf2
            sc3 = bufA.ap().rearrange("p (h c) -> p h c", c=DK)
            xn3 = bufB.ap().rearrange("p (h c) -> p h c", c=DK)

            def sc_mult(g):
                buf = sc3 if g % 2 == 0 else xn3
                eng = "V" if g % 4 == 0 else "G"
                wres = resA if g % 2 == 0 else resB
                A(eng, (lambda q3, g, buf: lambda e: e.tensor_mul(
                    buf, q3[:, :, 0:DK],
                    q3[:, g:g + 1, DK:2 * DK].broadcast_to([P, NH, DK])))(q3, g, buf),
                  reads=[r_qkv[t]], writes=[wres], tag=f"c{c}_scm{t}_{g}")

            def sc_red(g):
                buf = sc3 if g % 2 == 0 else xn3
                rres = resA if g % 2 == 0 else resB
                A("V", (lambda g, buf: lambda e: e.tensor_reduce(
                    sco3[:, :, g:g + 1], buf, AX.X, ALU.add))(g, buf),
                  reads=[rres], writes=[r_scores], tag=f"c{c}_scr{t}_{g}")

            sc_mult(0)
            for g in range(1, NH):
                sc_mult(g)
                sc_red(g - 1)
            sc_red(NH - 1)
            A("V", lambda e: e.tensor_reduce(nm16.ap()[:].unsqueeze(2), sco3,
                                             AX.X, ALU.max, negate=True),
              reads=[r_scores], writes=[r_smx], tag=f"c{c}_nm{t}")
            A("V", lambda e: e.tensor_tensor(
                pr3, sco3, nm16.ap()[:].unsqueeze(2).broadcast_to([P, NH, NH]),
                ALU.add),
              reads=[r_scores, r_smx], writes=[r_probs], tag=f"c{c}_sub{t}")
            A("S", lambda e: e.activation(scoreb.ap()[:], probb.ap()[:], AF.Exp,
                                          scale=0.125),
              reads=[r_probs], writes=[r_scores], tag=f"c{c}_exp{t}")
            A("V", lambda e: e.tensor_reduce(den16.ap()[:].unsqueeze(2), sco3,
                                             AX.X, ALU.add),
              reads=[r_scores], writes=[r_smx], tag=f"c{c}_den{t}")
            A("S", lambda e: e.activation(nm16.ap()[:], den16.ap()[:], AF.Ln),
              reads=[r_smx], writes=[r_smx], tag=f"c{c}_ld{t}")
            A("S", lambda e: e.activation(rec16.ap()[:], nm16.ap()[:], AF.Exp,
                                          scale=-1.0),
              reads=[r_smx], writes=[r_smx], tag=f"c{c}_rc{t}")
            A("V", lambda e: e.tensor_tensor(
                pr3, sco3, rec16.ap()[:].unsqueeze(2).broadcast_to([P, NH, NH]),
                ALU.mult),
              reads=[r_scores, r_smx], writes=[r_probs], tag=f"c{c}_dv{t}")
            if debug and c == 0:
                A("S", (lambda t: lambda e: e.dma_start(
                    out=dbg["dbg_probs"][t * P:(t + 1) * P, :], in_=probb.ap()[:]))(t),
                  reads=[r_probs], writes=[r_dbg], dma=True, tag=f"dp{t}")
            A("V", (lambda q3: lambda e: e.tensor_mul(
                acc3[0], pr3[:, :, 0:1].broadcast_to([P, NH, DK]),
                q3[:, 0:1, 2 * DK:3 * DK].broadcast_to([P, NH, DK])))(q3),
              reads=[r_probs, r_qkv[t]], writes=[r_acc], tag=f"c{c}_cx0{t}")
            cur = 0
            for g in range(1, NH):
                buf = sc3 if g % 2 == 0 else xn3
                bres = resA if g % 2 == 0 else resB
                eng = "V" if g % 3 == 0 else "G"
                A(eng, (lambda q3, g, buf: lambda e: e.tensor_mul(
                    buf, pr3[:, :, g:g + 1].broadcast_to([P, NH, DK]),
                    q3[:, g:g + 1, 2 * DK:3 * DK].broadcast_to([P, NH, DK])))(q3, g, buf),
                  reads=[r_probs, r_qkv[t]], writes=[bres], tag=f"c{c}_cm{t}_{g}")
                A("V", (lambda cur, buf: lambda e: e.tensor_tensor(
                    acc3[1 - cur], acc3[cur], buf, ALU.add))(cur, buf),
                  reads=[bres, r_acc], writes=[r_acc], tag=f"c{c}_ca{t}_{g}")
                cur = 1 - cur
            acc_fin = acc_a if cur == 0 else acc_b
            if debug and c == 0:
                A("S", (lambda t, acc_fin: lambda e: e.dma_start(
                    out=dbg["dbg_ctx"][t * P:(t + 1) * P, :],
                    in_=acc_fin.ap()[:]))(t, acc_fin),
                  reads=[r_acc], writes=[r_dbg], dma=True, tag=f"dc{t}")
            A("V", (lambda t, acc_fin: lambda e: e.tensor_copy(
                tokh.ap()[:, t * D:(t + 1) * D], acc_fin.ap()[:]))(t, acc_fin),
              reads=[r_acc], writes=[r_tok[t]], tag=f"c{c}_chi{t}")
            A("V", (lambda t, acc_fin: lambda e: e.tensor_sub(
                tokl.ap()[:, t * D:(t + 1) * D], acc_fin.ap()[:],
                tokh.ap()[:, t * D:(t + 1) * D]))(t, acc_fin),
              reads=[r_acc], writes=[r_tok[t]], tag=f"c{c}_clo{t}")
            for hi in (True, False):
                tsrc = tokh if hi else tokl
                tdst = ctxT_hi if hi else ctxT_lo
                for jj in range(2):
                    bank = 2 + jj
                    pbf = pb[bank].ap().bitcast(BF16)
                    for j4 in range(4):
                        j = jj * 4 + j4
                        A("T", (lambda t, j, j4, pbf, tsrc: lambda e: e.transpose(
                            pbf[:, j4 * P:(j4 + 1) * P],
                            tsrc.ap()[:, t * D + j * P: t * D + (j + 1) * P],
                            ident.ap()[:]))(t, j, j4, pbf, tsrc),
                          reads=[r_tok[t], r_ident], writes=[r_pb[bank]],
                          tag=f"c{c}_ct{t}_{j}")
                    dst = tdst.ap().rearrange("p (j n) -> p j n", n=NB)[
                        :, jj * 4:(jj + 1) * 4, t * P:(t + 1) * P]
                    src = pbf[:, 0:4 * P].rearrange("p (j n) -> p j n", n=P)
                    A("S", (lambda dst, src: lambda e: e.copy(dst, src))(dst, src),
                      reads=[r_pb[bank]], writes=[r_ctxT[t]], tag=f"c{c}_ce{t}_{jj}")

        gi = 0
        for eb in range(D // EBW):
            slot = gi % NSLOT
            wv = wslab_slot(slot)
            A(ldq, (lambda eb, wv: lambda e: e.dma_start(
                out=wv[:, 0:KD * EBW].rearrange("p (k n) -> p k n", n=EBW),
                in_=s_out_hi[:, eb * EBW:(eb + 1) * EBW]
                .rearrange("(k p) n -> p k n", p=P)))(eb, wv),
              reads=[r_scr["out_hi"]], writes=[r_wslab[slot]], dma=True,
              tag=f"c{c}_owh{eb}")
            A(ldq, (lambda eb, wv: lambda e: e.dma_start(
                out=wv[:, KD * EBW:2 * KD * EBW].rearrange("p (k n) -> p k n", n=EBW),
                in_=s_out_lo[:, eb * EBW:(eb + 1) * EBW]
                .rearrange("(k p) n -> p k n", p=P)))(eb, wv),
              reads=[r_scr["out_lo"]], writes=[r_wslab[slot]], dma=True,
              tag=f"c{c}_owl{eb}")
            for t in range(TT):
                bank = (eb * TT + t) % 2
                pairs = [(ones2.ap()[:], b2_out.ap()[:, eb * EBW:(eb + 1) * EBW])]
                for k in range(KD):
                    ch_sl = ctxT_hi.ap()[:, k * NB + t * P: k * NB + (t + 1) * P]
                    cl_sl = ctxT_lo.ap()[:, k * NB + t * P: k * NB + (t + 1) * P]
                    whi = wv[:, k * EBW:(k + 1) * EBW]
                    wlo = wv[:, KD * EBW + k * EBW: KD * EBW + (k + 1) * EBW]
                    pairs += [(ch_sl, whi), (cl_sl, whi), (ch_sl, wlo)]
                mm_group_n(bank, EBW, pairs,
                           [r_ctxT[t], r_wslab[slot], r_ones2, r_b2o],
                           f"c{c}_ok{t}_{eb}")
                A("V", (lambda t, eb, bank: lambda e: e.tensor_add(
                    x2b.ap()[:, t * D + eb * EBW: t * D + (eb + 1) * EBW],
                    pb[bank].ap()[:, 0:EBW],
                    xb.ap()[:, t * D + eb * EBW: t * D + (eb + 1) * EBW]))(t, eb, bank),
                  reads=[r_pb[bank], r_xb[t]], writes=[r_x2[t]],
                  tag=f"c{c}_oe{t}_{eb}")
            gi += 1

        if debug and c == 0:
            for t in range(TT):
                A("S", (lambda t: lambda e: e.dma_start(
                    out=dbg["dbg_x2"][t * P:(t + 1) * P, :],
                    in_=x2b.ap()[:, t * D:(t + 1) * D]))(t),
                  reads=[r_x2[t]], writes=[r_dbg], dma=True, tag=f"dx2{t}")

        ln_stage(x2_tile, r_x2, f"c{c}L2")
        A("S", (lambda tok0: lambda e: e.dma_start(
            out=s_x2[tok0:tok0 + NB, :].rearrange("(i p) d -> p i d", p=P),
            in_=x2b.ap().rearrange("p (i d) -> p i d", d=D)))(tok0),
          reads=[r_x2[0], r_x2[1], r_x2[2], r_x2[3]], writes=[r_scr["x2sp"]],
          dma=True, tag=f"c{c}_sp")

        ldq = "Y"

        pl.alias_barrier(FFN_ARENA, ATTN_ARENA + [r_prepst[0], r_prepst[1]])
        if c == 0:
            pl.alias_barrier([r_wslab[2], r_wslab[3]],
                             [r_prepst[0], r_prepst[1]])
        OGW = 2
        for og in range(KH // OGW):
            slot = og % 4
            wv = wslab_slot(slot)
            ocols = slice(og * OGW * P, (og + 1) * OGW * P)
            A(ldq, (lambda ocols, wv: lambda e: e.dma_start(
                out=wv[:, 0:KD * OGW * P].rearrange("p (k o) -> p k o", o=OGW * P),
                in_=s_t["w1"][:, ocols].rearrange("(k p) o -> p k o", p=P)))(
                    ocols, wv),
              reads=[r_scr["t_w1"]], writes=[r_wslab[slot]], dma=True,
              tag=f"c{c}_w1l{og}")
            A(ldq, (lambda ocols, wv: lambda e: e.dma_start(
                out=wv[:, KD * OGW * P:2 * KD * OGW * P]
                .rearrange("p (k o) -> p k o", o=OGW * P),
                in_=s_t["w2"][:, ocols].rearrange("(k p) o -> p k o", p=P)))(
                    ocols, wv),
              reads=[r_scr["t_w2"]], writes=[r_wslab[slot]], dma=True,
              tag=f"c{c}_w2l{og}")
            for ol in range(OGW):
                o = og * OGW + ol
                b1 = o % 2
                b2b = 4 + (o % 2)
                pairs1, pairs2 = [], []
                for k in range(KD):
                    w1sl = wv[:, k * OGW * P + ol * P: k * OGW * P + (ol + 1) * P]
                    w2sl = wv[:, KD * OGW * P + k * OGW * P + ol * P:
                              KD * OGW * P + k * OGW * P + (ol + 1) * P]
                    mh = xnT_hi.ap()[:, k * NB:(k + 1) * NB]
                    mlo = xnT_lo.ap()[:, k * NB:(k + 1) * NB]
                    pairs1 += [(w1sl, mh), (w1sl, mlo)]
                    pairs2 += [(w2sl, mh), (w2sl, mlo)]
                mm_group(b1, pairs1, [r_xnT, r_wslab[slot]], f"c{c}_t1_{o}")
                mm_group(b2b, pairs2, [r_xnT, r_wslab[slot]], f"c{c}_t2_{o}")
                tf = tf512 if o % 2 == 0 else tf512b
                rtf = r_tf if o % 2 == 0 else r_tfb
                A("S", (lambda o, b1, tf: lambda e: e.activation(
                    tf.ap()[:], pb[b1].ap()[:], AF.Silu,
                    scale=s1c.ap()[:, o:o + 1]))(o, b1, tf),
                  reads=[r_pb[b1], r_scl["s1"]], writes=[rtf], tag=f"c{c}_si{o}")
                A("V", (lambda o, b2b, tf: lambda e: e.scalar_tensor_tensor(
                    h0f.ap()[:], pb[b2b].ap()[:], s2c.ap()[:, o:o + 1],
                    tf.ap()[:], ALU.mult, ALU.mult))(o, b2b, tf),
                  reads=[r_pb[b2b], rtf, r_scl["s2"]], writes=[r_h0f],
                  tag=f"c{c}_h0f{o}")
                A("V", (lambda o: lambda e: e.tensor_copy(
                    h0_hi.ap()[:, o * NB:(o + 1) * NB], h0f.ap()[:]))(o),
                  reads=[r_h0f], writes=[r_h0], tag=f"c{c}_h0h{o}")
                A("V", (lambda o: lambda e: e.tensor_sub(
                    h0_lo.ap()[:, o * NB:(o + 1) * NB], h0f.ap()[:],
                    h0_hi.ap()[:, o * NB:(o + 1) * NB]))(o),
                  reads=[r_h0f], writes=[r_h0], tag=f"c{c}_h0l{o}")

        if debug and c == 0:
            A("S", lambda e: e.dma_start(
                out=dbg["dbg_h0"].rearrange("(k p) n -> p k n", p=P),
                in_=h0_hi.ap().rearrange("p (k n) -> p k n", n=NB)),
              reads=[r_h0], writes=[r_dbg], dma=True, tag="dh0")

        pl.alias_barrier([r_h1p], TOKSIDE)
        for sidx, wkey, scl, rscl, shi, slo, sres, dhi, dlo, dres in (
                (0, "h1", h1sc, r_scl["h1"],
                 lambda k: h0_hi.ap()[:, k * NB:(k + 1) * NB],
                 lambda k: h0_lo.ap()[:, k * NB:(k + 1) * NB], r_h0,
                 lambda o: h1p_slice(True, o), lambda o: h1p_slice(False, o), r_h1p),
                (1, "h2", h2sc, r_scl["h2"],
                 lambda k: h1p_slice(True, k), lambda k: h1p_slice(False, k), r_h1p,
                 lambda o: h0_hi.ap()[:, o * NB:(o + 1) * NB],
                 lambda o: h0_lo.ap()[:, o * NB:(o + 1) * NB], r_h0)):
            OGH = 2
            NPAIR = 2
            if c == 0:
                pl.alias_barrier([r_wslab[2], r_wslab[3]],
                                 [r_prepst[0], r_prepst[1]])
            for og in range(KH // OGH):
                pairi = og % NPAIR
                wv = (wslab.ap()[:, 0:8192] if pairi == 0
                      else prepst_t.ap()[:, 0:8192])
                pres = ([r_wslab[0], r_wslab[1]] if pairi == 0
                        else [r_wslab[2], r_wslab[3]])
                ocols = slice(og * OGH * P, (og + 1) * OGH * P)
                A(ldq, (lambda wkey, ocols, wv: lambda e: e.dma_start(
                    out=wv[:, 0:8192].rearrange("p (k o) -> p k o", o=OGH * P),
                    in_=s_t[wkey][:, ocols].rearrange("(k p) o -> p k o", p=P)))(
                        wkey, ocols, wv),
                  reads=[r_scr["t_" + wkey]], writes=pres, dma=True,
                  tag=f"c{c}_{wkey}l{og}")
                for ol in range(OGH):
                    o = og * OGH + ol
                    bank = o % 2
                    tf = tf512 if o % 2 == 0 else tf512b
                    rtf = r_tf if o % 2 == 0 else r_tfb
                    pairs = []
                    for k in range(KH):
                        wsl = wv[:, k * OGH * P + ol * P: k * OGH * P + (ol + 1) * P]
                        pairs += [(wsl, shi(k)), (wsl, slo(k))]
                    mm_group(bank, pairs, [sres] + pres,
                             f"c{c}_{wkey}_{o}")
                    A("S", (lambda o, bank, scl, tf: lambda e: e.activation(
                        tf.ap()[:], pb[bank].ap()[:], AF.Gelu,
                        scale=scl.ap()[:, o:o + 1]))(o, bank, scl, tf),
                      reads=[r_pb[bank], rscl], writes=[rtf],
                      tag=f"c{c}_{wkey}g{o}")
                    A("V", (lambda o, dhi, tf: lambda e: e.tensor_copy(
                        dhi(o), tf.ap()[:]))(o, dhi, tf),
                      reads=[rtf], writes=[dres], tag=f"c{c}_{wkey}h{o}")
                    A("V", (lambda o, dhi, dlo, tf: lambda e: e.tensor_sub(
                        dlo(o), tf.ap()[:], dhi(o)))(o, dhi, dlo, tf),
                      reads=[rtf], writes=[dres], tag=f"c{c}_{wkey}lo{o}")

            if debug and c == 0 and sidx == 0:
                for pi in range(2):
                    A("S", (lambda pi: lambda e: e.dma_start(
                        out=dbg["dbg_h1p"][pi * 16 * P:(pi + 1) * 16 * P, :]
                        .rearrange("(k p) n -> p k n", p=P),
                        in_=h1p_hi_parts[pi].ap().rearrange("p (k n) -> p k n", n=NB)))(pi),
                      reads=[r_h1p], writes=[r_dbg], dma=True, tag=f"dh1p{pi}")

        pl.alias_barrier([r_x2[0], r_x2[1], r_x2[2], r_x2[3]], [r_h1p])
        A("S" if c == 0 else "Y", (lambda tok0: lambda e: e.dma_start(
            out=x2b.ap().rearrange("p (i d) -> p i d", d=D),
            in_=s_x2[tok0:tok0 + NB, :].rearrange("(i p) d -> p i d", p=P)))(tok0),
          reads=[r_scr["x2sp"]], writes=[r_x2[0], r_x2[1], r_x2[2], r_x2[3]],
          dma=True, tag=f"c{c}_x2r")
        NPIECE = 4
        for piece in range(KH // NPIECE):
            wslot = piece % 2
            w3v = w3slab.ap()[:, wslot * NPIECE * D:(wslot + 1) * NPIECE * D]
            A(ldq, (lambda piece, w3v: lambda e: e.dma_start(
                out=w3v.rearrange("p (k d) -> p k d", d=D),
                in_=s_t["w3"][piece * NPIECE * P:(piece + 1) * NPIECE * P, :]
                .rearrange("(k p) d -> p k d", p=P)))(piece, w3v),
              reads=[r_scr["t_w3"]], writes=[r_w3slab[wslot]], dma=True,
              tag=f"c{c}_w3l{piece}")
            for t in range(TT):
                for kl in range(NPIECE):
                    k = piece * NPIECE + kl
                    sh_ = h0_hi.ap()[:, k * NB + t * P: k * NB + (t + 1) * P]
                    sl_ = h0_lo.ap()[:, k * NB + t * P: k * NB + (t + 1) * P]
                    last = (piece == KH // NPIECE - 1 and kl == NPIECE - 1)
                    for db in range(2):
                        bank = t * 2 + db
                        mv = w3v[:, kl * D + db * NB: kl * D + (db + 1) * NB]
                        A("T", (lambda st, mv, bank, s0, s1: lambda e: e.matmul(
                            pb[bank].ap()[:], st, mv, start=s0, stop=s1))(
                                sh_, mv, bank, piece == 0 and kl == 0, False),
                          reads=[r_h0, r_w3slab[wslot]], writes=[r_pb[bank]],
                          tag=f"c{c}_w3h_p{piece}t{t}k{kl}d{db}")
                        A("T", (lambda st, mv, bank, s0, s1: lambda e: e.matmul(
                            pb[bank].ap()[:], st, mv, start=s0, stop=s1))(
                                sl_, mv, bank, False, last),
                          reads=[r_h0, r_w3slab[wslot]], writes=[r_pb[bank]],
                          tag=f"c{c}_w3lo_p{piece}t{t}k{kl}d{db}")
        pl.alias_barrier([r_tok[0], r_tok[1], r_tok[2], r_tok[3]], [r_h1p])
        ytokh = tokh.ap().bitcast(F32)
        ytokl = tokl.ap().bitcast(F32)
        def ystage(t, db):
            base = (t % 2) * D + db * NB
            return (ytokh if t < 2 else ytokl)[:, base:base + NB]
        for t in range(TT):
            for db in range(2):
                bank = t * 2 + db
                A("V", (lambda bank, db: lambda e: e.tensor_mul(
                    tf512.ap()[:], pb[bank].ap()[:],
                    s3_rep.ap()[:, db * NB:(db + 1) * NB]))(bank, db),
                  reads=[r_pb[bank], r_s3rep], writes=[r_tf],
                  tag=f"c{c}_w3s{t}_{db}")
                A("V", (lambda t, db: lambda e: e.tensor_add(
                    ystage(t, db), tf512.ap()[:],
                    x2b.ap()[:, t * D + db * NB: t * D + (db + 1) * NB]))(t, db),
                  reads=[r_tf, r_x2[t]], writes=[r_tok[t]], tag=f"c{c}_y{t}_{db}")
        A("S", (lambda tok0, ytokh: lambda e: e.dma_start(
            out=dy[tok0:tok0 + NB, :].rearrange("(i p) d -> p i d", p=P)[:, 0:2, :],
            in_=ytokh.rearrange("p (i d) -> p i d", d=D)))(tok0, ytokh),
          reads=[r_tok[0], r_tok[1]], writes=[r_ydram], dma=True,
          tag=f"c{c}_ysta")
        A("S", (lambda tok0, ytokl: lambda e: e.dma_start(
            out=dy[tok0:tok0 + NB, :].rearrange("(i p) d -> p i d", p=P)[:, 2:4, :],
            in_=ytokl.rearrange("p (i d) -> p i d", d=D)))(tok0, ytokl),
          reads=[r_tok[2], r_tok[3]], writes=[r_ydram], dma=True,
          tag=f"c{c}_ystb")

    A("S", lambda e: e.sem_inc(done_sem, 1), reads=[r_ydram, r_dbg], tag="fence")

    pl.finalize()
    with nc.Block() as block:
        pl.emit(block, sems)
    ctx_mgr.close()
    return nc, (list(dbg.keys()) if debug else []), pl


N_CORES = 8
T_FULL = 16384
T_CORE_DEFAULT = T_FULL // N_CORES


def kernel(L0_attn_x, L0_attn_W_qkv, L0_attn_b_qkv, L0_attn_W_out, L0_attn_b_out,
           L0_ffn_w1_w, L0_ffn_w1_s, L0_ffn_w2_w, L0_ffn_w2_s,
           L0_ffn_h1_w, L0_ffn_h1_s, L0_ffn_h2_w, L0_ffn_h2_s,
           L0_ffn_w3_w, L0_ffn_w3_s):
    x = np.ascontiguousarray(np.asarray(L0_attn_x).reshape(T_FULL, D))
    weights = {
        "W_qkv": np.ascontiguousarray(L0_attn_W_qkv),
        "b_qkv": np.ascontiguousarray(L0_attn_b_qkv),
        "W_out": np.ascontiguousarray(L0_attn_W_out),
        "b_out": np.ascontiguousarray(L0_attn_b_out),
        "w1_w": np.ascontiguousarray(L0_ffn_w1_w),
        "w1_s": np.ascontiguousarray(L0_ffn_w1_s),
        "w2_w": np.ascontiguousarray(L0_ffn_w2_w),
        "w2_s": np.ascontiguousarray(L0_ffn_w2_s),
        "h1_w": np.ascontiguousarray(L0_ffn_h1_w),
        "h1_s": np.ascontiguousarray(L0_ffn_h1_s),
        "h2_w": np.ascontiguousarray(L0_ffn_h2_w),
        "h2_s": np.ascontiguousarray(L0_ffn_h2_s),
        "w3_w": np.ascontiguousarray(L0_ffn_w3_w),
        "w3_s": np.ascontiguousarray(L0_ffn_w3_s),
    }
    nc, _dbg, _pl = build_core(T_CORE=T_CORE_DEFAULT, debug=False)
    ident = np.eye(P, dtype=ml_dtypes.bfloat16)
    in_maps = []
    for c in range(N_CORES):
        m = {"x": x[c * T_CORE_DEFAULT:(c + 1) * T_CORE_DEFAULT], "ident": ident}
        m.update(weights)
        in_maps.append(m)
    res = run_bass_kernel_spmd(nc, in_maps, core_ids=list(range(N_CORES)))
    y = np.concatenate([np.asarray(r["y"]) for r in res.results], axis=0)
    return y.reshape(1, T_FULL, D)
